# version 05
# baseline (speedup 1.0000x reference)
import math
from contextlib import ExitStack
import numpy as np
import concourse.bass as bass
import concourse.mybir as mybir
from concourse.bass_utils import run_bass_kernel_spmd

F32 = mybir.dt.float32
BF16 = mybir.dt.bfloat16
I32 = mybir.dt.int32
AF = mybir.ActivationFunctionType
ALU = mybir.AluOpType
AX = mybir.AxisListType

B, T, D, H, KV, HD, DFF = 2, 4096, 4096, 32, 8, 128, 11008
NC_ = 8
CH = 512
NG = 9
NPRE = NG * CH
NT = NG * 4
NOWN = 1024
DC = D // 128
FC = DFF // 128
EPS = 1e-5
ISQ = 1.0 / math.sqrt(HD)
R4096 = 1.0 / 4096.0


class Sem:
    def __init__(self, nc, name):
        self.h = nc.alloc_semaphore(name)
        self.n = 0

    def inc(self, instr, k=1):
        instr.then_inc(self.h, k)
        self.n += k
        return self.n


def _w(eng, sem, val):
    if val is not None and val > 0:
        eng.wait_ge(sem.h, val)


def _wt(eng, token):
    if token is not None:
        sem, v = token
        _w(eng, sem, v)


def _emit_oav(t_, tn, vn, sn, exp_done, mask_done, oav_done, onesm_b, v_sb, pt,
              o_ps, sums_ps, kb, gi, idx, kc, hf, g0, nh, ott_prev):
    _w(t_, sn, exp_done[gi])
    if idx <= 1:
        _w(t_, vn, mask_done[gi])
    if idx == 0:
        _w(t_, vn, ott_prev)
    first = (idx == 0)
    last = (idx == nh - 1)
    ins = None
    for u in range(2):
        ktile = kc * 4 + hf * 2 + u
        st = first and (u == 0)
        sp = last and (u == 1)
        t_.matmul(sums_ps[:], onesm_b[:],
                  pt[:, gi % 2, u * CH:(u + 1) * CH],
                  start=st, stop=sp, skip_group_check=True)
        ins = t_.matmul(o_ps[:], v_sb[:, kb, ktile, :],
                        pt[:, gi % 2, u * CH:(u + 1) * CH],
                        start=st, stop=sp, skip_group_check=True)
    tn.inc(ins)
    oav_done[gi] = tn.n


def build(phases=(1, 2, 3, 4, 5, 6), dbg=()):
    nc = bass.Bass("TRN2", target_bir_lowering=False)
    t_, v_, s_, g_, y_ = nc.tensor, nc.vector, nc.scalar, nc.gpsimd, nc.sync

    def scr(name, shape, dtype):
        k = "ExternalOutput" if name in dbg else "Internal"
        return nc.dram_tensor(name, shape, dtype, kind=k)

    x_pre = nc.dram_tensor("x_pre", [NPRE, D], F32, kind="ExternalInput")
    cosq = nc.dram_tensor("cosq", [HD, NPRE], F32, kind="ExternalInput")
    sinq = nc.dram_tensor("sinq", [HD, NPRE], F32, kind="ExternalInput")
    wk_in = nc.dram_tensor("wk", [D, KV * HD], F32, kind="ExternalInput")
    wv_in = nc.dram_tensor("wv", [D, KV * HD], F32, kind="ExternalInput")
    wq_in = nc.dram_tensor("wq", [D, H * HD], F32, kind="ExternalInput")
    wo_in = nc.dram_tensor("wo", [H * HD, D], F32, kind="ExternalInput")
    gate_rl = nc.dram_tensor("gate_rl", [D, DFF], I32, kind="ExternalInput")
    up_rl = nc.dram_tensor("up_rl", [D, DFF], I32, kind="ExternalInput")
    down_rl = nc.dram_tensor("down_rl", [DFF, D], I32, kind="ExternalInput")
    gs_in = nc.dram_tensor("gs", [128, FC], F32, kind="ExternalInput")
    us_in = nc.dram_tensor("us", [128, FC], F32, kind="ExternalInput")
    ds_in = nc.dram_tensor("ds", [128, DC], F32, kind="ExternalInput")
    maskt_in = nc.dram_tensor("maskt", [128, 2048], F32, kind="ExternalInput")
    ident_in = nc.dram_tensor("identf", [128, 128], F32, kind="ExternalInput")
    pperm_in = nc.dram_tensor("pperm", [128, 128], F32, kind="ExternalInput")
    ones1_in = nc.dram_tensor("ones1", [1, 128], F32, kind="ExternalInput")
    onesc_in = nc.dram_tensor("onesc", [128, 1], F32, kind="ExternalInput")
    onesm_in = nc.dram_tensor("onesm", [128, 128], F32, kind="ExternalInput")
    cid_in = nc.dram_tensor("cid", [1, 1], I32, kind="ExternalInput")
    out_own = nc.dram_tensor("out_own", [NOWN, D], F32, kind="ExternalOutput")

    xn_s = scr("xn_s", [DC, 128, NPRE], BF16)
    k_s = scr("k_s", [KV, HD, NPRE], BF16)
    v_s = scr("v_s", [KV, NT, 128, HD], BF16)
    h_s = scr("h_s", [FC, 128, NOWN], BF16)

    tn = Sem(nc, "tn")
    vn = Sem(nc, "vn")
    sn = Sem(nc, "sn")
    gn = Sem(nc, "gn")
    dP = Sem(nc, "dP")
    dPW = Sem(nc, "dPW")
    dWK = Sem(nc, "dWK")
    dWV = Sem(nc, "dWV")
    dXs = [Sem(nc, "dX0"), Sem(nc, "dX1"), Sem(nc, "dX2")]
    dCSs = [Sem(nc, "dCS0"), Sem(nc, "dCS1")]
    dXNs = [Sem(nc, "dXN0"), Sem(nc, "dXN1")]
    dKs = [Sem(nc, "dK0"), Sem(nc, "dK1")]
    dVs = [Sem(nc, "dV0"), Sem(nc, "dV1")]
    dXRs = [Sem(nc, "dXR0"), Sem(nc, "dXR1")]
    dGUs = [Sem(nc, "dGU0"), Sem(nc, "dGU1")]
    dHs = [Sem(nc, "dH0"), Sem(nc, "dH1")]
    dHRs = [Sem(nc, "dHR0"), Sem(nc, "dHR1")]
    dDNs = [Sem(nc, "dDN0"), Sem(nc, "dDN1")]
    dOAs = [Sem(nc, "dOA0"), Sem(nc, "dOA1")]
    dWQs = [Sem(nc, "dWQ0"), Sem(nc, "dWQ1")]
    dONE = Sem(nc, "dONE")
    dKVs = [Sem(nc, "dKV0"), Sem(nc, "dKV1")]
    dWOs = [Sem(nc, "dWO0"), Sem(nc, "dWO1")]
    dXR6 = [Sem(nc, "dXR6a"), Sem(nc, "dXR6b"), Sem(nc, "dXR6c"), Sem(nc, "dXR6d")]
    dACs = [Sem(nc, "dAC0"), Sem(nc, "dAC1"), Sem(nc, "dAC2"), Sem(nc, "dAC3")]

    all_dma_sems = ([tn, vn, sn, gn, dP, dPW, dWK, dWV, dONE] + dXs + dCSs + dXNs
                    + dKs + dVs + dXRs + dGUs + dHs + dHRs + dDNs + dOAs + dWQs
                    + dKVs + dWOs + dXR6 + dACs)

    def quiesce():
        for sem in all_dma_sems:
            if sem.n > 0:
                y_.wait_ge(sem.h, sem.n)
                g_.wait_ge(sem.h, sem.n)
        nc.all_engine_barrier()

    with ExitStack() as _st0:
        def _sb(name, shape, dtype):
            return _st0.enter_context(nc.sbuf_tensor(name, shape, dtype))
        ident_b = _sb("ident_b", [128, 128], BF16)
        ident_f = _sb("ident_f", [128, 128], F32)
        pperm_b = _sb("pperm_b", [128, 128], BF16)
        ones1_f = _sb("ones1_f", [1, 128], F32)
        onesc_b = _sb("onesc_b", [128, 1], BF16)
        onesm_b = _sb("onesm_b", [128, 128], BF16)
        maskt_b = _sb("maskt_b", [128, 2048], BF16)
        gs_sb = _sb("gs_sb", [128, FC], F32)
        us_sb = _sb("us_sb", [128, FC], F32)
        ds_sb = _sb("ds_sb", [128, DC], F32)
        cid_sb = _sb("cid_sb", [1, 1], I32)
        stat = _sb("stat", [128, 24], F32)
        for ap, src in ((ident_f, ident_in), (ones1_f, ones1_in), (gs_sb, gs_in),
                        (us_sb, us_in), (ds_sb, ds_in), (cid_sb, cid_in)):
            dP.n = (y_.dma_start(ap[:], src[:]).then_inc(dP.h, 16), dP.n + 16)[1]
        for ap, src in ((ident_b, ident_in), (pperm_b, pperm_in),
                        (onesc_b, onesc_in), (maskt_b, maskt_in),
                        (onesm_b, onesm_in)):
            dPW.n = (g_.dma_start(ap[:], src[:]).then_inc(dPW.h, 16), dPW.n + 16)[1]
        per_done = dP.n
        perw_done = dPW.n

        if 1 in phases:
            with ExitStack() as _st1:
                def _sb1(name, shape, dtype):
                    return _st1.enter_context(nc.sbuf_tensor(name, shape, dtype))
                def _ps1(name, shape, dtype):
                    return _st1.enter_context(nc.psum_tensor(name, shape, dtype))
                wk_sb = _sb1("wk_sb", [128, DC, KV * HD], BF16)
                x_sb = _sb1("x_sb", [128, 3, D], F32)
                xnb = _sb1("xnb", [128, 3, D], BF16)
                xnf = _sb1("xnf", [128, DC, CH], BF16)
                sqd = _sb1("sqd", [128, D], BF16)
                cos1 = _sb1("cos1", [128, 2, CH], F32)
                sin1 = _sb1("sin1", [128, 2, CH], F32)
                kbf = _sb1("kbf", [128, 2, CH], BF16)
                kc = _sb1("kc", [128, 2, CH], F32)
                rs = _sb1("rs", [128, 2, CH], F32)
                krot = _sb1("krot", [128, 2, CH], BF16)
                rbf = _sb1("rbf", [128, 2, CH], BF16)
                pst0 = _ps1("pst0", [128, 2048], BF16)
                pst1 = _ps1("pst1", [128, 2048], BF16)
                kps = (_ps1("kps0", [128, CH], F32), _ps1("kps1", [128, CH], F32))
                rps = (_ps1("rps0", [128, CH], F32), _ps1("rps1", [128, CH], F32))
                pst = (pst0, pst1)
                wk_r = wk_in.rearrange("(dc p) f -> p dc f", p=128)
                for q in range(4):
                    g_.dma_start(wk_sb[:, q * 8:(q + 1) * 8, :],
                                 wk_r[:, q * 8:(q + 1) * 8, :]).then_inc(dWK.h, 16)
                    dWK.n += 16
                wk_ready = dWK.n

                x_tok = {}
                sq_done = {}
                ln_done = {}
                nmr_prev = {}
                tr_done = {}
                ev_done = {}
                xnst_tok = {}
                cs_tok = {}
                kmm_done, cp_done, rh_done, rcp_done = {}, {}, {}, {}
                kc_done, rsn_done, ga_done = {}, {}, {}
                kst_tok = {}

                def _emit_tr(Tj):
                    if (Tj, 0) in tr_done:
                        return
                    slj = Tj % 3
                    ttj = Tj % 4
                    gj = Tj // 4
                    _w(t_, sn, ln_done[Tj])
                    if Tj == 0:
                        _w(t_, dPW, perw_done)
                    if ttj == 0 and gj > 0:
                        _wt(t_, xnst_tok[gj - 1])
                    for half in range(2):
                        _w(t_, vn, ev_done.get((Tj - 1, half)))
                        ins2 = None
                        for j2 in range(16):
                            dc2 = half * 16 + j2
                            ins2 = t_.transpose(pst[half][:, j2 * 128:(j2 + 1) * 128],
                                                xnb[:, slj, dc2 * 128:(dc2 + 1) * 128],
                                                ident_b[:])
                        tn.inc(ins2)
                        tr_done[(Tj, half)] = tn.n
                        _w(v_, tn, tr_done[(Tj, half)])
                        vn.inc(v_.tensor_copy(
                            xnf.ap()[:, half * 16:(half + 1) * 16,
                                     ttj * 128:(ttj + 1) * 128],
                            pst[half].ap().rearrange("p (dc t) -> p dc t", t=128)))
                        ev_done[(Tj, half)] = vn.n

                for g in range(NG):
                    _w(y_, vn, rsn_done.get((g - 2) * 8 + 7))
                    cse = dCSs[g % 2]
                    y_.dma_start(cos1[:, g % 2, :], cosq[:, g * CH:(g + 1) * CH])\
                        .then_inc(cse.h, 16)
                    y_.dma_start(sin1[:, g % 2, :], sinq[:, g * CH:(g + 1) * CH])\
                        .then_inc(cse.h, 16)
                    cse.n += 32
                    cs_tok[g] = (cse, cse.n)

                    for tt in range(4):
                        Ti = g * 4 + tt
                        sl = Ti % 3
                        o8 = sl * 8
                        _w(y_, sn, ln_done.get(Ti - 3))
                        xe = dXs[sl]
                        y_.dma_start(x_sb[:, sl, :],
                                     x_pre[Ti * 128:(Ti + 1) * 128, :]).then_inc(xe.h, 16)
                        xe.n += 16
                        x_tok[Ti] = (xe, xe.n)

                        _wt(s_, x_tok[Ti])
                        _w(s_, vn, nmr_prev.get(Ti - 3))
                        sn.inc(s_.activation(sqd[:], x_sb[:, sl, :], AF.Square,
                                             accum_out=stat[:, o8 + 0:o8 + 1]))
                        ssq_v = sn.n
                        _wt(v_, x_tok[Ti])
                        v_.reduce_sum(stat[:, o8 + 1:o8 + 2], x_sb[:, sl, :], axis=AX.X)
                        v_.drain()
                        v_.tensor_scalar_mul(stat[:, o8 + 2:o8 + 3],
                                             stat[:, o8 + 1:o8 + 2], R4096)
                        v_.drain()
                        v_.tensor_mul(stat[:, o8 + 3:o8 + 4], stat[:, o8 + 2:o8 + 3],
                                      stat[:, o8 + 2:o8 + 3])
                        _w(v_, sn, ssq_v)
                        v_.tensor_scalar(stat[:, o8 + 4:o8 + 5], stat[:, o8 + 0:o8 + 1],
                                         R4096, EPS, ALU.mult, ALU.add)
                        v_.drain()
                        vn.inc(v_.tensor_sub(stat[:, o8 + 5:o8 + 6],
                                             stat[:, o8 + 4:o8 + 5],
                                             stat[:, o8 + 3:o8 + 4]))
                        varf_v = vn.n
                        _w(s_, vn, varf_v)
                        sn.inc(s_.activation(stat[:, o8 + 6:o8 + 7],
                                             stat[:, o8 + 5:o8 + 6], AF.Sqrt))
                        sq_done[Ti] = sn.n
                        _w(v_, sn, sq_done[Ti])
                        v_.reciprocal(stat[:, o8 + 7:o8 + 8], stat[:, o8 + 6:o8 + 7])
                        v_.drain()
                        vn.inc(v_.tensor_scalar(stat[:, o8 + 2:o8 + 3],
                                                stat[:, o8 + 1:o8 + 2],
                                                -R4096, stat[:, o8 + 7:o8 + 8],
                                                ALU.mult, ALU.mult))
                        nmr_done = vn.n
                        nmr_prev[Ti] = vn.n
                        _w(s_, vn, nmr_done)
                        _w(s_, tn, tr_done.get((Ti - 3, 1)))
                        sn.inc(s_.activation(xnb[:, sl, :], x_sb[:, sl, :],
                                             AF.Identity,
                                             bias=stat[:, o8 + 2:o8 + 3],
                                             scale=stat[:, o8 + 7:o8 + 8]))
                        ln_done[Ti] = sn.n

                        if Ti >= 1:
                            _emit_tr(Ti - 1)
                    _emit_tr(g * 4 + 3)

                    _w(y_, vn, ev_done[(g * 4 + 3, 1)])
                    xne = dXNs[g % 2]
                    y_.dma_start(xn_s.ap()[:, :, g * CH:(g + 1) * CH]
                                 .rearrange("dc p t -> p dc t"), xnf[:]).then_inc(xne.h, 16)
                    xne.n += 16
                    xnst_tok[g] = (xne, xne.n)

                    def _rope_fin(rj):
                        gj, fj = rj // 8, rj % 8
                        fbj = rj % 2
                        _w(t_, sn, cp_done[rj])
                        _w(t_, sn, rcp_done.get(rj - 2))
                        tn.inc(t_.matmul(rps[fbj][:], pperm_b[:], kbf[:, fbj, :],
                                         start=True, stop=True))
                        rh_done[rj] = tn.n
                        _w(s_, tn, rh_done[rj])
                        _w(s_, gn, ga_done.get(rj - 2))
                        sn.inc(s_.activation(rbf[:, fbj, :], rps[fbj][:], AF.Copy))
                        rcp_done[rj] = sn.n
                        _w(g_, sn, rcp_done[rj])
                        _wt(g_, cs_tok[gj])
                        _w(g_, gn, ga_done.get(rj - 2))
                        g_.tensor_mul(kc[:, fbj, :], kbf[:, fbj, :],
                                      cos1[:, gj % 2, :])
                        g_.tensor_mul(rs[:, fbj, :], rbf[:, fbj, :],
                                      sin1[:, gj % 2, :])
                        g_.drain()
                        _wt(g_, kst_tok.get(rj - 2))
                        gn.inc(g_.tensor_add(krot[:, fbj, :], kc[:, fbj, :],
                                             rs[:, fbj, :]))
                        ga_done[rj] = gn.n
                        _w(y_, gn, ga_done[rj])
                        ke = dKs[fbj]
                        y_.dma_start(k_s[fj, :, gj * CH:(gj + 1) * CH], krot[:, fbj, :])\
                            .then_inc(ke.h, 16)
                        ke.n += 16
                        kst_tok[rj] = (ke, ke.n)

                    for f in range(KV):
                        ri = g * 8 + f
                        fb = ri % 2
                        if ri == 0:
                            _w(t_, dWK, wk_ready)
                        if f == 0:
                            _w(t_, vn, ev_done[(g * 4 + 3, 1)])
                        _w(t_, sn, cp_done.get(ri - 2))
                        ins = None
                        for dc in range(DC):
                            ins = t_.matmul(kps[fb][:],
                                            wk_sb[:, dc, f * 128:(f + 1) * 128],
                                            xnf[:, dc, :],
                                            start=(dc == 0), stop=(dc == DC - 1))
                        tn.inc(ins)
                        kmm_done[ri] = tn.n
                        _w(s_, tn, kmm_done[ri])
                        _w(s_, tn, rh_done.get(ri - 2))
                        _w(s_, gn, ga_done.get(ri - 2))
                        sn.inc(s_.activation(kbf[:, fb, :], kps[fb][:], AF.Copy))
                        cp_done[ri] = sn.n
                        if ri >= 1:
                            _rope_fin(ri - 1)
                    _rope_fin(NG * 8 - 1) if g == NG - 1 else None
            quiesce()

        if 2 in phases:
            with ExitStack() as _st2:
                wv_sb = _st2.enter_context(nc.sbuf_tensor("wv_sb", [128, DC, KV * HD], BF16))
                xng = _st2.enter_context(nc.sbuf_tensor("xng", [128, 2, DC, CH], BF16))
                vbf = _st2.enter_context(nc.sbuf_tensor("vbf", [128, 2, KV * HD], BF16))
                vps = (_st2.enter_context(nc.psum_tensor("vps0", [128, 1024], F32)),
                       _st2.enter_context(nc.psum_tensor("vps1", [128, 1024], F32)))
                wv_r = wv_in.rearrange("(dc p) f -> p dc f", p=128)
                for q in range(4):
                    g_.dma_start(wv_sb[:, q * 8:(q + 1) * 8, :],
                                 wv_r[:, q * 8:(q + 1) * 8, :]).then_inc(dWV.h, 16)
                    dWV.n += 16
                wv_ready = dWV.n
                xg_tok = {}
                vmm_done = {}
                vcp_done = {}
                vst_tok = {}
                for g in range(NG):
                    _w(y_, tn, vmm_done.get((g - 2, 3)))
                    xre = dXRs[g % 2]
                    y_.dma_start(xng[:, g % 2, :, :],
                                 xn_s.ap()[:, :, g * CH:(g + 1) * CH]
                                 .rearrange("dc p t -> p dc t")).then_inc(xre.h, 16)
                    xre.n += 16
                    xg_tok[g] = (xre, xre.n)
                    for tt in range(4):
                        Ti = g * 4 + tt
                        sl = Ti % 2
                        if Ti == 0:
                            _w(t_, dWV, wv_ready)
                        _wt(t_, xg_tok[g])
                        _w(t_, sn, vcp_done.get((g, tt - 2),
                                                vcp_done.get((g - 1, tt + 2))))
                        ins = None
                        for dc in range(DC):
                            for half in range(2):
                                ins = t_.matmul(vps[sl][:, half * 512:(half + 1) * 512],
                                                xng[:, g % 2, dc, tt * 128:(tt + 1) * 128],
                                                wv_sb[:, dc, half * 512:(half + 1) * 512],
                                                start=(dc == 0), stop=(dc == DC - 1))
                        tn.inc(ins)
                        vmm_done[(g, tt)] = tn.n
                        _w(s_, tn, vmm_done[(g, tt)])
                        _wt(s_, vst_tok.get((g, tt - 2), vst_tok.get((g - 1, tt + 2))))
                        sn.inc(s_.activation(vbf[:, sl, :], vps[sl][:], AF.Copy))
                        vcp_done[(g, tt)] = sn.n
                        _w(y_, sn, vcp_done[(g, tt)])
                        ve = dVs[sl]
                        y_.dma_start(v_s.ap()[:, Ti, :, :].rearrange("kv p d -> p kv d"),
                                     vbf.ap()[:, sl, :].rearrange("p (kv d) -> p kv d", d=HD))\
                            .then_inc(ve.h, 16)
                        ve.n += 16
                        vst_tok[(g, tt)] = (ve, ve.n)
            quiesce()

        if 3 in phases:
            with ExitStack() as _st3:
                def _sb3(name, shape, dtype):
                    return _st3.enter_context(nc.sbuf_tensor(name, shape, dtype))
                xno = _sb3("xno", [128, DC, NOWN], BF16)
                gu_sb = _sb3("gu_sb", [128, 2, 2, DC, 128], BF16)
                silu_sb = _sb3("silu_sb", [128, 2, NOWN], BF16)
                hst = _sb3("hst", [128, 2, NOWN], BF16)
                gps = (_st3.enter_context(nc.psum_tensor("gps0", [128, NOWN], F32)),
                       _st3.enter_context(nc.psum_tensor("gps1", [128, NOWN], F32)))
                ups = (_st3.enter_context(nc.psum_tensor("ups0", [128, NOWN], F32)),
                       _st3.enter_context(nc.psum_tensor("ups1", [128, NOWN], F32)))
                y_.dma_start(xno[:], xn_s.ap()[:, :, 0:NOWN]
                             .rearrange("dc p t -> p dc t")).then_inc(dONE.h, 16)
                dONE.n += 16
                xno_rd = dONE.n
                ga_r = gate_rl.rearrange("(dc p) f -> p dc f", p=128)
                up_r = up_rl.rearrange("(dc p) f -> p dc f", p=128)
                gumm = {}
                sil_done = {}
                hstt = {}
                h_tok = {}
                for c in range(FC):
                    cb = c % 2
                    _w(g_, tn, gumm.get(c - 2))
                    gue = dGUs[cb]
                    g_.dma_start(gu_sb[:, 0, cb, :, :],
                                 ga_r[:, :, c * 128:(c + 1) * 128]).then_inc(gue.h, 16)
                    g_.dma_start(gu_sb[:, 1, cb, :, :],
                                 up_r[:, :, c * 128:(c + 1) * 128]).then_inc(gue.h, 16)
                    gue.n += 32
                    _w(t_, gue, gue.n)
                    if c == 0:
                        _w(t_, dONE, xno_rd)
                    _w(t_, sn, sil_done.get(c - 2))
                    _w(t_, vn, hstt.get(c - 2))
                    ins = None
                    for dc in range(DC):
                        st0, st1 = (dc == 0), (dc == DC - 1)
                        for gg in range(2):
                            t_.matmul(gps[cb][:, gg * CH:(gg + 1) * CH],
                                      gu_sb[:, 0, cb, dc, :],
                                      xno[:, dc, gg * CH:(gg + 1) * CH],
                                      start=st0, stop=st1)
                        for gg in range(2):
                            ins = t_.matmul(ups[cb][:, gg * CH:(gg + 1) * CH],
                                            gu_sb[:, 1, cb, dc, :],
                                            xno[:, dc, gg * CH:(gg + 1) * CH],
                                            start=st0, stop=st1)
                    tn.inc(ins)
                    gumm[c] = tn.n
                    _w(s_, tn, gumm[c])
                    _w(s_, vn, hstt.get(c - 2))
                    ins = None
                    for gg in range(2):
                        ins = s_.activation(silu_sb[:, cb, gg * CH:(gg + 1) * CH],
                                            gps[cb][:, gg * CH:(gg + 1) * CH],
                                            AF.Silu, scale=gs_sb[:, c:c + 1])
                    sn.inc(ins)
                    sil_done[c] = sn.n
                    _w(v_, sn, sil_done[c])
                    _wt(v_, h_tok.get(c - 2))
                    ins = None
                    for gg in range(2):
                        ins = v_.scalar_tensor_tensor(
                            hst[:, cb, gg * CH:(gg + 1) * CH],
                            ups[cb][:, gg * CH:(gg + 1) * CH],
                            us_sb[:, c:c + 1],
                            silu_sb[:, cb, gg * CH:(gg + 1) * CH],
                            ALU.mult, ALU.mult)
                    vn.inc(ins)
                    hstt[c] = vn.n
                    _w(y_, vn, hstt[c])
                    he = dHs[cb]
                    y_.dma_start(h_s[c, :, :], hst[:, cb, :]).then_inc(he.h, 16)
                    he.n += 16
                    h_tok[c] = (he, he.n)
                h_all = [(dHs[0], dHs[0].n), (dHs[1], dHs[1].n)]
            quiesce()

            with ExitStack() as _st3b:
                def _sb3b(name, shape, dtype):
                    return _st3b.enter_context(nc.sbuf_tensor(name, shape, dtype))
                h_g = _sb3b("h_g", [128, FC, CH], BF16)
                dn_sb = _sb3b("dn_sb", [128, 2, FC, 128], BF16)
                dsc = _sb3b("dsc", [128, 2, CH], F32)
                stage = _sb3b("stage", [128, 2, 4, 4, 128], F32)
                dps = (_st3b.enter_context(nc.psum_tensor("dps0", [128, CH], F32)),
                       _st3b.enter_context(nc.psum_tensor("dps1", [128, CH], F32)))
                tps = (_st3b.enter_context(nc.psum_tensor("tps0", [128, CH], F32)),
                       _st3b.enter_context(nc.psum_tensor("tps1", [128, CH], F32)))
                dn_r = down_rl.rearrange("(fc p) d -> p fc d", p=128)
                dmm = {}
                dsc_done = {}
                tr2_done = {}
                stg_done = {}
                fl_tok = {}
                hg_tok = None
                for g in range(2):
                    _w(y_, tn, dmm.get((g - 1) * DC + DC - 1))
                    for tok_ in h_all:
                        _wt(y_, tok_)
                    hge = dHRs[g % 2]
                    for hh in range(2):
                        y_.dma_start(h_g[:, hh * 43:(hh + 1) * 43, :],
                                     h_s.ap()[hh * 43:(hh + 1) * 43, :,
                                              g * CH:(g + 1) * CH]
                                     .rearrange("fc p t -> p fc t")).then_inc(hge.h, 16)
                    hge.n += 32
                    hg_tok = (hge, hge.n)
                    def _dn_fin(dj):
                        gj, dmj = dj // DC, dj % DC
                        dbj = dj % 2
                        _w(t_, vn, dsc_done[dj])
                        _w(t_, vn, stg_done.get(dj - 2))
                        ins2 = None
                        for tt in range(4):
                            ins2 = t_.transpose(tps[dbj][:, tt * 128:(tt + 1) * 128],
                                                dsc[:, dbj, tt * 128:(tt + 1) * 128],
                                                ident_f[:])
                        tn.inc(ins2)
                        tr2_done[dj] = tn.n
                        sbj = (dj // 4) % 2
                        _w(v_, tn, tr2_done[dj])
                        if dj % 4 == 0:
                            _wt(v_, fl_tok.get(dj - 5))
                        vn.inc(v_.tensor_copy(
                            stage.ap()[:, sbj, :, dmj % 4, :],
                            tps[dbj].ap().rearrange("p (tt u) -> p tt u", u=128)))
                        stg_done[dj] = vn.n
                        if dmj % 4 == 3:
                            _w(y_, vn, stg_done[dj])
                            oae = dOAs[sbj]
                            for tt in range(4):
                                y_.dma_start(
                                    out_own[gj * CH + tt * 128:gj * CH + (tt + 1) * 128,
                                            (dmj - 3) * 128:(dmj + 1) * 128],
                                    stage[:, sbj, tt, :, :]).then_inc(oae.h, 16)
                            oae.n += 64
                            fl_tok[dj] = (oae, oae.n)

                    for dm in range(DC):
                        di = g * DC + dm
                        db = di % 2
                        _w(g_, tn, dmm.get(di - 2))
                        dne = dDNs[db]
                        for hh in range(2):
                            g_.dma_start(dn_sb[:, db, hh * 43:(hh + 1) * 43, :],
                                         dn_r[:, hh * 43:(hh + 1) * 43,
                                              dm * 128:(dm + 1) * 128]).then_inc(dne.h, 16)
                        dne.n += 32
                        _w(t_, dne, dne.n)
                        if dm == 0:
                            _wt(t_, hg_tok)
                        _w(t_, vn, dsc_done.get(di - 2))
                        ins = None
                        for fc in range(FC):
                            ins = t_.matmul(dps[db][:], dn_sb[:, db, fc, :],
                                            h_g[:, fc, :],
                                            start=(fc == 0), stop=(fc == FC - 1))
                        tn.inc(ins)
                        dmm[di] = tn.n
                        _w(v_, tn, dmm[di])
                        _w(v_, tn, tr2_done.get(di - 2))
                        vn.inc(v_.tensor_scalar_mul(dsc[:, db, :], dps[db][:],
                                                    ds_sb[:, dm:dm + 1]))
                        dsc_done[di] = vn.n
                        if di >= 1:
                            _dn_fin(di - 1)
                        if g == 1 and dm == DC - 1:
                            _dn_fin(di)
                p3_out = [(dOAs[0], dOAs[0].n), (dOAs[1], dOAs[1].n)]
            quiesce()
        else:
            p3_out = []

        q_sb = None
        if 4 in phases:
            _stq = ExitStack()
            q_sb = _stq.enter_context(nc.sbuf_tensor("q_sb", [128, H, NOWN], BF16))
            with ExitStack() as _st4:
                def _sb4(name, shape, dtype):
                    return _st4.enter_context(nc.sbuf_tensor(name, shape, dtype))
                xno2 = _sb4("xno2", [128, DC, NOWN], BF16)
                wq_sb = _sb4("wq_sb", [128, 2, DC, 128], BF16)
                qbf = _sb4("qbf", [128, 2, NOWN], BF16)
                qc = _sb4("qc", [128, 2, NOWN], F32)
                rs2 = _sb4("rs2", [128, 2, NOWN], F32)
                cs_own = _sb4("cs_own", [128, 2, NOWN], F32)
                qps = (_st4.enter_context(nc.psum_tensor("qps0", [128, NOWN], F32)),
                       _st4.enter_context(nc.psum_tensor("qps1", [128, NOWN], F32)))
                rqps = (_st4.enter_context(nc.psum_tensor("rqps0", [128, NOWN], F32)),
                        _st4.enter_context(nc.psum_tensor("rqps1", [128, NOWN], F32)))
                y_.dma_start(xno2[:], xn_s.ap()[:, :, 0:NOWN]
                             .rearrange("dc p t -> p dc t")).then_inc(dONE.h, 16)
                y_.dma_start(cs_own[:, 0, :], cosq[:, 0:NOWN]).then_inc(dONE.h, 16)
                y_.dma_start(cs_own[:, 1, :], sinq[:, 0:NOWN]).then_inc(dONE.h, 16)
                dONE.n += 48
                p4_rd = dONE.n
                wq_r = wq_in.rearrange("(dc p) f -> p dc f", p=128)
                qmm, qcp, qrh, qcd, qrs, qga = {}, {}, {}, {}, {}, {}

                def _qrope_fin(fj):
                    fbj = fj % 2
                    _w(t_, sn, qcp[fj])
                    _w(t_, vn, qrs.get(fj - 2))
                    ins2 = None
                    for gg in range(2):
                        ins2 = t_.matmul(rqps[fbj][:, gg * CH:(gg + 1) * CH], pperm_b[:],
                                         qbf[:, fbj, gg * CH:(gg + 1) * CH],
                                         start=True, stop=True)
                    tn.inc(ins2)
                    qrh[fj] = tn.n
                    _w(v_, sn, qcp[fj])
                    _w(v_, gn, qga.get(fj - 2))
                    vn.inc(v_.tensor_mul(qc[:, fbj, :], qps[fbj][:], cs_own[:, 0, :]))
                    qcd[fj] = vn.n
                    _w(v_, tn, qrh[fj])
                    vn.inc(v_.tensor_mul(rs2[:, fbj, :], rqps[fbj][:], cs_own[:, 1, :]))
                    qrs[fj] = vn.n
                    _w(g_, vn, qrs[fj])
                    gn.inc(g_.tensor_add(q_sb[:, fj, :], qc[:, fbj, :], rs2[:, fbj, :]))
                    qga[fj] = gn.n

                for f in range(H):
                    fb = f % 2
                    _w(g_, tn, qmm.get(f - 2))
                    wqe = dWQs[fb]
                    g_.dma_start(wq_sb[:, fb, :, :],
                                 wq_r[:, :, f * 128:(f + 1) * 128]).then_inc(wqe.h, 16)
                    wqe.n += 16
                    _w(t_, wqe, wqe.n)
                    if f == 0:
                        _w(t_, dONE, p4_rd)
                    _w(t_, vn, qcd.get(f - 2))
                    ins = None
                    for dc in range(DC):
                        for gg in range(2):
                            ins = t_.matmul(qps[fb][:, gg * CH:(gg + 1) * CH],
                                            wq_sb[:, fb, dc, :],
                                            xno2[:, dc, gg * CH:(gg + 1) * CH],
                                            start=(dc == 0), stop=(dc == DC - 1))
                    tn.inc(ins)
                    qmm[f] = tn.n
                    _w(s_, tn, qmm[f])
                    _w(s_, tn, qrh.get(f - 2))
                    sn.inc(s_.activation(qbf[:, fb, :], qps[fb][:], AF.Copy))
                    qcp[f] = sn.n
                    if f >= 1:
                        _qrope_fin(f - 1)
                _qrope_fin(H - 1)
                if "q_dbg" in dbg:
                    q_dbg = nc.dram_tensor("q_dbg", [H, 128, NOWN], BF16,
                                           kind="ExternalOutput")
                    _w(y_, gn, gn.n)
                    for f in range(H):
                        y_.dma_start(q_dbg[f, :, :], q_sb[:, f, :]).then_inc(dONE.h, 16)
                        dONE.n += 16
                    y_.wait_ge(dONE.h, dONE.n)
            quiesce()

        if 5 in phases:
            _sto = ExitStack()
            o_sb = _sto.enter_context(nc.sbuf_tensor("o_sb", [128, H, NOWN], BF16,
                                                     side="right"))
            with ExitStack() as _st5:
                def _sb5(name, shape, dtype):
                    return _st5.enter_context(nc.sbuf_tensor(name, shape, dtype))
                k_sb = _sb5("k_sb", [128, 2, NPRE], BF16)
                v_sb = _sb5("v_sb", [128, 2, NT, HD], BF16)
                pt = _sb5("pt", [128, 2, 1024], BF16)
                ptraw = _sb5("ptraw", [128, 2, 1024], BF16)
                rm = _sb5("rm", [128, 8], F32)
                mx_sb = _sb5("mx_sb", [1, 2], F32)
                mbias_all = _sb5("mbias_all", [128, 64], F32)
                rc2 = _sb5("rc2", [1, 2, CH], F32)
                bc_sb = _sb5("bc_sb", [128, 2, CH], F32)
                sst = (_st5.enter_context(nc.psum_tensor("sst0", [128, 1024], F32)),
                       _st5.enter_context(nc.psum_tensor("sst1", [128, 1024], F32)))

                with ExitStack() as _stp:
                    kdiag = _stp.enter_context(
                        nc.sbuf_tensor("kdiag", [128, KV, 1024], BF16))
                    mbps2 = _stp.enter_context(nc.psum_tensor("mbps2", [128, 2], F32))
                    for kv in range(KV):
                        y_.dma_start(kdiag[:, kv, :], k_s[kv, :, 0:1024])\
                            .then_inc(dONE.h, 16)
                        dONE.n += 16
                    kdiag_rd = dONE.n
                    red_done, gp_done, bcp_done, ts_done = {}, {}, {}, {}
                    for hg in range(64):
                        kv, j, g = hg // 8, (hg // 2) % 4, hg % 2
                        h = kv * 4 + j
                        if hg == 0:
                            _w(t_, dONE, kdiag_rd)
                            _w(t_, dPW, perw_done)
                            _w(t_, dP, per_done)
                        _w(t_, vn, red_done.get(hg - 1))
                        sc_u = []
                        for u in range(4):
                            buf, half = sst[u // 2], u % 2
                            tn.inc(t_.matmul(buf[:, half * CH:(half + 1) * CH],
                                             kdiag[:, kv, (g * 4 + u) * 128:
                                                   (g * 4 + u + 1) * 128],
                                             q_sb[:, h, g * CH:(g + 1) * CH],
                                             start=True, stop=True,
                                             skip_group_check=True))
                            sc_u.append(tn.n)
                        o4 = (hg % 2) * 4
                        ins = None
                        for u in range(4):
                            buf, half = sst[u // 2], u % 2
                            _w(v_, tn, sc_u[u])
                            if u == 0:
                                _w(v_, gn, gp_done.get(hg - 2))
                            ins = v_.reduce_max(rm[:, o4 + u:o4 + u + 1],
                                                buf[:, half * CH:(half + 1) * CH],
                                                axis=AX.X)
                        vn.inc(ins)
                        red_done[hg] = vn.n
                        _w(g_, vn, red_done[hg])
                        _w(g_, tn, bcp_done.get(hg - 2))
                        gn.inc(g_.tensor_reduce(mx_sb[0:1, hg % 2:hg % 2 + 1],
                                                rm[:, o4:o4 + 4],
                                                axis=AX.XYZWC, op=ALU.max))
                        gp_done[hg] = gn.n
                        _w(t_, gn, gp_done[hg])
                        _w(t_, vn, ts_done.get(hg - 2))
                        tn.inc(t_.matmul(mbps2[:, hg % 2:hg % 2 + 1], ones1_f[:],
                                         mx_sb[0:1, hg % 2:hg % 2 + 1],
                                         start=True, stop=True,
                                         skip_group_check=True))
                        bcp_done[hg] = tn.n
                        _w(v_, tn, bcp_done[hg])
                        vn.inc(v_.tensor_scalar_mul(mbias_all[:, hg:hg + 1],
                                                    mbps2[:, hg % 2:hg % 2 + 1],
                                                    -ISQ))
                        ts_done[hg] = vn.n
                    _stp.close()

                o_psd = (_st5.enter_context(nc.psum_tensor("o_ps0", [128, CH], F32)),
                         _st5.enter_context(nc.psum_tensor("o_ps1", [128, CH], F32)))
                smbc = (_st5.enter_context(nc.psum_tensor("smbc0", [128, CH], F32)),
                        _st5.enter_context(nc.psum_tensor("smbc1", [128, CH], F32)))
                prep_t, prep_v, prep_g = tn.n, vn.n, gn.n

                engines = [t_, v_, s_, g_, y_]
                idxr = []
                for eng in engines:
                    _w(eng, dP, per_done)
                    _w(eng, dPW, perw_done)
                    _w(eng, tn, prep_t)
                    _w(eng, vn, prep_v)
                    _w(eng, gn, prep_g)
                    idxr.append(eng.value_load(cid_sb[:]))

                sems = [tn, vn, sn, gn, dONE] + dKVs
                snap0 = [s.n for s in sems]
                finals = None
                for c0 in nc.Switch(engines=engines, index=idxr, n=8):
                    for s, n0 in zip(sems, snap0):
                        s.n = n0
                    exp_done, oav_done, sc_done, mask_done = {}, {}, {}, {}
                    kv_tok, av_last = {}, {}
                    oav_stop, norm_done = {}, {}
                    ptraw_prev = {0: None, 1: None}
                    gi = 0
                    hg = 0

                    def _norm_fin(hp):
                        kvp = hp // 8
                        hp_h = kvp * 4 + (hp // 2) % 4
                        hp_g = hp % 2
                        p = hp % 2
                        _w(v_, tn, oav_stop[hp])
                        v_.reciprocal(bc_sb[:, p, :], smbc[p][:])
                        v_.drain()
                        vn.inc(v_.tensor_mul(o_sb[:, hp_h, hp_g * CH:(hp_g + 1) * CH],
                                             o_psd[p][:], bc_sb[:, p, :]))
                        norm_done[hp] = vn.n

                    for kv in range(KV):
                        kb = kv % 2
                        _w(y_, tn, av_last.get(kv - 2))
                        kve = dKVs[kb]
                        y_.dma_start(k_sb[:, kb, :], k_s[kv, :, :]).then_inc(kve.h, 16)
                        y_.dma_start(v_sb[:, kb, :, :],
                                     v_s.ap()[kv, :, :, :].rearrange("kt p d -> p kt d"))\
                            .then_inc(kve.h, 16)
                        kve.n += 32
                        kv_tok[kv] = (kve, kve.n)
                        for j in range(4):
                            h = kv * 4 + j
                            for g in range(2):
                                if g == 0:
                                    kchunks = [0] + [2 + jj for jj in range(c0)]
                                else:
                                    kchunks = [1] + [2 + c0 + jj for jj in range(7 - c0)]
                                halves = [(kcx, hf) for kcx in kchunks for hf in range(2)]
                                nh = len(halves)
                                g0 = gi
                                pend = []
                                for idx, (kcx, hf) in enumerate(halves):
                                    sb2 = gi % 2
                                    if j == 0 and g == 0 and idx == 0:
                                        _wt(t_, kv_tok[kv])
                                    _w(t_, sn, exp_done.get(gi - 2))
                                    ins = None
                                    for u in range(2):
                                        ktile = kcx * 4 + hf * 2 + u
                                        ins = t_.matmul(
                                            sst[sb2][:, u * CH:(u + 1) * CH],
                                            k_sb[:, kb, ktile * 128:(ktile + 1) * 128],
                                            q_sb[:, h, g * CH:(g + 1) * CH],
                                            start=True, stop=True,
                                            skip_group_check=True)
                                    tn.inc(ins)
                                    sc_done[gi] = tn.n
                                    if idx >= 2:
                                        a = pend.pop(0)
                                        _emit_oav(t_, tn, vn, sn, exp_done, mask_done,
                                                  oav_done, onesm_b, v_sb, pt,
                                                  o_psd[hg % 2], smbc[hg % 2], kb,
                                                  a[0], a[1], a[2], a[3], g0, nh,
                                                  norm_done.get(hg - 2))
                                    _w(s_, tn, sc_done[gi])
                                    _w(s_, tn, oav_done.get(gi - 2))
                                    if idx <= 1:
                                        _w(s_, vn, ptraw_prev[idx])
                                        dst = ptraw[:, sb2, :]
                                    else:
                                        dst = pt[:, sb2, :]
                                    sn.inc(s_.activation(dst, sst[sb2][:], AF.Exp,
                                                         bias=mbias_all[:, hg:hg + 1],
                                                         scale=ISQ))
                                    exp_done[gi] = sn.n
                                    if idx <= 1:
                                        _w(v_, sn, exp_done[gi])
                                        vn.inc(v_.tensor_mul(
                                            pt[:, sb2, :], ptraw[:, sb2, :],
                                            maskt_b[:, idx * 1024:(idx + 1) * 1024]))
                                        mask_done[gi] = vn.n
                                        ptraw_prev[idx] = vn.n
                                    pend.append((gi, idx, kcx, hf))
                                    gi += 1
                                while pend:
                                    a = pend.pop(0)
                                    _emit_oav(t_, tn, vn, sn, exp_done, mask_done,
                                              oav_done, onesm_b, v_sb, pt,
                                              o_psd[hg % 2], smbc[hg % 2], kb,
                                              a[0], a[1], a[2], a[3], g0, nh,
                                              norm_done.get(hg - 2))
                                av_last[kv] = tn.n
                                oav_stop[hg] = oav_done[gi - 1]
                                if hg >= 1:
                                    _norm_fin(hg - 1)
                                hg += 1
                    _norm_fin(63)
                    if finals is None:
                        finals = [s.n for s in sems]
                    else:
                        assert finals == [s.n for s in sems], \
                            f"arm {c0} sem counts differ"
                for s, nf in zip(sems, finals):
                    s.n = nf
                if "o_dbg" in dbg:
                    o_dbg2 = nc.dram_tensor("o_dbg", [H, 128, NOWN], BF16,
                                            kind="ExternalOutput")
                    _w(y_, vn, vn.n)
                    for h in range(H):
                        y_.dma_start(o_dbg2[h, :, :], o_sb[:, h, :]).then_inc(dONE.h, 16)
                        dONE.n += 16
                    y_.wait_ge(dONE.h, dONE.n)
            quiesce()
        if q_sb is not None:
            _stq.close()

        if 6 in phases:
            with ExitStack() as _st6:
                def _sb6(name, shape, dtype):
                    return _st6.enter_context(nc.sbuf_tensor(name, shape, dtype))
                wo_sb = _sb6("wo_sb", [128, 2, H, CH], BF16)
                xres = _sb6("xres", [128, 4, CH], F32)
                res_sb = _sb6("res_sb", [128, 4, CH], F32)
                p6 = [_st6.enter_context(nc.psum_tensor(f"p6_{t}", [128, CH], F32))
                      for t in range(8)]
                wo_r = wo_in.rearrange("(hh p) d -> p hh d", p=128)
                p6mm = {}
                res_done = {}
                ac_tok = {}
                xr_tok = {}
                for dm in range(8):
                    db = dm % 2
                    _w(g_, tn, p6mm.get((dm - 2, 7)))
                    woe = dWOs[db]
                    g_.dma_start(wo_sb[:, db, :, :],
                                 wo_r[:, :, dm * CH:(dm + 1) * CH]).then_inc(woe.h, 16)
                    woe.n += 16
                    _w(t_, woe, woe.n)
                    for th in range(2):
                        hx = dm * 2 + th
                        _w(t_, vn, res_done.get((dm - 1, th * 4 + 3)))
                        for hh in range(H):
                            for t4 in range(4):
                                t = th * 4 + t4
                                ins = t_.matmul(p6[t][:],
                                                o_sb[:, hh, t * 128:(t + 1) * 128],
                                                wo_sb[:, db, hh, :],
                                                start=(hh == 0), stop=(hh == H - 1),
                                                skip_group_check=True)
                                if hh == H - 1:
                                    tn.inc(ins)
                                    p6mm[(dm, t)] = tn.n
                        for t4 in range(4):
                            t = th * 4 + t4
                            ti = dm * 8 + t
                            sl = ti % 4
                            prev = (dm, t - 4) if t >= 4 else ((dm - 1, t + 4) if dm else None)
                            _w(y_, vn, res_done.get(prev))
                            xe6 = dXR6[sl]
                            y_.dma_start(xres[:, sl, :],
                                         x_pre[t * 128:(t + 1) * 128,
                                               dm * CH:(dm + 1) * CH]).then_inc(xe6.h, 16)
                            xe6.n += 16
                            xr_tok[(dm, t)] = (xe6, xe6.n)
                            _w(v_, tn, p6mm[(dm, t)])
                            _wt(v_, xr_tok[(dm, t)])
                            _wt(v_, ac_tok.get(prev))
                            vn.inc(v_.scalar_tensor_tensor(res_sb[:, sl, :], p6[t][:],
                                                           1.0, xres[:, sl, :],
                                                           ALU.mult, ALU.add))
                            res_done[(dm, t)] = vn.n
                            _w(g_, vn, res_done[(dm, t)])
                            if dm == 0 and t == 0:
                                for tok_ in p3_out:
                                    _wt(g_, tok_)
                            ace = dACs[sl]
                            g_.dma_start(out_own[t * 128:(t + 1) * 128,
                                                 dm * CH:(dm + 1) * CH],
                                         res_sb[:, sl, :],
                                         accum_op=ALU.add).then_inc(ace.h, 16)
                            ace.n += 16
                            ac_tok[(dm, t)] = (ace, ace.n)
                for ace in dACs:
                    g_.wait_ge(ace.h, ace.n)
        if 5 in phases:
            _sto.close()

    return nc



def _rope_tables():
    THETA, FACTOR, HI_FF, LO_FF, ORIG_MAX = 500000.0, 8.0, 4.0, 1.0, 8192
    inv = 1.0 / THETA ** (np.arange(0, HD, 2, dtype=np.float64) / HD)
    wavelen = 2.0 * np.pi / inv
    low_wl = ORIG_MAX / LO_FF
    high_wl = ORIG_MAX / HI_FF
    smooth = (ORIG_MAX / wavelen - LO_FF) / (HI_FF - LO_FF)
    scaled = np.where(wavelen > low_wl, inv / FACTOR, inv)
    mid = (wavelen <= low_wl) & (wavelen >= high_wl)
    scaled = np.where(mid, (1 - smooth) * inv / FACTOR + smooth * inv, scaled)
    inv_f = scaled.astype(np.float32)
    pos = np.arange(T, dtype=np.float32)
    freqs = pos[:, None] * inv_f[None, :]
    emb = np.concatenate([freqs, freqs], axis=-1)
    return np.cos(emb).astype(np.float32), np.sin(emb).astype(np.float32)


def _host_inputs(L0_attn_x, L0_attn_wq, L0_attn_wk, L0_attn_wv, L0_attn_wo,
                 L0_ffn_gate_w, L0_ffn_gate_s, L0_ffn_up_w, L0_ffn_up_s,
                 L0_ffn_down_w, L0_ffn_down_s):
    x = np.asarray(L0_attn_x, dtype=np.float32)
    cos_t, sin_t = _rope_tables()
    gate_rl_ = np.ascontiguousarray(np.asarray(L0_ffn_gate_w, dtype=np.int32).T)
    up_rl_ = np.ascontiguousarray(np.asarray(L0_ffn_up_w, dtype=np.int32).T)
    down_rl_ = np.ascontiguousarray(np.asarray(L0_ffn_down_w, dtype=np.int32).T)
    gs = np.ascontiguousarray(np.asarray(L0_ffn_gate_s, np.float32).reshape(FC, 128).T)
    us = np.ascontiguousarray(np.asarray(L0_ffn_up_s, np.float32).reshape(FC, 128).T)
    ds = np.ascontiguousarray(np.asarray(L0_ffn_down_s, np.float32).reshape(DC, 128).T)
    mt = np.zeros((128, 2048), dtype=np.float32)
    for j in range(4):
        kt = 128 * j + np.arange(128)[:, None]
        q = np.arange(512)[None, :]
        mt[:, j * 512:(j + 1) * 512] = (kt <= q).astype(np.float32)
    ident = np.eye(128, dtype=np.float32)
    pperm = np.zeros((128, 128), dtype=np.float32)
    for m in range(64):
        pperm[m + 64, m] = -1.0
    for m in range(64, 128):
        pperm[m - 64, m] = 1.0
    shared = dict(wk=np.ascontiguousarray(L0_attn_wk, np.float32),
                  wv=np.ascontiguousarray(L0_attn_wv, np.float32),
                  wq=np.ascontiguousarray(L0_attn_wq, np.float32),
                  wo=np.ascontiguousarray(L0_attn_wo, np.float32),
                  gate_rl=gate_rl_, up_rl=up_rl_, down_rl=down_rl_,
                  gs=gs, us=us, ds=ds, maskt=mt, identf=ident, pperm=pperm,
                  ones1=np.ones((1, 128), dtype=np.float32),
                  onesc=np.ones((128, 1), dtype=np.float32),
                  onesm=np.ones((128, 128), dtype=np.float32))
    in_maps = []
    for c in range(NC_):
        c0, c1 = c, 7 - c
        pos = np.concatenate([
            np.arange(c0 * CH, (c0 + 1) * CH),
            np.arange(c1 * CH, (c1 + 1) * CH),
            np.arange(0, c0 * CH),
            np.arange(0, c1 * CH),
        ])
        ba = np.concatenate([np.zeros(CH, np.int64), np.ones(CH, np.int64),
                             np.zeros(c0 * CH, np.int64), np.ones(c1 * CH, np.int64)])
        m = dict(shared)
        m.update(x_pre=np.ascontiguousarray(x[ba, pos, :]),
                 cosq=np.ascontiguousarray(cos_t[pos].T),
                 sinq=np.ascontiguousarray(sin_t[pos].T),
                 cid=np.array([[c]], dtype=np.int32))
        in_maps.append(m)
    return in_maps


_NC_CACHE = {}


def kernel(**inputs):
    in_maps = _host_inputs(**inputs)
    if "full" not in _NC_CACHE:
        _NC_CACHE["full"] = build()
    res = run_bass_kernel_spmd(_NC_CACHE["full"], in_maps, core_ids=list(range(NC_)))
    out = np.empty((B, T, D), dtype=np.float32)
    for c in range(NC_):
        oo = res.results[c]["out_own"]
        out[0, c * CH:(c + 1) * CH, :] = oo[0:CH]
        out[1, (7 - c) * CH:(8 - c) * CH, :] = oo[CH:NOWN]
    return out


# version 08
# speedup vs baseline: 1.2068x; 1.2068x over previous
import math
from contextlib import ExitStack
import numpy as np
import concourse.bass as bass
import concourse.mybir as mybir
from concourse.bass_utils import run_bass_kernel_spmd

F32 = mybir.dt.float32
BF16 = mybir.dt.bfloat16
I32 = mybir.dt.int32
AF = mybir.ActivationFunctionType
ALU = mybir.AluOpType
AX = mybir.AxisListType

B, T, D, H, KV, HD, DFF = 2, 4096, 4096, 32, 8, 128, 11008
NC_ = 8
CH = 512
NG = 9
NPRE = NG * CH
NT = NG * 4
NOWN = 1024
DC = D // 128
FC = DFF // 128
EPS = 1e-5
ISQ = 1.0 / math.sqrt(HD)
R4096 = 1.0 / 4096.0


class Sem:
    def __init__(self, nc, name):
        self.h = nc.alloc_semaphore(name)
        self.n = 0

    def inc(self, instr, k=1):
        instr.then_inc(self.h, k)
        self.n += k
        return self.n


def _w(eng, sem, val):
    if val is not None and val > 0:
        eng.wait_ge(sem.h, val)


def _wt(eng, token):
    if token is not None:
        sem, v = token
        _w(eng, sem, v)


def _emit_oav(t_, tn, vn, sn, exp_done, mask_done, oav_done, onesm_b, v_sb, pt,
              o_ps, sums_ps, kb, gi, idx, kc, hf, g0, nh, ott_prev):
    _w(t_, sn, exp_done[gi])
    if idx <= 1:
        _w(t_, vn, mask_done[gi])
    if idx == 0:
        _w(t_, vn, ott_prev)
    first = (idx == 0)
    last = (idx == nh - 1)
    ins = None
    for u in range(2):
        ktile = kc * 4 + hf * 2 + u
        st = first and (u == 0)
        sp = last and (u == 1)
        t_.matmul(sums_ps[:], onesm_b[:],
                  pt[:, gi % 2, u * CH:(u + 1) * CH],
                  start=st, stop=sp, skip_group_check=True)
        ins = t_.matmul(o_ps[:], v_sb[:, kb, ktile, :],
                        pt[:, gi % 2, u * CH:(u + 1) * CH],
                        start=st, stop=sp, skip_group_check=True)
    tn.inc(ins)
    oav_done[gi] = tn.n


def build(phases=(1, 2, 3, 4, 5, 6), dbg=()):
    nc = bass.Bass("TRN2", target_bir_lowering=False)
    t_, v_, s_, g_, y_ = nc.tensor, nc.vector, nc.scalar, nc.gpsimd, nc.sync

    def scr(name, shape, dtype):
        k = "ExternalOutput" if name in dbg else "Internal"
        return nc.dram_tensor(name, shape, dtype, kind=k)

    x_pre = nc.dram_tensor("x_pre", [NPRE, D], F32, kind="ExternalInput")
    cosq = nc.dram_tensor("cosq", [HD, NPRE], F32, kind="ExternalInput")
    sinq = nc.dram_tensor("sinq", [HD, NPRE], F32, kind="ExternalInput")
    wk_in = nc.dram_tensor("wk", [D, KV * HD], F32, kind="ExternalInput")
    wv_in = nc.dram_tensor("wv", [D, KV * HD], F32, kind="ExternalInput")
    wq_in = nc.dram_tensor("wq", [D, H * HD], F32, kind="ExternalInput")
    wo_in = nc.dram_tensor("wo", [H * HD, D], F32, kind="ExternalInput")
    gate_rl = nc.dram_tensor("gate_rl", [D, DFF], mybir.dt.int8, kind="ExternalInput")
    up_rl = nc.dram_tensor("up_rl", [D, DFF], mybir.dt.int8, kind="ExternalInput")
    down_r8 = nc.dram_tensor("down_r8", [FC, 128, D], mybir.dt.int8, kind="ExternalInput")
    gs_in = nc.dram_tensor("gs", [128, FC], F32, kind="ExternalInput")
    us_in = nc.dram_tensor("us", [128, FC], F32, kind="ExternalInput")
    dsb_in = nc.dram_tensor("dsb", [128, D], F32, kind="ExternalInput")
    maskt_in = nc.dram_tensor("maskt", [128, 2048], F32, kind="ExternalInput")
    ident_in = nc.dram_tensor("identf", [128, 128], F32, kind="ExternalInput")
    pperm_in = nc.dram_tensor("pperm", [128, 128], F32, kind="ExternalInput")
    ones1_in = nc.dram_tensor("ones1", [1, 128], F32, kind="ExternalInput")
    onesc_in = nc.dram_tensor("onesc", [128, 1], F32, kind="ExternalInput")
    onesm_in = nc.dram_tensor("onesm", [128, 128], F32, kind="ExternalInput")
    cid_in = nc.dram_tensor("cid", [1, 1], I32, kind="ExternalInput")
    out_own = nc.dram_tensor("out_own", [NOWN, D], F32, kind="ExternalOutput")

    xn_s = scr("xn_s", [DC, 128, NPRE], BF16)
    k_s = scr("k_s", [KV, HD, NPRE], BF16)
    v_s = scr("v_s", [KV, NT, 128, HD], BF16)
    h_s = scr("h_s", [FC, 128, NOWN], BF16)
    wq_b = scr("wq_b", [H, 128, DC * 128], BF16)
    wo_b = scr("wo_b", [8, 128, H * CH], BF16)

    tn = Sem(nc, "tn")
    vn = Sem(nc, "vn")
    sn = Sem(nc, "sn")
    gn = Sem(nc, "gn")
    dP = Sem(nc, "dP")
    dPW = Sem(nc, "dPW")
    dWK = Sem(nc, "dWK")
    dWV = Sem(nc, "dWV")
    dXs = [Sem(nc, "dX0"), Sem(nc, "dX1"), Sem(nc, "dX2")]
    dCSs = [Sem(nc, "dCS0"), Sem(nc, "dCS1")]
    dXNs = [Sem(nc, "dXN0"), Sem(nc, "dXN1")]
    dKs = [Sem(nc, "dK0"), Sem(nc, "dK1")]
    dVs = [Sem(nc, "dV0"), Sem(nc, "dV1")]
    dXRs = [Sem(nc, "dXR0"), Sem(nc, "dXR1")]
    dGUs = [Sem(nc, "dGU0"), Sem(nc, "dGU1")]
    dHs = [Sem(nc, "dH0"), Sem(nc, "dH1")]
    dHRs = [Sem(nc, "dHR0"), Sem(nc, "dHR1")]
    dDNs = [Sem(nc, "dDN0"), Sem(nc, "dDN1")]
    dOAs = [Sem(nc, "dOA0"), Sem(nc, "dOA1")]
    dWQs = [Sem(nc, "dWQ0"), Sem(nc, "dWQ1")]
    dONE = Sem(nc, "dONE")
    dKVs = [Sem(nc, "dKV0"), Sem(nc, "dKV1")]
    dWOs = [Sem(nc, "dWO0"), Sem(nc, "dWO1")]
    dXR6 = [Sem(nc, "dXR6a"), Sem(nc, "dXR6b"), Sem(nc, "dXR6c"), Sem(nc, "dXR6d")]
    dACs = [Sem(nc, "dAC0"), Sem(nc, "dAC1"), Sem(nc, "dAC2"), Sem(nc, "dAC3")]
    dQC = [Sem(nc, "dQC0"), Sem(nc, "dQC1")]
    dQS = [Sem(nc, "dQS0"), Sem(nc, "dQS1")]
    dOC = Sem(nc, "dOC")
    dOS = Sem(nc, "dOS")

    all_dma_sems = ([tn, vn, sn, gn, dP, dPW, dWK, dWV, dONE] + dXs + dCSs + dXNs
                    + dKs + dVs + dXRs + dGUs + dHs + dHRs + dDNs + dOAs + dWQs
                    + dKVs + dWOs + dXR6 + dACs + dQC + dQS + [dOC, dOS])

    def quiesce():
        for sem in all_dma_sems:
            if sem.n > 0:
                y_.wait_ge(sem.h, sem.n)
                g_.wait_ge(sem.h, sem.n)
        nc.all_engine_barrier()

    with ExitStack() as _st0:
        def _sb(name, shape, dtype):
            return _st0.enter_context(nc.sbuf_tensor(name, shape, dtype))
        ident_b = _sb("ident_b", [128, 128], BF16)
        ident_f = _sb("ident_f", [128, 128], F32)
        pperm_b = _sb("pperm_b", [128, 128], BF16)
        ones1_f = _sb("ones1_f", [1, 128], F32)
        onesc_b = _sb("onesc_b", [128, 1], BF16)
        onesm_b = _sb("onesm_b", [128, 128], BF16)
        maskt_b = _sb("maskt_b", [128, 2048], BF16)
        gs_sb = _sb("gs_sb", [128, FC], F32)
        us_sb = _sb("us_sb", [128, FC], F32)
        cid_sb = _sb("cid_sb", [1, 1], I32)
        stat = _sb("stat", [128, 24], F32)
        for ap, src in ((ident_f, ident_in), (ones1_f, ones1_in), (gs_sb, gs_in),
                        (us_sb, us_in), (cid_sb, cid_in)):
            dP.n = (y_.dma_start(ap[:], src[:]).then_inc(dP.h, 16), dP.n + 16)[1]
        for ap, src in ((ident_b, ident_in), (pperm_b, pperm_in),
                        (onesc_b, onesc_in), (maskt_b, maskt_in),
                        (onesm_b, onesm_in)):
            dPW.n = (g_.dma_start(ap[:], src[:]).then_inc(dPW.h, 16), dPW.n + 16)[1]
        per_done = dP.n
        perw_done = dPW.n

        if 1 in phases:
            with ExitStack() as _st1:
                def _sb1(name, shape, dtype):
                    return _st1.enter_context(nc.sbuf_tensor(name, shape, dtype))
                def _ps1(name, shape, dtype):
                    return _st1.enter_context(nc.psum_tensor(name, shape, dtype))
                wk_sb = _sb1("wk_sb", [128, DC, KV * HD], BF16)
                x_sb = _sb1("x_sb", [128, 3, D], F32)
                xnb = _sb1("xnb", [128, 3, D], BF16)
                xnf = _sb1("xnf", [128, DC, CH], BF16)
                sqd = _sb1("sqd", [128, D], BF16)
                cos1 = _sb1("cos1", [128, 2, CH], F32)
                sin1 = _sb1("sin1", [128, 2, CH], F32)
                kbf = _sb1("kbf", [128, 2, CH], BF16)
                kc = _sb1("kc", [128, 2, CH], F32)
                rs = _sb1("rs", [128, 2, CH], F32)
                krot = _sb1("krot", [128, 2, CH], BF16)
                rbf = _sb1("rbf", [128, 2, CH], BF16)
                pst0 = _ps1("pst0", [128, 2048], BF16)
                pst1 = _ps1("pst1", [128, 2048], BF16)
                kps = (_ps1("kps0", [128, CH], F32), _ps1("kps1", [128, CH], F32))
                rps = (_ps1("rps0", [128, CH], F32), _ps1("rps1", [128, CH], F32))
                pst = (pst0, pst1)
                wk_r = wk_in.rearrange("(dc p) f -> p dc f", p=128)
                for q in range(4):
                    g_.dma_start(wk_sb[:, q * 8:(q + 1) * 8, :],
                                 wk_r[:, q * 8:(q + 1) * 8, :]).then_inc(dWK.h, 16)
                    dWK.n += 16
                wk_ready = dWK.n

                x_tok = {}
                sq_done = {}
                ln_done = {}
                nmr_prev = {}
                tr_done = {}
                ev_done = {}
                xnst_tok = {}
                cs_tok = {}
                kmm_done, cp_done, rh_done, rcp_done = {}, {}, {}, {}
                kc_done, rsn_done, ga_done = {}, {}, {}
                kst_tok = {}

                def _emit_tr(Tj):
                    if (Tj, 0) in tr_done:
                        return
                    slj = Tj % 3
                    ttj = Tj % 4
                    gj = Tj // 4
                    _w(t_, sn, ln_done[Tj])
                    if Tj == 0:
                        _w(t_, dPW, perw_done)
                    if ttj == 0 and gj > 0:
                        _wt(t_, xnst_tok[gj - 1])
                    for half in range(2):
                        _w(t_, vn, ev_done.get((Tj - 1, half)))
                        ins2 = None
                        for j2 in range(16):
                            dc2 = half * 16 + j2
                            ins2 = t_.transpose(pst[half][:, j2 * 128:(j2 + 1) * 128],
                                                xnb[:, slj, dc2 * 128:(dc2 + 1) * 128],
                                                ident_b[:])
                        tn.inc(ins2)
                        tr_done[(Tj, half)] = tn.n
                        _w(v_, tn, tr_done[(Tj, half)])
                        vn.inc(v_.tensor_copy(
                            xnf.ap()[:, half * 16:(half + 1) * 16,
                                     ttj * 128:(ttj + 1) * 128],
                            pst[half].ap().rearrange("p (dc t) -> p dc t", t=128)))
                        ev_done[(Tj, half)] = vn.n

                for g in range(NG):
                    _w(g_, gn, ga_done.get((g - 2) * 8 + 7))
                    cse = dCSs[g % 2]
                    g_.dma_start(cos1[:, g % 2, :], cosq[:, g * CH:(g + 1) * CH])\
                        .then_inc(cse.h, 16)
                    g_.dma_start(sin1[:, g % 2, :], sinq[:, g * CH:(g + 1) * CH])\
                        .then_inc(cse.h, 16)
                    cse.n += 32
                    cs_tok[g] = (cse, cse.n)

                    for tt in range(4):
                        Ti = g * 4 + tt
                        sl = Ti % 3
                        o8 = sl * 8
                        _w(y_, sn, ln_done.get(Ti - 3))
                        xe = dXs[sl]
                        y_.dma_start(x_sb[:, sl, :],
                                     x_pre[Ti * 128:(Ti + 1) * 128, :]).then_inc(xe.h, 16)
                        xe.n += 16
                        x_tok[Ti] = (xe, xe.n)

                        _wt(s_, x_tok[Ti])
                        _w(s_, vn, nmr_prev.get(Ti - 3))
                        sn.inc(s_.activation(sqd[:], x_sb[:, sl, :], AF.Square,
                                             accum_out=stat[:, o8 + 0:o8 + 1]))
                        ssq_v = sn.n
                        _wt(v_, x_tok[Ti])
                        v_.reduce_sum(stat[:, o8 + 1:o8 + 2], x_sb[:, sl, :], axis=AX.X)
                        v_.drain()
                        v_.tensor_scalar_mul(stat[:, o8 + 2:o8 + 3],
                                             stat[:, o8 + 1:o8 + 2], R4096)
                        v_.drain()
                        v_.tensor_mul(stat[:, o8 + 3:o8 + 4], stat[:, o8 + 2:o8 + 3],
                                      stat[:, o8 + 2:o8 + 3])
                        _w(v_, sn, ssq_v)
                        v_.tensor_scalar(stat[:, o8 + 4:o8 + 5], stat[:, o8 + 0:o8 + 1],
                                         R4096, EPS, ALU.mult, ALU.add)
                        v_.drain()
                        vn.inc(v_.tensor_sub(stat[:, o8 + 5:o8 + 6],
                                             stat[:, o8 + 4:o8 + 5],
                                             stat[:, o8 + 3:o8 + 4]))
                        varf_v = vn.n
                        _w(s_, vn, varf_v)
                        sn.inc(s_.activation(stat[:, o8 + 6:o8 + 7],
                                             stat[:, o8 + 5:o8 + 6], AF.Sqrt))
                        sq_done[Ti] = sn.n
                        _w(v_, sn, sq_done[Ti])
                        v_.reciprocal(stat[:, o8 + 7:o8 + 8], stat[:, o8 + 6:o8 + 7])
                        v_.drain()
                        vn.inc(v_.tensor_scalar(stat[:, o8 + 2:o8 + 3],
                                                stat[:, o8 + 1:o8 + 2],
                                                -R4096, stat[:, o8 + 7:o8 + 8],
                                                ALU.mult, ALU.mult))
                        nmr_done = vn.n
                        nmr_prev[Ti] = vn.n
                        _w(s_, vn, nmr_done)
                        _w(s_, tn, tr_done.get((Ti - 3, 1)))
                        sn.inc(s_.activation(xnb[:, sl, :], x_sb[:, sl, :],
                                             AF.Identity,
                                             bias=stat[:, o8 + 2:o8 + 3],
                                             scale=stat[:, o8 + 7:o8 + 8]))
                        ln_done[Ti] = sn.n

                        if Ti >= 1:
                            _emit_tr(Ti - 1)
                    _emit_tr(g * 4 + 3)

                    _w(s_, vn, ev_done[(g * 4 + 3, 1)])
                    xne = dXNs[g % 2]
                    s_.dma_start(xn_s.ap()[:, :, g * CH:(g + 1) * CH]
                                 .rearrange("dc p t -> p dc t"), xnf[:]).then_inc(xne.h, 16)
                    xne.n += 16
                    xnst_tok[g] = (xne, xne.n)

                    def _rope_fin(rj):
                        gj, fj = rj // 8, rj % 8
                        fbj = rj % 2
                        _w(t_, sn, cp_done[rj])
                        _w(t_, sn, rcp_done.get(rj - 2))
                        tn.inc(t_.matmul(rps[fbj][:], pperm_b[:], kbf[:, fbj, :],
                                         start=True, stop=True))
                        rh_done[rj] = tn.n
                        _w(s_, tn, rh_done[rj])
                        _w(s_, gn, ga_done.get(rj - 2))
                        sn.inc(s_.activation(rbf[:, fbj, :], rps[fbj][:], AF.Copy))
                        rcp_done[rj] = sn.n
                        _w(g_, sn, rcp_done[rj])
                        _wt(g_, cs_tok[gj])
                        _w(g_, gn, ga_done.get(rj - 2))
                        g_.tensor_mul(kc[:, fbj, :], kbf[:, fbj, :],
                                      cos1[:, gj % 2, :])
                        g_.tensor_mul(rs[:, fbj, :], rbf[:, fbj, :],
                                      sin1[:, gj % 2, :])
                        g_.drain()
                        _wt(g_, kst_tok.get(rj - 2))
                        gn.inc(g_.tensor_add(krot[:, fbj, :], kc[:, fbj, :],
                                             rs[:, fbj, :]))
                        ga_done[rj] = gn.n
                        g_.drain()
                        ke = dKs[fbj]
                        g_.dma_start(k_s[fj, :, gj * CH:(gj + 1) * CH], krot[:, fbj, :])\
                            .then_inc(ke.h, 16)
                        ke.n += 16
                        kst_tok[rj] = (ke, ke.n)

                    for f in range(KV):
                        ri = g * 8 + f
                        fb = ri % 2
                        if ri == 0:
                            _w(t_, dWK, wk_ready)
                        if f == 0:
                            _w(t_, vn, ev_done[(g * 4 + 3, 1)])
                        _w(t_, sn, cp_done.get(ri - 2))
                        ins = None
                        for dc in range(DC):
                            ins = t_.matmul(kps[fb][:],
                                            wk_sb[:, dc, f * 128:(f + 1) * 128],
                                            xnf[:, dc, :],
                                            start=(dc == 0), stop=(dc == DC - 1))
                        tn.inc(ins)
                        kmm_done[ri] = tn.n
                        _w(s_, tn, kmm_done[ri])
                        _w(s_, tn, rh_done.get(ri - 2))
                        _w(s_, gn, ga_done.get(ri - 2))
                        sn.inc(s_.activation(kbf[:, fb, :], kps[fb][:], AF.Copy))
                        cp_done[ri] = sn.n
                        if ri >= 1:
                            _rope_fin(ri - 1)
                    _rope_fin(NG * 8 - 1) if g == NG - 1 else None
            quiesce()

        if 2 in phases:
            with ExitStack() as _st2:
                wv_sb = _st2.enter_context(nc.sbuf_tensor("wv_sb", [128, DC, KV * HD], BF16))
                xng = _st2.enter_context(nc.sbuf_tensor("xng", [128, 2, DC, CH], BF16))
                vbf = _st2.enter_context(nc.sbuf_tensor("vbf", [128, 2, KV * HD], BF16))
                vps = (_st2.enter_context(nc.psum_tensor("vps0", [128, 1024], F32)),
                       _st2.enter_context(nc.psum_tensor("vps1", [128, 1024], F32)))
                wv_r = wv_in.rearrange("(dc p) f -> p dc f", p=128)
                for q in range(4):
                    g_.dma_start(wv_sb[:, q * 8:(q + 1) * 8, :],
                                 wv_r[:, q * 8:(q + 1) * 8, :]).then_inc(dWV.h, 16)
                    dWV.n += 16
                wv_ready = dWV.n
                xg_tok = {}
                vmm_done = {}
                vcp_done = {}
                vst_tok = {}
                for g in range(NG):
                    _w(y_, tn, vmm_done.get((g - 2, 3)))
                    xre = dXRs[g % 2]
                    y_.dma_start(xng[:, g % 2, :, :],
                                 xn_s.ap()[:, :, g * CH:(g + 1) * CH]
                                 .rearrange("dc p t -> p dc t")).then_inc(xre.h, 16)
                    xre.n += 16
                    xg_tok[g] = (xre, xre.n)
                    for tt in range(4):
                        Ti = g * 4 + tt
                        sl = Ti % 2
                        if Ti == 0:
                            _w(t_, dWV, wv_ready)
                        _wt(t_, xg_tok[g])
                        _w(t_, sn, vcp_done.get((g, tt - 2),
                                                vcp_done.get((g - 1, tt + 2))))
                        ins = None
                        for dc in range(DC):
                            for half in range(2):
                                ins = t_.matmul(vps[sl][:, half * 512:(half + 1) * 512],
                                                xng[:, g % 2, dc, tt * 128:(tt + 1) * 128],
                                                wv_sb[:, dc, half * 512:(half + 1) * 512],
                                                start=(dc == 0), stop=(dc == DC - 1))
                        tn.inc(ins)
                        vmm_done[(g, tt)] = tn.n
                        _w(s_, tn, vmm_done[(g, tt)])
                        _wt(s_, vst_tok.get((g, tt - 2), vst_tok.get((g - 1, tt + 2))))
                        sn.inc(s_.activation(vbf[:, sl, :], vps[sl][:], AF.Copy))
                        vcp_done[(g, tt)] = sn.n
                        s_.drain()
                        ve = dVs[sl]
                        s_.dma_start(v_s.ap()[:, Ti, :, :].rearrange("kv p d -> p kv d"),
                                     vbf.ap()[:, sl, :].rearrange("p (kv d) -> p kv d", d=HD))\
                            .then_inc(ve.h, 16)
                        ve.n += 16
                        vst_tok[(g, tt)] = (ve, ve.n)
            quiesce()

        if 3 in phases:
            with ExitStack() as _st3:
                def _sb3(name, shape, dtype):
                    return _st3.enter_context(nc.sbuf_tensor(name, shape, dtype))
                xno = _sb3("xno", [128, DC, NOWN], BF16)
                gu_sb = _sb3("gu_sb", [128, 2, 2, DC, 128], BF16)
                silu_sb = _sb3("silu_sb", [128, 2, NOWN], BF16)
                hst = _sb3("hst", [128, 2, NOWN], BF16)
                wqstg = _sb3("wqstg", [128, 2, DC, 128], BF16)
                gps = (_st3.enter_context(nc.psum_tensor("gps0", [128, NOWN], F32)),
                       _st3.enter_context(nc.psum_tensor("gps1", [128, NOWN], F32)))
                ups = (_st3.enter_context(nc.psum_tensor("ups0", [128, NOWN], F32)),
                       _st3.enter_context(nc.psum_tensor("ups1", [128, NOWN], F32)))
                y_.dma_start(xno[:], xn_s.ap()[:, :, 0:NOWN]
                             .rearrange("dc p t -> p dc t")).then_inc(dONE.h, 16)
                dONE.n += 16
                xno_rd = dONE.n
                ga_r = gate_rl.rearrange("(dc p) f -> p dc f", p=128)
                up_r = up_rl.rearrange("(dc p) f -> p dc f", p=128)
                wq_r3 = wq_in.rearrange("(dc p) f -> p dc f", p=128)
                gumm = {}
                sil_done = {}
                hstt = {}
                h_tok = {}
                wqb_ld, wqb_st = {}, {}
                for c in range(FC):
                    cb = c % 2
                    _w(g_, tn, gumm.get(c - 2))
                    gue = dGUs[cb]
                    for gu_i, gu_r in ((0, ga_r), (1, up_r)):
                        for hdc in range(2):
                            g_.dma_start(gu_sb[:, gu_i, cb, hdc * 16:(hdc + 1) * 16, :],
                                         gu_r[:, hdc * 16:(hdc + 1) * 16,
                                              c * 128:(c + 1) * 128]).then_inc(gue.h, 16)
                    gue.n += 64
                    if c >= 2 and c % 2 == 0 and (c - 2) // 2 < H:
                        f = (c - 2) // 2
                        sq = f % 2
                        _wt(g_, wqb_st.get(f - 2))
                        qce = dQC[sq]
                        g_.dma_start(wqstg[:, sq, :, :],
                                     wq_r3[:, :, f * 128:(f + 1) * 128])\
                            .then_inc(qce.h, 16)
                        qce.n += 16
                        wqb_ld[f] = (qce, qce.n)
                        _wt(y_, wqb_ld[f])
                        qse = dQS[sq]
                        y_.dma_start(wq_b.ap()[f, :, :]
                                     .rearrange("p (dc u) -> p dc u", u=128),
                                     wqstg[:, sq, :, :]).then_inc(qse.h, 16)
                        qse.n += 16
                        wqb_st[f] = (qse, qse.n)
                    _w(t_, gue, gue.n)
                    if c == 0:
                        _w(t_, dONE, xno_rd)
                    _w(t_, sn, sil_done.get(c - 2))
                    _w(t_, vn, hstt.get(c - 2))
                    ins = None
                    for dc in range(DC):
                        st0, st1 = (dc == 0), (dc == DC - 1)
                        for gg in range(2):
                            t_.matmul(gps[cb][:, gg * CH:(gg + 1) * CH],
                                      gu_sb[:, 0, cb, dc, :],
                                      xno[:, dc, gg * CH:(gg + 1) * CH],
                                      start=st0, stop=st1)
                        for gg in range(2):
                            ins = t_.matmul(ups[cb][:, gg * CH:(gg + 1) * CH],
                                            gu_sb[:, 1, cb, dc, :],
                                            xno[:, dc, gg * CH:(gg + 1) * CH],
                                            start=st0, stop=st1)
                    tn.inc(ins)
                    gumm[c] = tn.n
                    _w(s_, tn, gumm[c])
                    _w(s_, vn, hstt.get(c - 2))
                    ins = None
                    for gg in range(2):
                        ins = s_.activation(silu_sb[:, cb, gg * CH:(gg + 1) * CH],
                                            gps[cb][:, gg * CH:(gg + 1) * CH],
                                            AF.Silu, scale=gs_sb[:, c:c + 1])
                    sn.inc(ins)
                    sil_done[c] = sn.n
                    _w(v_, sn, sil_done[c])
                    _wt(v_, h_tok.get(c - 2))
                    ins = None
                    for gg in range(2):
                        ins = v_.scalar_tensor_tensor(
                            hst[:, cb, gg * CH:(gg + 1) * CH],
                            ups[cb][:, gg * CH:(gg + 1) * CH],
                            us_sb[:, c:c + 1],
                            silu_sb[:, cb, gg * CH:(gg + 1) * CH],
                            ALU.mult, ALU.mult)
                    vn.inc(ins)
                    hstt[c] = vn.n
                    _w(y_, vn, hstt[c])
                    he = dHs[cb]
                    y_.dma_start(h_s[c, :, :], hst[:, cb, :]).then_inc(he.h, 16)
                    he.n += 16
                    h_tok[c] = (he, he.n)
                h_all = [(dHs[0], dHs[0].n), (dHs[1], dHs[1].n)]
            quiesce()

            with ExitStack() as _st3b:
                def _sb3b(name, shape, dtype):
                    return _st3b.enter_context(nc.sbuf_tensor(name, shape, dtype))
                h_g = _sb3b("h_g", [128, FC, CH], BF16)
                dn2 = _sb3b("dn2", [128, 2, 4, CH], BF16)
                dsb = _sb3b("dsb_t", [128, D], F32)
                out_sb = _sb3b("out_sb", [128, 4, CH], F32)
                wostg = _sb3b("wostg", [128, H, CH], BF16)
                wo_r3 = wo_in.rearrange("(hh p) d -> p hh d", p=128)
                wob_ld = wob_st = None
                dps8 = [_st3b.enter_context(nc.psum_tensor(f"dps{t}", [128, CH], F32))
                        for t in range(8)]
                y_.dma_start(dsb[:], dsb_in[:]).then_inc(dONE.h, 16)
                dONE.n += 16
                dsb_rd = dONE.n
                quads = []
                f0 = 0
                while f0 < FC:
                    quads.append((f0, min(4, FC - f0)))
                    f0 += 4
                NQ = len(quads)
                qmm = {}
                blk_done = {}
                ev_done = {}
                st_tok = {}
                ld_tok = {}
                hg_tok = None
                for g in range(2):
                    _w(y_, tn, blk_done.get((g - 1, 7)))
                    for tok_ in h_all:
                        _wt(y_, tok_)
                    hge = dHRs[g % 2]
                    for hh in range(2):
                        y_.dma_start(h_g[:, hh * 43:(hh + 1) * 43, :],
                                     h_s.ap()[hh * 43:(hh + 1) * 43, :,
                                              g * CH:(g + 1) * CH]
                                     .rearrange("fc p t -> p fc t")).then_inc(hge.h, 16)
                    hge.n += 32
                    hg_tok = (hge, hge.n)
                    for dblk in range(8):
                        bb = (dblk % 2) * 4
                        bi2 = g * 8 + dblk
                        dm2 = bi2 // 2
                        if bi2 % 2 == 0:
                            _wt(g_, wob_st)
                            g_.dma_start(wostg[:],
                                         wo_r3[:, :, dm2 * CH:(dm2 + 1) * CH])\
                                .then_inc(dOC.h, 16)
                            dOC.n += 16
                            wob_ld = (dOC, dOC.n)
                        else:
                            _wt(y_, wob_ld)
                            y_.dma_start(wo_b.ap()[dm2, :, :]
                                         .rearrange("p (hh u) -> p hh u", u=CH),
                                         wostg[:]).then_inc(dOS.h, 16)
                            dOS.n += 16
                            wob_st = (dOS, dOS.n)
                        for qi, (f0, qn) in enumerate(quads):
                            qq = (g * 8 + dblk) * NQ + qi
                            slot = qq % 2
                            _w(g_, tn, qmm.get(qq - 2))
                            dne = dDNs[slot]
                            g_.dma_start(dn2[:, slot, 0:qn, :],
                                         down_r8[f0:f0 + qn, :,
                                                 dblk * CH:(dblk + 1) * CH]
                                         .rearrange("f p d -> p f d")).then_inc(dne.h, 16)
                            dne.n += 16
                            ld_tok[qq] = (dne, dne.n)
                            _wt(t_, ld_tok[qq])
                            if qi == 0:
                                if dblk == 0:
                                    _wt(t_, hg_tok)
                                bi = g * 8 + dblk - 2
                                if bi >= 0:
                                    _w(t_, vn, ev_done.get((bi // 8, bi % 8, 3)))
                            ins = None
                            for q in range(qn):
                                fc = f0 + q
                                for tt in range(4):
                                    ins = t_.matmul(dps8[bb + tt][:],
                                                    h_g[:, fc, tt * 128:(tt + 1) * 128],
                                                    dn2[:, slot, q, :],
                                                    start=(fc == 0), stop=(fc == FC - 1))
                            tn.inc(ins)
                            qmm[qq] = tn.n
                        blk_done[(g, dblk)] = tn.n
                        for tt in range(4):
                            _w(v_, tn, blk_done[(g, dblk)])
                            if dblk == 0 and g == 0 and tt == 0:
                                _w(v_, dONE, dsb_rd)
                            _wt(v_, st_tok.get((g, dblk - 1, tt),
                                               st_tok.get((g - 1, 7, tt))))
                            vn.inc(v_.tensor_mul(out_sb[:, tt, :], dps8[bb + tt][:],
                                                 dsb[:, dblk * CH:(dblk + 1) * CH]))
                            ev_done[(g, dblk, tt)] = vn.n
                            _w(y_, vn, ev_done[(g, dblk, tt)])
                            oae = dOAs[(dblk * 4 + tt) % 2]
                            y_.dma_start(out_own[g * CH + tt * 128:
                                                 g * CH + (tt + 1) * 128,
                                                 dblk * CH:(dblk + 1) * CH],
                                         out_sb[:, tt, :]).then_inc(oae.h, 16)
                            oae.n += 16
                            st_tok[(g, dblk, tt)] = (oae, oae.n)
                p3_out = [(dOAs[0], dOAs[0].n), (dOAs[1], dOAs[1].n)]
            quiesce()
        else:
            p3_out = []

        q_sb = None
        if 4 in phases:
            _stq = ExitStack()
            q_sb = _stq.enter_context(nc.sbuf_tensor("q_sb", [128, H, NOWN], BF16))
            with ExitStack() as _st4:
                def _sb4(name, shape, dtype):
                    return _st4.enter_context(nc.sbuf_tensor(name, shape, dtype))
                xno2 = _sb4("xno2", [128, DC, NOWN], BF16)
                wq_sb = _sb4("wq_sb", [128, 3, DC, 128], BF16)
                qbf = _sb4("qbf", [128, 2, NOWN], BF16)
                qc = _sb4("qc", [128, 2, NOWN], F32)
                rs2 = _sb4("rs2", [128, 2, NOWN], F32)
                cs_own = _sb4("cs_own", [128, 2, NOWN], F32)
                qps = (_st4.enter_context(nc.psum_tensor("qps0", [128, NOWN], F32)),
                       _st4.enter_context(nc.psum_tensor("qps1", [128, NOWN], F32)))
                rqps = (_st4.enter_context(nc.psum_tensor("rqps0", [128, NOWN], F32)),
                        _st4.enter_context(nc.psum_tensor("rqps1", [128, NOWN], F32)))
                y_.dma_start(xno2[:], xn_s.ap()[:, :, 0:NOWN]
                             .rearrange("dc p t -> p dc t")).then_inc(dONE.h, 16)
                y_.dma_start(cs_own[:, 0, :], cosq[:, 0:NOWN]).then_inc(dONE.h, 16)
                y_.dma_start(cs_own[:, 1, :], sinq[:, 0:NOWN]).then_inc(dONE.h, 16)
                dONE.n += 48
                p4_rd = dONE.n
                qmm, qcp, qrh, qcd, qrs, qga = {}, {}, {}, {}, {}, {}

                def _qrope_fin(fj):
                    fbj = fj % 2
                    _w(t_, sn, qcp[fj])
                    _w(t_, vn, qrs.get(fj - 2))
                    ins2 = None
                    for gg in range(2):
                        ins2 = t_.matmul(rqps[fbj][:, gg * CH:(gg + 1) * CH], pperm_b[:],
                                         qbf[:, fbj, gg * CH:(gg + 1) * CH],
                                         start=True, stop=True)
                    tn.inc(ins2)
                    qrh[fj] = tn.n
                    _w(v_, sn, qcp[fj])
                    _w(v_, gn, qga.get(fj - 2))
                    vn.inc(v_.tensor_mul(qc[:, fbj, :], qps[fbj][:], cs_own[:, 0, :]))
                    qcd[fj] = vn.n
                    _w(v_, tn, qrh[fj])
                    vn.inc(v_.tensor_mul(rs2[:, fbj, :], rqps[fbj][:], cs_own[:, 1, :]))
                    qrs[fj] = vn.n
                    _w(g_, vn, qrs[fj])
                    gn.inc(g_.tensor_add(q_sb[:, fj, :], qc[:, fbj, :], rs2[:, fbj, :]))
                    qga[fj] = gn.n

                for f in range(H):
                    fb = f % 2
                    ws = f % 3
                    _w(y_, tn, qmm.get(f - 3))
                    wqe = dWQs[f % 2]
                    y_.dma_start(wq_sb[:, ws, :, :],
                                 wq_b.ap()[f, :, :]
                                 .rearrange("p (dc u) -> p dc u", u=128))\
                        .then_inc(wqe.h, 16)
                    wqe.n += 16
                    _w(t_, wqe, wqe.n)
                    if f == 0:
                        _w(t_, dONE, p4_rd)
                    _w(t_, vn, qcd.get(f - 2))
                    ins = None
                    for dc in range(DC):
                        for gg in range(2):
                            ins = t_.matmul(qps[fb][:, gg * CH:(gg + 1) * CH],
                                            wq_sb[:, ws, dc, :],
                                            xno2[:, dc, gg * CH:(gg + 1) * CH],
                                            start=(dc == 0), stop=(dc == DC - 1))
                    tn.inc(ins)
                    qmm[f] = tn.n
                    _w(s_, tn, qmm[f])
                    _w(s_, tn, qrh.get(f - 2))
                    sn.inc(s_.activation(qbf[:, fb, :], qps[fb][:], AF.Copy))
                    qcp[f] = sn.n
                    if f >= 1:
                        _qrope_fin(f - 1)
                _qrope_fin(H - 1)
                if "q_dbg" in dbg:
                    q_dbg = nc.dram_tensor("q_dbg", [H, 128, NOWN], BF16,
                                           kind="ExternalOutput")
                    _w(y_, gn, gn.n)
                    for f in range(H):
                        y_.dma_start(q_dbg[f, :, :], q_sb[:, f, :]).then_inc(dONE.h, 16)
                        dONE.n += 16
                    y_.wait_ge(dONE.h, dONE.n)
            quiesce()

        if 5 in phases:
            _sto = ExitStack()
            o_sb = _sto.enter_context(nc.sbuf_tensor("o_sb", [128, H, NOWN], BF16,
                                                     side="right"))
            with ExitStack() as _st5:
                def _sb5(name, shape, dtype):
                    return _st5.enter_context(nc.sbuf_tensor(name, shape, dtype))
                k_sb = _sb5("k_sb", [128, 2, NPRE], BF16)
                v_sb = _sb5("v_sb", [128, 2, NT, HD], BF16)
                pt = _sb5("pt", [128, 2, 1024], BF16)
                ptraw = _sb5("ptraw", [128, 2, 1024], BF16)
                rm = _sb5("rm", [128, 8], F32)
                mx_sb = _sb5("mx_sb", [1, 2], F32)
                mbias_all = _sb5("mbias_all", [128, 64], F32)
                rc2 = _sb5("rc2", [1, 2, CH], F32)
                bc_sb = _sb5("bc_sb", [128, 2, CH], F32)
                sst = (_st5.enter_context(nc.psum_tensor("sst0", [128, 1024], F32)),
                       _st5.enter_context(nc.psum_tensor("sst1", [128, 1024], F32)))

                with ExitStack() as _stp:
                    kdiag = _stp.enter_context(
                        nc.sbuf_tensor("kdiag", [128, KV, 1024], BF16))
                    mbps2 = _stp.enter_context(nc.psum_tensor("mbps2", [128, 2], F32))
                    for kv in range(KV):
                        y_.dma_start(kdiag[:, kv, :], k_s[kv, :, 0:1024])\
                            .then_inc(dONE.h, 16)
                        dONE.n += 16
                    kdiag_rd = dONE.n
                    red_done, gp_done, bcp_done, ts_done = {}, {}, {}, {}
                    for hg in range(64):
                        kv, j, g = hg // 8, (hg // 2) % 4, hg % 2
                        h = kv * 4 + j
                        if hg == 0:
                            _w(t_, dONE, kdiag_rd)
                            _w(t_, dPW, perw_done)
                            _w(t_, dP, per_done)
                        _w(t_, vn, red_done.get(hg - 1))
                        sc_u = []
                        for u in range(4):
                            buf, half = sst[u // 2], u % 2
                            tn.inc(t_.matmul(buf[:, half * CH:(half + 1) * CH],
                                             kdiag[:, kv, (g * 4 + u) * 128:
                                                   (g * 4 + u + 1) * 128],
                                             q_sb[:, h, g * CH:(g + 1) * CH],
                                             start=True, stop=True,
                                             skip_group_check=True))
                            sc_u.append(tn.n)
                        o4 = (hg % 2) * 4
                        ins = None
                        for u in range(4):
                            buf, half = sst[u // 2], u % 2
                            _w(v_, tn, sc_u[u])
                            if u == 0:
                                _w(v_, gn, gp_done.get(hg - 2))
                            ins = v_.reduce_max(rm[:, o4 + u:o4 + u + 1],
                                                buf[:, half * CH:(half + 1) * CH],
                                                axis=AX.X)
                        vn.inc(ins)
                        red_done[hg] = vn.n
                        _w(g_, vn, red_done[hg])
                        _w(g_, tn, bcp_done.get(hg - 2))
                        gn.inc(g_.tensor_reduce(mx_sb[0:1, hg % 2:hg % 2 + 1],
                                                rm[:, o4:o4 + 4],
                                                axis=AX.XYZWC, op=ALU.max))
                        gp_done[hg] = gn.n
                        _w(t_, gn, gp_done[hg])
                        _w(t_, vn, ts_done.get(hg - 2))
                        tn.inc(t_.matmul(mbps2[:, hg % 2:hg % 2 + 1], ones1_f[:],
                                         mx_sb[0:1, hg % 2:hg % 2 + 1],
                                         start=True, stop=True,
                                         skip_group_check=True))
                        bcp_done[hg] = tn.n
                        _w(v_, tn, bcp_done[hg])
                        vn.inc(v_.tensor_scalar_mul(mbias_all[:, hg:hg + 1],
                                                    mbps2[:, hg % 2:hg % 2 + 1],
                                                    -ISQ))
                        ts_done[hg] = vn.n
                    _stp.close()

                o_psd = (_st5.enter_context(nc.psum_tensor("o_ps0", [128, CH], F32)),
                         _st5.enter_context(nc.psum_tensor("o_ps1", [128, CH], F32)))
                smbc = (_st5.enter_context(nc.psum_tensor("smbc0", [128, CH], F32)),
                        _st5.enter_context(nc.psum_tensor("smbc1", [128, CH], F32)))
                prep_t, prep_v, prep_g = tn.n, vn.n, gn.n

                engines = [t_, v_, s_, g_, y_]
                idxr = []
                for eng in engines:
                    _w(eng, dP, per_done)
                    _w(eng, dPW, perw_done)
                    _w(eng, tn, prep_t)
                    _w(eng, vn, prep_v)
                    _w(eng, gn, prep_g)
                    idxr.append(eng.value_load(cid_sb[:]))

                sems = [tn, vn, sn, gn, dONE] + dKVs
                snap0 = [s.n for s in sems]
                finals = None
                for c0 in nc.Switch(engines=engines, index=idxr, n=8):
                    for s, n0 in zip(sems, snap0):
                        s.n = n0
                    exp_done, oav_done, sc_done, mask_done = {}, {}, {}, {}
                    kv_tok, av_last = {}, {}
                    oav_stop, norm_done = {}, {}
                    ptraw_prev = {0: None, 1: None}
                    gi = 0
                    hg = 0

                    def _norm_fin(hp):
                        kvp = hp // 8
                        hp_h = kvp * 4 + (hp // 2) % 4
                        hp_g = hp % 2
                        p = hp % 2
                        _w(v_, tn, oav_stop[hp])
                        v_.reciprocal(bc_sb[:, p, :], smbc[p][:])
                        v_.drain()
                        vn.inc(v_.tensor_mul(o_sb[:, hp_h, hp_g * CH:(hp_g + 1) * CH],
                                             o_psd[p][:], bc_sb[:, p, :]))
                        norm_done[hp] = vn.n

                    for kv in range(KV):
                        kb = kv % 2
                        _w(y_, tn, av_last.get(kv - 2))
                        kve = dKVs[kb]
                        y_.dma_start(k_sb[:, kb, :], k_s[kv, :, :]).then_inc(kve.h, 16)
                        y_.dma_start(v_sb[:, kb, :, :],
                                     v_s.ap()[kv, :, :, :].rearrange("kt p d -> p kt d"))\
                            .then_inc(kve.h, 16)
                        kve.n += 32
                        kv_tok[kv] = (kve, kve.n)
                        for j in range(4):
                            h = kv * 4 + j
                            for g in range(2):
                                if g == 0:
                                    kchunks = [0] + [2 + jj for jj in range(c0)]
                                else:
                                    kchunks = [1] + [2 + c0 + jj for jj in range(7 - c0)]
                                halves = [(kcx, hf) for kcx in kchunks for hf in range(2)]
                                nh = len(halves)
                                g0 = gi
                                pend = []
                                for idx, (kcx, hf) in enumerate(halves):
                                    sb2 = gi % 2
                                    if j == 0 and g == 0 and idx == 0:
                                        _wt(t_, kv_tok[kv])
                                    _w(t_, sn, exp_done.get(gi - 2))
                                    ins = None
                                    for u in range(2):
                                        ktile = kcx * 4 + hf * 2 + u
                                        ins = t_.matmul(
                                            sst[sb2][:, u * CH:(u + 1) * CH],
                                            k_sb[:, kb, ktile * 128:(ktile + 1) * 128],
                                            q_sb[:, h, g * CH:(g + 1) * CH],
                                            start=True, stop=True,
                                            skip_group_check=True)
                                    tn.inc(ins)
                                    sc_done[gi] = tn.n
                                    if idx >= 2:
                                        a = pend.pop(0)
                                        _emit_oav(t_, tn, vn, sn, exp_done, mask_done,
                                                  oav_done, onesm_b, v_sb, pt,
                                                  o_psd[hg % 2], smbc[hg % 2], kb,
                                                  a[0], a[1], a[2], a[3], g0, nh,
                                                  norm_done.get(hg - 2))
                                    _w(s_, tn, sc_done[gi])
                                    _w(s_, tn, oav_done.get(gi - 2))
                                    if idx <= 1:
                                        _w(s_, vn, ptraw_prev[idx])
                                        dst = ptraw[:, sb2, :]
                                    else:
                                        dst = pt[:, sb2, :]
                                    sn.inc(s_.activation(dst, sst[sb2][:], AF.Exp,
                                                         bias=mbias_all[:, hg:hg + 1],
                                                         scale=ISQ))
                                    exp_done[gi] = sn.n
                                    if idx <= 1:
                                        _w(v_, sn, exp_done[gi])
                                        vn.inc(v_.tensor_mul(
                                            pt[:, sb2, :], ptraw[:, sb2, :],
                                            maskt_b[:, idx * 1024:(idx + 1) * 1024]))
                                        mask_done[gi] = vn.n
                                        ptraw_prev[idx] = vn.n
                                    pend.append((gi, idx, kcx, hf))
                                    gi += 1
                                while pend:
                                    a = pend.pop(0)
                                    _emit_oav(t_, tn, vn, sn, exp_done, mask_done,
                                              oav_done, onesm_b, v_sb, pt,
                                              o_psd[hg % 2], smbc[hg % 2], kb,
                                              a[0], a[1], a[2], a[3], g0, nh,
                                              norm_done.get(hg - 2))
                                av_last[kv] = tn.n
                                oav_stop[hg] = oav_done[gi - 1]
                                if hg >= 1:
                                    _norm_fin(hg - 1)
                                hg += 1
                    _norm_fin(63)
                    if finals is None:
                        finals = [s.n for s in sems]
                    else:
                        assert finals == [s.n for s in sems], \
                            f"arm {c0} sem counts differ"
                for s, nf in zip(sems, finals):
                    s.n = nf
                if "o_dbg" in dbg:
                    o_dbg2 = nc.dram_tensor("o_dbg", [H, 128, NOWN], BF16,
                                            kind="ExternalOutput")
                    _w(y_, vn, vn.n)
                    for h in range(H):
                        y_.dma_start(o_dbg2[h, :, :], o_sb[:, h, :]).then_inc(dONE.h, 16)
                        dONE.n += 16
                    y_.wait_ge(dONE.h, dONE.n)
            quiesce()
        if q_sb is not None:
            _stq.close()

        if 6 in phases:
            with ExitStack() as _st6:
                def _sb6(name, shape, dtype):
                    return _st6.enter_context(nc.sbuf_tensor(name, shape, dtype))
                wo_sb = _sb6("wo_sb", [128, 2, H, CH], BF16)
                xres = _sb6("xres", [128, 4, CH], F32)
                res_sb = _sb6("res_sb", [128, 4, CH], F32)
                p6 = [_st6.enter_context(nc.psum_tensor(f"p6_{t}", [128, CH], F32))
                      for t in range(8)]
                p6mm = {}
                res_done = {}
                ac_tok = {}
                xr_tok = {}
                for dm in range(8):
                    db = dm % 2
                    _w(y_, tn, p6mm.get((dm - 2, 7)))
                    woe = dWOs[db]
                    y_.dma_start(wo_sb[:, db, :, :],
                                 wo_b.ap()[dm, :, :]
                                 .rearrange("p (hh u) -> p hh u", u=CH))\
                        .then_inc(woe.h, 16)
                    woe.n += 16
                    _w(t_, woe, woe.n)
                    for th in range(2):
                        hx = dm * 2 + th
                        _w(t_, vn, res_done.get((dm - 1, th * 4 + 3)))
                        for hh in range(H):
                            for t4 in range(4):
                                t = th * 4 + t4
                                ins = t_.matmul(p6[t][:],
                                                o_sb[:, hh, t * 128:(t + 1) * 128],
                                                wo_sb[:, db, hh, :],
                                                start=(hh == 0), stop=(hh == H - 1),
                                                skip_group_check=True)
                                if hh == H - 1:
                                    tn.inc(ins)
                                    p6mm[(dm, t)] = tn.n
                        for t4 in range(4):
                            t = th * 4 + t4
                            ti = dm * 8 + t
                            sl = ti % 4
                            prev = (dm, t - 4) if t >= 4 else ((dm - 1, t + 4) if dm else None)
                            _w(y_, vn, res_done.get(prev))
                            xe6 = dXR6[sl]
                            y_.dma_start(xres[:, sl, :],
                                         x_pre[t * 128:(t + 1) * 128,
                                               dm * CH:(dm + 1) * CH]).then_inc(xe6.h, 16)
                            xe6.n += 16
                            xr_tok[(dm, t)] = (xe6, xe6.n)
                            _w(v_, tn, p6mm[(dm, t)])
                            _wt(v_, xr_tok[(dm, t)])
                            _wt(v_, ac_tok.get(prev))
                            vn.inc(v_.scalar_tensor_tensor(res_sb[:, sl, :], p6[t][:],
                                                           1.0, xres[:, sl, :],
                                                           ALU.mult, ALU.add))
                            res_done[(dm, t)] = vn.n
                            _w(g_, vn, res_done[(dm, t)])
                            if dm == 0 and t == 0:
                                for tok_ in p3_out:
                                    _wt(g_, tok_)
                            ace = dACs[sl]
                            g_.dma_start(out_own[t * 128:(t + 1) * 128,
                                                 dm * CH:(dm + 1) * CH],
                                         res_sb[:, sl, :],
                                         accum_op=ALU.add).then_inc(ace.h, 16)
                            ace.n += 16
                            ac_tok[(dm, t)] = (ace, ace.n)
                for ace in dACs:
                    g_.wait_ge(ace.h, ace.n)
        if 5 in phases:
            _sto.close()

    return nc



def _rope_tables():
    THETA, FACTOR, HI_FF, LO_FF, ORIG_MAX = 500000.0, 8.0, 4.0, 1.0, 8192
    inv = 1.0 / THETA ** (np.arange(0, HD, 2, dtype=np.float64) / HD)
    wavelen = 2.0 * np.pi / inv
    low_wl = ORIG_MAX / LO_FF
    high_wl = ORIG_MAX / HI_FF
    smooth = (ORIG_MAX / wavelen - LO_FF) / (HI_FF - LO_FF)
    scaled = np.where(wavelen > low_wl, inv / FACTOR, inv)
    mid = (wavelen <= low_wl) & (wavelen >= high_wl)
    scaled = np.where(mid, (1 - smooth) * inv / FACTOR + smooth * inv, scaled)
    inv_f = scaled.astype(np.float32)
    pos = np.arange(T, dtype=np.float32)
    freqs = pos[:, None] * inv_f[None, :]
    emb = np.concatenate([freqs, freqs], axis=-1)
    return np.cos(emb).astype(np.float32), np.sin(emb).astype(np.float32)


def _host_inputs(L0_attn_x, L0_attn_wq, L0_attn_wk, L0_attn_wv, L0_attn_wo,
                 L0_ffn_gate_w, L0_ffn_gate_s, L0_ffn_up_w, L0_ffn_up_s,
                 L0_ffn_down_w, L0_ffn_down_s):
    x = np.asarray(L0_attn_x, dtype=np.float32)
    cos_t, sin_t = _rope_tables()
    gate_rl_ = np.ascontiguousarray(np.asarray(L0_ffn_gate_w, dtype=np.int8).T)
    up_rl_ = np.ascontiguousarray(np.asarray(L0_ffn_up_w, dtype=np.int8).T)
    down_r8_ = np.ascontiguousarray(
        np.asarray(L0_ffn_down_w, dtype=np.int8).T.reshape(FC, 128, D))
    gs = np.ascontiguousarray(np.asarray(L0_ffn_gate_s, np.float32).reshape(FC, 128).T)
    us = np.ascontiguousarray(np.asarray(L0_ffn_up_s, np.float32).reshape(FC, 128).T)
    dsb_ = np.ascontiguousarray(np.broadcast_to(
        np.asarray(L0_ffn_down_s, np.float32)[None, :], (128, D)))
    mt = np.zeros((128, 2048), dtype=np.float32)
    for j in range(4):
        kt = 128 * j + np.arange(128)[:, None]
        q = np.arange(512)[None, :]
        mt[:, j * 512:(j + 1) * 512] = (kt <= q).astype(np.float32)
    ident = np.eye(128, dtype=np.float32)
    pperm = np.zeros((128, 128), dtype=np.float32)
    for m in range(64):
        pperm[m + 64, m] = -1.0
    for m in range(64, 128):
        pperm[m - 64, m] = 1.0
    shared = dict(wk=np.ascontiguousarray(L0_attn_wk, np.float32),
                  wv=np.ascontiguousarray(L0_attn_wv, np.float32),
                  wq=np.ascontiguousarray(L0_attn_wq, np.float32),
                  wo=np.ascontiguousarray(L0_attn_wo, np.float32),
                  gate_rl=gate_rl_, up_rl=up_rl_, down_r8=down_r8_,
                  gs=gs, us=us, dsb=dsb_, maskt=mt, identf=ident, pperm=pperm,
                  ones1=np.ones((1, 128), dtype=np.float32),
                  onesc=np.ones((128, 1), dtype=np.float32),
                  onesm=np.ones((128, 128), dtype=np.float32))
    in_maps = []
    for c in range(NC_):
        c0, c1 = c, 7 - c
        pos = np.concatenate([
            np.arange(c0 * CH, (c0 + 1) * CH),
            np.arange(c1 * CH, (c1 + 1) * CH),
            np.arange(0, c0 * CH),
            np.arange(0, c1 * CH),
        ])
        ba = np.concatenate([np.zeros(CH, np.int64), np.ones(CH, np.int64),
                             np.zeros(c0 * CH, np.int64), np.ones(c1 * CH, np.int64)])
        m = dict(shared)
        m.update(x_pre=np.ascontiguousarray(x[ba, pos, :]),
                 cosq=np.ascontiguousarray(cos_t[pos].T),
                 sinq=np.ascontiguousarray(sin_t[pos].T),
                 cid=np.array([[c]], dtype=np.int32))
        in_maps.append(m)
    return in_maps


_NC_CACHE = {}


def kernel(**inputs):
    in_maps = _host_inputs(**inputs)
    if "full" not in _NC_CACHE:
        _NC_CACHE["full"] = build()
    res = run_bass_kernel_spmd(_NC_CACHE["full"], in_maps, core_ids=list(range(NC_)))
    out = np.empty((B, T, D), dtype=np.float32)
    for c in range(NC_):
        oo = res.results[c]["out_own"]
        out[0, c * CH:(c + 1) * CH, :] = oo[0:CH]
        out[1, (7 - c) * CH:(8 - c) * CH, :] = oo[CH:NOWN]
    return out


# version 10
# speedup vs baseline: 1.2420x; 1.0292x over previous
import math
from contextlib import ExitStack
import numpy as np
import concourse.bass as bass
import concourse.mybir as mybir
from concourse.bass_utils import run_bass_kernel_spmd

F32 = mybir.dt.float32
BF16 = mybir.dt.bfloat16
I32 = mybir.dt.int32
AF = mybir.ActivationFunctionType
ALU = mybir.AluOpType
AX = mybir.AxisListType

B, T, D, H, KV, HD, DFF = 2, 4096, 4096, 32, 8, 128, 11008
NC_ = 8
CH = 512
NG = 9
NPRE = NG * CH
NT = NG * 4
NOWN = 1024
DC = D // 128
FC = DFF // 128
EPS = 1e-5
ISQ = 1.0 / math.sqrt(HD)
R4096 = 1.0 / 4096.0


class Sem:
    def __init__(self, nc, name):
        self.h = nc.alloc_semaphore(name)
        self.n = 0

    def inc(self, instr, k=1):
        instr.then_inc(self.h, k)
        self.n += k
        return self.n


def _w(eng, sem, val):
    if val is not None and val > 0:
        eng.wait_ge(sem.h, val)


def _wt(eng, token):
    if token is not None:
        sem, v = token
        _w(eng, sem, v)


def _emit_oav(t_, tn, vn, sn, exp_done, mask_done, oav_done, onesm_b, v_sb, pt,
              o_ps, sums_ps, kb, gi, idx, kc, hf, g0, nh, ott_prev):
    _w(t_, sn, exp_done[gi])
    if idx <= 1:
        _w(t_, vn, mask_done[gi])
    if idx == 0:
        _w(t_, vn, ott_prev)
    first = (idx == 0)
    last = (idx == nh - 1)
    ins = None
    for u in range(2):
        ktile = kc * 4 + hf * 2 + u
        st = first and (u == 0)
        sp = last and (u == 1)
        t_.matmul(sums_ps[:], onesm_b[:],
                  pt[:, gi % 2, u * CH:(u + 1) * CH],
                  start=st, stop=sp, skip_group_check=True)
        ins = t_.matmul(o_ps[:], v_sb[:, kb, ktile, :],
                        pt[:, gi % 2, u * CH:(u + 1) * CH],
                        start=st, stop=sp, skip_group_check=True)
    tn.inc(ins)
    oav_done[gi] = tn.n


def build(phases=(1, 2, 3, 4, 5, 6), dbg=()):
    nc = bass.Bass("TRN2", target_bir_lowering=False)
    t_, v_, s_, g_, y_ = nc.tensor, nc.vector, nc.scalar, nc.gpsimd, nc.sync

    def scr(name, shape, dtype):
        k = "ExternalOutput" if name in dbg else "Internal"
        return nc.dram_tensor(name, shape, dtype, kind=k)

    x_pre = nc.dram_tensor("x_pre", [NPRE, D], F32, kind="ExternalInput")
    cosq = nc.dram_tensor("cosq", [HD, NPRE], F32, kind="ExternalInput")
    sinq = nc.dram_tensor("sinq", [HD, NPRE], F32, kind="ExternalInput")
    wk_in = nc.dram_tensor("wk", [D, KV * HD], F32, kind="ExternalInput")
    wv_in = nc.dram_tensor("wv", [D, KV * HD], F32, kind="ExternalInput")
    wq_in = nc.dram_tensor("wq", [D, H * HD], F32, kind="ExternalInput")
    wo_in = nc.dram_tensor("wo", [H * HD, D], F32, kind="ExternalInput")
    gate_rl = nc.dram_tensor("gate_rl", [D, DFF], mybir.dt.int8, kind="ExternalInput")
    up_rl = nc.dram_tensor("up_rl", [D, DFF], mybir.dt.int8, kind="ExternalInput")
    down_r8 = nc.dram_tensor("down_r8", [FC, 128, D], mybir.dt.int8, kind="ExternalInput")
    gs_in = nc.dram_tensor("gs", [128, FC], F32, kind="ExternalInput")
    us_in = nc.dram_tensor("us", [128, FC], F32, kind="ExternalInput")
    dsb_in = nc.dram_tensor("dsb", [128, D], F32, kind="ExternalInput")
    maskt_in = nc.dram_tensor("maskt", [128, 2048], F32, kind="ExternalInput")
    ident_in = nc.dram_tensor("identf", [128, 128], F32, kind="ExternalInput")
    pperm_in = nc.dram_tensor("pperm", [128, 128], F32, kind="ExternalInput")
    ones1_in = nc.dram_tensor("ones1", [1, 128], F32, kind="ExternalInput")
    onesc_in = nc.dram_tensor("onesc", [128, 1], F32, kind="ExternalInput")
    onesm_in = nc.dram_tensor("onesm", [128, 128], F32, kind="ExternalInput")
    cid_in = nc.dram_tensor("cid", [1, 1], I32, kind="ExternalInput")
    out_own = nc.dram_tensor("out_own", [NOWN, D], F32, kind="ExternalOutput")

    xn_s = scr("xn_s", [DC, 128, NPRE], BF16)
    k_s = scr("k_s", [KV, HD, NPRE], BF16)
    v_s = scr("v_s", [KV, NT, 128, HD], BF16)
    h_s = scr("h_s", [FC, 128, NOWN], BF16)
    wq_b = scr("wq_b", [H, 128, DC * 128], BF16)
    wo_b = scr("wo_b", [8, 128, H * CH], BF16)

    tn = Sem(nc, "tn")
    vn = Sem(nc, "vn")
    sn = Sem(nc, "sn")
    gn = Sem(nc, "gn")
    dP = Sem(nc, "dP")
    dPW = Sem(nc, "dPW")
    dWK = Sem(nc, "dWK")
    dWV = Sem(nc, "dWV")
    dXs = [Sem(nc, "dX0"), Sem(nc, "dX1"), Sem(nc, "dX2")]
    dCSs = [Sem(nc, "dCS0"), Sem(nc, "dCS1")]
    dXNs = [Sem(nc, "dXN0"), Sem(nc, "dXN1")]
    dKs = [Sem(nc, "dK0"), Sem(nc, "dK1")]
    dVs = [Sem(nc, "dV0"), Sem(nc, "dV1")]
    dXRs = [Sem(nc, "dXR0"), Sem(nc, "dXR1")]
    dGUs = [Sem(nc, "dGU0"), Sem(nc, "dGU1")]
    dHs = [Sem(nc, "dH0"), Sem(nc, "dH1")]
    dHRs = [Sem(nc, "dHR0"), Sem(nc, "dHR1")]
    dDNs = [Sem(nc, "dDN0"), Sem(nc, "dDN1"),
            Sem(nc, "dDN2"), Sem(nc, "dDN3")]
    dOAs = [Sem(nc, "dOA0"), Sem(nc, "dOA1")]
    dWQs = [Sem(nc, "dWQ0"), Sem(nc, "dWQ1")]
    dONE = Sem(nc, "dONE")
    dKVs = [Sem(nc, "dKV0"), Sem(nc, "dKV1")]
    dWOs = [Sem(nc, "dWO0"), Sem(nc, "dWO1")]
    dXR6 = [Sem(nc, "dXR6a"), Sem(nc, "dXR6b"), Sem(nc, "dXR6c"), Sem(nc, "dXR6d")]
    dACs = [Sem(nc, "dAC0"), Sem(nc, "dAC1"), Sem(nc, "dAC2"), Sem(nc, "dAC3")]
    dQC = [Sem(nc, "dQC0"), Sem(nc, "dQC1")]
    dQS = [Sem(nc, "dQS0"), Sem(nc, "dQS1")]
    dOC = Sem(nc, "dOC")
    dOS = Sem(nc, "dOS")

    all_dma_sems = ([tn, vn, sn, gn, dP, dPW, dWK, dWV, dONE] + dXs + dCSs + dXNs
                    + dKs + dVs + dXRs + dGUs + dHs + dHRs + dDNs + dOAs + dWQs
                    + dKVs + dWOs + dXR6 + dACs + dQC + dQS + [dOC, dOS])

    def quiesce():
        for sem in all_dma_sems:
            if sem.n > 0:
                y_.wait_ge(sem.h, sem.n)
                g_.wait_ge(sem.h, sem.n)
        nc.all_engine_barrier()

    with ExitStack() as _st0:
        def _sb(name, shape, dtype):
            return _st0.enter_context(nc.sbuf_tensor(name, shape, dtype))
        ident_b = _sb("ident_b", [128, 128], BF16)
        ident_f = _sb("ident_f", [128, 128], F32)
        pperm_b = _sb("pperm_b", [128, 128], BF16)
        ones1_f = _sb("ones1_f", [1, 128], F32)
        onesc_b = _sb("onesc_b", [128, 1], BF16)
        onesm_b = _sb("onesm_b", [128, 128], BF16)
        maskt_b = _sb("maskt_b", [128, 2048], BF16)
        gs_sb = _sb("gs_sb", [128, FC], F32)
        us_sb = _sb("us_sb", [128, FC], F32)
        cid_sb = _sb("cid_sb", [1, 1], I32)
        stat = _sb("stat", [128, 24], F32)
        for ap, src in ((ident_f, ident_in), (ones1_f, ones1_in), (gs_sb, gs_in),
                        (us_sb, us_in), (cid_sb, cid_in)):
            dP.n = (y_.dma_start(ap[:], src[:]).then_inc(dP.h, 16), dP.n + 16)[1]
        for ap, src in ((ident_b, ident_in), (pperm_b, pperm_in),
                        (onesc_b, onesc_in), (maskt_b, maskt_in),
                        (onesm_b, onesm_in)):
            dPW.n = (g_.dma_start(ap[:], src[:]).then_inc(dPW.h, 16), dPW.n + 16)[1]
        per_done = dP.n
        perw_done = dPW.n

        if 1 in phases:
            with ExitStack() as _st1:
                def _sb1(name, shape, dtype):
                    return _st1.enter_context(nc.sbuf_tensor(name, shape, dtype))
                def _ps1(name, shape, dtype):
                    return _st1.enter_context(nc.psum_tensor(name, shape, dtype))
                wk_sb = _sb1("wk_sb", [128, DC, KV * HD], BF16)
                x_sb = _sb1("x_sb", [128, 3, D], F32)
                xnb = _sb1("xnb", [128, 3, D], BF16)
                xnf = _sb1("xnf", [128, DC, CH], BF16)
                sqd = _sb1("sqd", [128, D], BF16)
                cos1 = _sb1("cos1", [128, 2, CH], F32)
                sin1 = _sb1("sin1", [128, 2, CH], F32)
                kbf = _sb1("kbf", [128, 2, CH], BF16)
                kc = _sb1("kc", [128, 2, CH], F32)
                rs = _sb1("rs", [128, 2, CH], F32)
                krot = _sb1("krot", [128, 2, CH], BF16)
                rbf = _sb1("rbf", [128, 2, CH], BF16)
                pst0 = _ps1("pst0", [128, 2048], BF16)
                pst1 = _ps1("pst1", [128, 2048], BF16)
                kps = (_ps1("kps0", [128, CH], F32), _ps1("kps1", [128, CH], F32))
                rps = (_ps1("rps0", [128, CH], F32), _ps1("rps1", [128, CH], F32))
                pst = (pst0, pst1)
                wk_r = wk_in.rearrange("(dc p) f -> p dc f", p=128)
                for q in range(4):
                    g_.dma_start(wk_sb[:, q * 8:(q + 1) * 8, :],
                                 wk_r[:, q * 8:(q + 1) * 8, :]).then_inc(dWK.h, 16)
                    dWK.n += 16
                wk_ready = dWK.n

                x_tok = {}
                sq_done = {}
                ln_done = {}
                nmr_prev = {}
                tr_done = {}
                ev_done = {}
                xnst_tok = {}
                cs_tok = {}
                kmm_done, cp_done, rh_done, rcp_done = {}, {}, {}, {}
                kc_done, rsn_done, ga_done = {}, {}, {}
                kst_tok = {}

                def _emit_tr(Tj):
                    if (Tj, 0) in tr_done:
                        return
                    slj = Tj % 3
                    ttj = Tj % 4
                    gj = Tj // 4
                    _w(t_, sn, ln_done[Tj])
                    if Tj == 0:
                        _w(t_, dPW, perw_done)
                    if ttj == 0 and gj > 0:
                        _wt(t_, xnst_tok[gj - 1])
                    for half in range(2):
                        _w(t_, vn, ev_done.get((Tj - 1, half)))
                        ins2 = None
                        for j2 in range(16):
                            dc2 = half * 16 + j2
                            ins2 = t_.transpose(pst[half][:, j2 * 128:(j2 + 1) * 128],
                                                xnb[:, slj, dc2 * 128:(dc2 + 1) * 128],
                                                ident_b[:])
                        tn.inc(ins2)
                        tr_done[(Tj, half)] = tn.n
                        _w(v_, tn, tr_done[(Tj, half)])
                        vn.inc(v_.tensor_copy(
                            xnf.ap()[:, half * 16:(half + 1) * 16,
                                     ttj * 128:(ttj + 1) * 128],
                            pst[half].ap().rearrange("p (dc t) -> p dc t", t=128)))
                        ev_done[(Tj, half)] = vn.n

                for g in range(NG):
                    _w(g_, gn, ga_done.get((g - 2) * 8 + 7))
                    cse = dCSs[g % 2]
                    g_.dma_start(cos1[:, g % 2, :], cosq[:, g * CH:(g + 1) * CH])\
                        .then_inc(cse.h, 16)
                    g_.dma_start(sin1[:, g % 2, :], sinq[:, g * CH:(g + 1) * CH])\
                        .then_inc(cse.h, 16)
                    cse.n += 32
                    cs_tok[g] = (cse, cse.n)

                    for tt in range(4):
                        Ti = g * 4 + tt
                        sl = Ti % 3
                        o8 = sl * 8
                        _w(y_, sn, ln_done.get(Ti - 3))
                        xe = dXs[sl]
                        for hp in range(2):
                            y_.dma_start(x_sb[hp * 64:(hp + 1) * 64, sl, :],
                                         x_pre[Ti * 128 + hp * 64:
                                               Ti * 128 + (hp + 1) * 64, :])\
                                .then_inc(xe.h, 16)
                        xe.n += 32
                        x_tok[Ti] = (xe, xe.n)

                        _wt(s_, x_tok[Ti])
                        _w(s_, vn, nmr_prev.get(Ti - 3))
                        sn.inc(s_.activation(sqd[:], x_sb[:, sl, :], AF.Square,
                                             accum_out=stat[:, o8 + 0:o8 + 1]))
                        ssq_v = sn.n
                        _wt(v_, x_tok[Ti])
                        v_.reduce_sum(stat[:, o8 + 1:o8 + 2], x_sb[:, sl, :], axis=AX.X)
                        v_.drain()
                        v_.tensor_scalar_mul(stat[:, o8 + 2:o8 + 3],
                                             stat[:, o8 + 1:o8 + 2], R4096)
                        v_.drain()
                        v_.tensor_mul(stat[:, o8 + 3:o8 + 4], stat[:, o8 + 2:o8 + 3],
                                      stat[:, o8 + 2:o8 + 3])
                        _w(v_, sn, ssq_v)
                        v_.tensor_scalar(stat[:, o8 + 4:o8 + 5], stat[:, o8 + 0:o8 + 1],
                                         R4096, EPS, ALU.mult, ALU.add)
                        v_.drain()
                        vn.inc(v_.tensor_sub(stat[:, o8 + 5:o8 + 6],
                                             stat[:, o8 + 4:o8 + 5],
                                             stat[:, o8 + 3:o8 + 4]))
                        varf_v = vn.n
                        _w(s_, vn, varf_v)
                        sn.inc(s_.activation(stat[:, o8 + 6:o8 + 7],
                                             stat[:, o8 + 5:o8 + 6], AF.Sqrt))
                        sq_done[Ti] = sn.n
                        _w(v_, sn, sq_done[Ti])
                        v_.reciprocal(stat[:, o8 + 7:o8 + 8], stat[:, o8 + 6:o8 + 7])
                        v_.drain()
                        vn.inc(v_.tensor_scalar(stat[:, o8 + 2:o8 + 3],
                                                stat[:, o8 + 1:o8 + 2],
                                                -R4096, stat[:, o8 + 7:o8 + 8],
                                                ALU.mult, ALU.mult))
                        nmr_done = vn.n
                        nmr_prev[Ti] = vn.n
                        _w(s_, vn, nmr_done)
                        _w(s_, tn, tr_done.get((Ti - 3, 1)))
                        sn.inc(s_.activation(xnb[:, sl, :], x_sb[:, sl, :],
                                             AF.Identity,
                                             bias=stat[:, o8 + 2:o8 + 3],
                                             scale=stat[:, o8 + 7:o8 + 8]))
                        ln_done[Ti] = sn.n

                        if Ti >= 1:
                            _emit_tr(Ti - 1)
                    _emit_tr(g * 4 + 3)

                    _w(s_, vn, ev_done[(g * 4 + 3, 1)])
                    xne = dXNs[g % 2]
                    for hp in range(2):
                        s_.dma_start(xn_s.ap()[:, hp * 64:(hp + 1) * 64,
                                               g * CH:(g + 1) * CH]
                                     .rearrange("dc p t -> p dc t"),
                                     xnf[hp * 64:(hp + 1) * 64, :, :])\
                            .then_inc(xne.h, 16)
                    xne.n += 32
                    xnst_tok[g] = (xne, xne.n)

                    def _rope_fin(rj):
                        gj, fj = rj // 8, rj % 8
                        fbj = rj % 2
                        _w(t_, sn, cp_done[rj])
                        _w(t_, sn, rcp_done.get(rj - 2))
                        tn.inc(t_.matmul(rps[fbj][:], pperm_b[:], kbf[:, fbj, :],
                                         start=True, stop=True))
                        rh_done[rj] = tn.n
                        _w(s_, tn, rh_done[rj])
                        _w(s_, gn, ga_done.get(rj - 2))
                        sn.inc(s_.activation(rbf[:, fbj, :], rps[fbj][:], AF.Copy))
                        rcp_done[rj] = sn.n
                        _w(g_, sn, rcp_done[rj])
                        _wt(g_, cs_tok[gj])
                        _w(g_, gn, ga_done.get(rj - 2))
                        g_.tensor_mul(kc[:, fbj, :], kbf[:, fbj, :],
                                      cos1[:, gj % 2, :])
                        g_.tensor_mul(rs[:, fbj, :], rbf[:, fbj, :],
                                      sin1[:, gj % 2, :])
                        g_.drain()
                        _wt(g_, kst_tok.get(rj - 2))
                        gn.inc(g_.tensor_add(krot[:, fbj, :], kc[:, fbj, :],
                                             rs[:, fbj, :]))
                        ga_done[rj] = gn.n
                        g_.drain()
                        ke = dKs[fbj]
                        g_.dma_start(k_s[fj, :, gj * CH:(gj + 1) * CH], krot[:, fbj, :])\
                            .then_inc(ke.h, 16)
                        ke.n += 16
                        kst_tok[rj] = (ke, ke.n)

                    for f in range(KV):
                        ri = g * 8 + f
                        fb = ri % 2
                        if ri == 0:
                            _w(t_, dWK, wk_ready)
                        if f == 0:
                            _w(t_, vn, ev_done[(g * 4 + 3, 1)])
                        _w(t_, sn, cp_done.get(ri - 2))
                        ins = None
                        for dc in range(DC):
                            ins = t_.matmul(kps[fb][:],
                                            wk_sb[:, dc, f * 128:(f + 1) * 128],
                                            xnf[:, dc, :],
                                            start=(dc == 0), stop=(dc == DC - 1))
                        tn.inc(ins)
                        kmm_done[ri] = tn.n
                        _w(s_, tn, kmm_done[ri])
                        _w(s_, tn, rh_done.get(ri - 2))
                        _w(s_, gn, ga_done.get(ri - 2))
                        sn.inc(s_.activation(kbf[:, fb, :], kps[fb][:], AF.Copy))
                        cp_done[ri] = sn.n
                        if ri >= 1:
                            _rope_fin(ri - 1)
                    _rope_fin(NG * 8 - 1) if g == NG - 1 else None
            quiesce()

        if 2 in phases:
            with ExitStack() as _st2:
                wv_sb = _st2.enter_context(nc.sbuf_tensor("wv_sb", [128, DC, KV * HD], BF16))
                xng = _st2.enter_context(nc.sbuf_tensor("xng", [128, 2, DC, CH], BF16))
                vbf = _st2.enter_context(nc.sbuf_tensor("vbf", [128, 2, KV * HD], BF16))
                vps = (_st2.enter_context(nc.psum_tensor("vps0", [128, 1024], F32)),
                       _st2.enter_context(nc.psum_tensor("vps1", [128, 1024], F32)))
                wv_r = wv_in.rearrange("(dc p) f -> p dc f", p=128)
                for q in range(4):
                    g_.dma_start(wv_sb[:, q * 8:(q + 1) * 8, :],
                                 wv_r[:, q * 8:(q + 1) * 8, :]).then_inc(dWV.h, 16)
                    dWV.n += 16
                wv_ready = dWV.n
                xg_tok = {}
                vmm_done = {}
                vcp_done = {}
                vst_tok = {}
                for g in range(NG):
                    _w(y_, tn, vmm_done.get((g - 2, 3)))
                    xre = dXRs[g % 2]
                    y_.dma_start(xng[:, g % 2, :, :],
                                 xn_s.ap()[:, :, g * CH:(g + 1) * CH]
                                 .rearrange("dc p t -> p dc t")).then_inc(xre.h, 16)
                    xre.n += 16
                    xg_tok[g] = (xre, xre.n)
                    for tt in range(4):
                        Ti = g * 4 + tt
                        sl = Ti % 2
                        if Ti == 0:
                            _w(t_, dWV, wv_ready)
                        _wt(t_, xg_tok[g])
                        _w(t_, sn, vcp_done.get((g, tt - 2),
                                                vcp_done.get((g - 1, tt + 2))))
                        ins = None
                        for dc in range(DC):
                            for half in range(2):
                                ins = t_.matmul(vps[sl][:, half * 512:(half + 1) * 512],
                                                xng[:, g % 2, dc, tt * 128:(tt + 1) * 128],
                                                wv_sb[:, dc, half * 512:(half + 1) * 512],
                                                start=(dc == 0), stop=(dc == DC - 1))
                        tn.inc(ins)
                        vmm_done[(g, tt)] = tn.n
                        _w(s_, tn, vmm_done[(g, tt)])
                        _wt(s_, vst_tok.get((g, tt - 2), vst_tok.get((g - 1, tt + 2))))
                        sn.inc(s_.activation(vbf[:, sl, :], vps[sl][:], AF.Copy))
                        vcp_done[(g, tt)] = sn.n
                        s_.drain()
                        ve = dVs[sl]
                        s_.dma_start(v_s.ap()[:, Ti, :, :].rearrange("kv p d -> p kv d"),
                                     vbf.ap()[:, sl, :].rearrange("p (kv d) -> p kv d", d=HD))\
                            .then_inc(ve.h, 16)
                        ve.n += 16
                        vst_tok[(g, tt)] = (ve, ve.n)
            quiesce()

        if 3 in phases:
            with ExitStack() as _st3:
                def _sb3(name, shape, dtype):
                    return _st3.enter_context(nc.sbuf_tensor(name, shape, dtype))
                xno = _sb3("xno", [128, DC, NOWN], BF16)
                gu_sb = _sb3("gu_sb", [128, 2, 2, DC, 128], BF16)
                silu_sb = _sb3("silu_sb", [128, 2, NOWN], BF16)
                hst = _sb3("hst", [128, 2, NOWN], BF16)
                wqstg = _sb3("wqstg", [128, 2, DC, 128], BF16)
                gps = (_st3.enter_context(nc.psum_tensor("gps0", [128, NOWN], F32)),
                       _st3.enter_context(nc.psum_tensor("gps1", [128, NOWN], F32)))
                ups = (_st3.enter_context(nc.psum_tensor("ups0", [128, NOWN], F32)),
                       _st3.enter_context(nc.psum_tensor("ups1", [128, NOWN], F32)))
                y_.dma_start(xno[:], xn_s.ap()[:, :, 0:NOWN]
                             .rearrange("dc p t -> p dc t")).then_inc(dONE.h, 16)
                dONE.n += 16
                xno_rd = dONE.n
                ga_r = gate_rl.rearrange("(dc p) f -> p dc f", p=128)
                up_r = up_rl.rearrange("(dc p) f -> p dc f", p=128)
                wq_r3 = wq_in.rearrange("(dc p) f -> p dc f", p=128)
                gumm = {}
                sil_done = {}
                hstt = {}
                h_tok = {}
                wqb_ld, wqb_st = {}, {}
                for c in range(FC):
                    cb = c % 2
                    _w(g_, tn, gumm.get(c - 2))
                    gue = dGUs[cb]
                    for gu_i, gu_r in ((0, ga_r), (1, up_r)):
                        for hdc in range(2):
                            g_.dma_start(gu_sb[:, gu_i, cb, hdc * 16:(hdc + 1) * 16, :],
                                         gu_r[:, hdc * 16:(hdc + 1) * 16,
                                              c * 128:(c + 1) * 128]).then_inc(gue.h, 16)
                    gue.n += 64
                    if c >= 2 and c % 2 == 0 and (c - 2) // 2 < H:
                        f = (c - 2) // 2
                        sq = f % 2
                        _wt(g_, wqb_st.get(f - 2))
                        qce = dQC[sq]
                        g_.dma_start(wqstg[:, sq, :, :],
                                     wq_r3[:, :, f * 128:(f + 1) * 128])\
                            .then_inc(qce.h, 16)
                        qce.n += 16
                        wqb_ld[f] = (qce, qce.n)
                        _wt(y_, wqb_ld[f])
                        qse = dQS[sq]
                        y_.dma_start(wq_b.ap()[f, :, :]
                                     .rearrange("p (dc u) -> p dc u", u=128),
                                     wqstg[:, sq, :, :]).then_inc(qse.h, 16)
                        qse.n += 16
                        wqb_st[f] = (qse, qse.n)
                    _w(t_, gue, gue.n)
                    if c == 0:
                        _w(t_, dONE, xno_rd)
                    _w(t_, sn, sil_done.get(c - 2))
                    _w(t_, vn, hstt.get(c - 2))
                    ins = None
                    for dc in range(DC):
                        st0, st1 = (dc == 0), (dc == DC - 1)
                        for gg in range(2):
                            t_.matmul(gps[cb][:, gg * CH:(gg + 1) * CH],
                                      gu_sb[:, 0, cb, dc, :],
                                      xno[:, dc, gg * CH:(gg + 1) * CH],
                                      start=st0, stop=st1)
                        for gg in range(2):
                            ins = t_.matmul(ups[cb][:, gg * CH:(gg + 1) * CH],
                                            gu_sb[:, 1, cb, dc, :],
                                            xno[:, dc, gg * CH:(gg + 1) * CH],
                                            start=st0, stop=st1)
                    tn.inc(ins)
                    gumm[c] = tn.n
                    _w(s_, tn, gumm[c])
                    _w(s_, vn, hstt.get(c - 2))
                    ins = None
                    for gg in range(2):
                        ins = s_.activation(silu_sb[:, cb, gg * CH:(gg + 1) * CH],
                                            gps[cb][:, gg * CH:(gg + 1) * CH],
                                            AF.Silu, scale=gs_sb[:, c:c + 1])
                    sn.inc(ins)
                    sil_done[c] = sn.n
                    _w(v_, sn, sil_done[c])
                    _wt(v_, h_tok.get(c - 2))
                    ins = None
                    for gg in range(2):
                        ins = v_.scalar_tensor_tensor(
                            hst[:, cb, gg * CH:(gg + 1) * CH],
                            ups[cb][:, gg * CH:(gg + 1) * CH],
                            us_sb[:, c:c + 1],
                            silu_sb[:, cb, gg * CH:(gg + 1) * CH],
                            ALU.mult, ALU.mult)
                    vn.inc(ins)
                    hstt[c] = vn.n
                    _w(y_, vn, hstt[c])
                    he = dHs[cb]
                    y_.dma_start(h_s[c, :, :], hst[:, cb, :]).then_inc(he.h, 16)
                    he.n += 16
                    h_tok[c] = (he, he.n)
                h_all = [(dHs[0], dHs[0].n), (dHs[1], dHs[1].n)]
            quiesce()

            with ExitStack() as _st3b:
                def _sb3b(name, shape, dtype):
                    return _st3b.enter_context(nc.sbuf_tensor(name, shape, dtype))
                h_g = _sb3b("h_g", [128, FC, CH], BF16)
                dn2 = _sb3b("dn2", [128, 4, 4, CH], BF16)
                dsb = _sb3b("dsb_t", [128, D], F32)
                out_sb = _sb3b("out_sb", [128, 4, CH], F32)
                wostg = _sb3b("wostg", [128, H, CH], BF16)
                wo_r3 = wo_in.rearrange("(hh p) d -> p hh d", p=128)
                wob_ld = wob_st = None
                dps8 = [_st3b.enter_context(nc.psum_tensor(f"dps{t}", [128, CH], F32))
                        for t in range(8)]
                y_.dma_start(dsb[:], dsb_in[:]).then_inc(dONE.h, 16)
                dONE.n += 16
                dsb_rd = dONE.n
                quads = []
                f0 = 0
                while f0 < FC:
                    quads.append((f0, min(4, FC - f0)))
                    f0 += 4
                NQ = len(quads)
                qmm = {}
                blk_done = {}
                ev_done = {}
                st_tok = {}
                ld_tok = {}
                hg_tok = None
                for g in range(2):
                    _w(y_, tn, blk_done.get((g - 1, 7)))
                    for tok_ in h_all:
                        _wt(y_, tok_)
                    hge = dHRs[g % 2]
                    fcq = [(0, 21), (21, 43), (43, 64), (64, 86)]
                    for a, b in fcq:
                        y_.dma_start(h_g[:, a:b, :],
                                     h_s.ap()[a:b, :, g * CH:(g + 1) * CH]
                                     .rearrange("fc p t -> p fc t")).then_inc(hge.h, 16)
                    hge.n += 64
                    hg_tok = (hge, hge.n)
                    for dblk in range(8):
                        bb = (dblk % 2) * 4
                        bi2 = g * 8 + dblk
                        dm2 = bi2 // 2
                        for qi, (f0, qn) in enumerate(quads):
                            if qi in (8, 15) and bi2 % 2 == 0:
                                _wt(g_, wob_st)
                                h16 = 0 if qi == 8 else 1
                                g_.dma_start(wostg[:, h16 * 16:(h16 + 1) * 16, :],
                                             wo_r3[:, h16 * 16:(h16 + 1) * 16,
                                                   dm2 * CH:(dm2 + 1) * CH])\
                                    .then_inc(dOC.h, 16)
                                dOC.n += 16
                                wob_ld = (dOC, dOC.n)
                            if qi == 8 and bi2 % 2 == 1:
                                _wt(y_, wob_ld)
                                y_.dma_start(wo_b.ap()[dm2, :, :]
                                             .rearrange("p (hh u) -> p hh u", u=CH),
                                             wostg[:]).then_inc(dOS.h, 16)
                                dOS.n += 16
                                wob_st = (dOS, dOS.n)
                            qq = (g * 8 + dblk) * NQ + qi
                            slot = qq % 4
                            _w(g_, tn, qmm.get(qq - 4))
                            dne = dDNs[slot]
                            g_.dma_start(dn2[:, slot, 0:qn, :],
                                         down_r8[f0:f0 + qn, :,
                                                 dblk * CH:(dblk + 1) * CH]
                                         .rearrange("f p d -> p f d")).then_inc(dne.h, 16)
                            dne.n += 16
                            ld_tok[qq] = (dne, dne.n)
                            _wt(t_, ld_tok[qq])
                            if qi == 0:
                                if dblk == 0:
                                    _wt(t_, hg_tok)
                                bi = g * 8 + dblk - 2
                                if bi >= 0:
                                    _w(t_, vn, ev_done.get((bi // 8, bi % 8, 3)))
                            ins = None
                            for q in range(qn):
                                fc = f0 + q
                                for tt in range(4):
                                    ins = t_.matmul(dps8[bb + tt][:],
                                                    h_g[:, fc, tt * 128:(tt + 1) * 128],
                                                    dn2[:, slot, q, :],
                                                    start=(fc == 0), stop=(fc == FC - 1))
                            tn.inc(ins)
                            qmm[qq] = tn.n
                        blk_done[(g, dblk)] = tn.n
                        for tt in range(4):
                            _w(v_, tn, blk_done[(g, dblk)])
                            if dblk == 0 and g == 0 and tt == 0:
                                _w(v_, dONE, dsb_rd)
                            _wt(v_, st_tok.get((g, dblk - 1, tt),
                                               st_tok.get((g - 1, 7, tt))))
                            vn.inc(v_.tensor_mul(out_sb[:, tt, :], dps8[bb + tt][:],
                                                 dsb[:, dblk * CH:(dblk + 1) * CH]))
                            ev_done[(g, dblk, tt)] = vn.n
                            _w(y_, vn, ev_done[(g, dblk, tt)])
                            oae = dOAs[(dblk * 4 + tt) % 2]
                            y_.dma_start(out_own[g * CH + tt * 128:
                                                 g * CH + (tt + 1) * 128,
                                                 dblk * CH:(dblk + 1) * CH],
                                         out_sb[:, tt, :]).then_inc(oae.h, 16)
                            oae.n += 16
                            st_tok[(g, dblk, tt)] = (oae, oae.n)
                p3_out = [(dOAs[0], dOAs[0].n), (dOAs[1], dOAs[1].n)]
            quiesce()
        else:
            p3_out = []

        q_sb = None
        if 4 in phases:
            _stq = ExitStack()
            q_sb = _stq.enter_context(nc.sbuf_tensor("q_sb", [128, H, NOWN], BF16))
            with ExitStack() as _st4:
                def _sb4(name, shape, dtype):
                    return _st4.enter_context(nc.sbuf_tensor(name, shape, dtype))
                xno2 = _sb4("xno2", [128, DC, NOWN], BF16)
                wq_sb = _sb4("wq_sb", [128, 3, DC, 128], BF16)
                qbf = _sb4("qbf", [128, 2, NOWN], BF16)
                qc = _sb4("qc", [128, 2, NOWN], F32)
                rs2 = _sb4("rs2", [128, 2, NOWN], F32)
                cs_own = _sb4("cs_own", [128, 2, NOWN], F32)
                qps = (_st4.enter_context(nc.psum_tensor("qps0", [128, NOWN], F32)),
                       _st4.enter_context(nc.psum_tensor("qps1", [128, NOWN], F32)))
                rqps = (_st4.enter_context(nc.psum_tensor("rqps0", [128, NOWN], F32)),
                        _st4.enter_context(nc.psum_tensor("rqps1", [128, NOWN], F32)))
                for hp in range(4):
                    y_.dma_start(xno2[hp * 32:(hp + 1) * 32, :, :],
                                 xn_s.ap()[:, hp * 32:(hp + 1) * 32, 0:NOWN]
                                 .rearrange("dc p t -> p dc t")).then_inc(dONE.h, 16)
                g_.dma_start(cs_own[:, 0, :], cosq[:, 0:NOWN]).then_inc(dONE.h, 16)
                g_.dma_start(cs_own[:, 1, :], sinq[:, 0:NOWN]).then_inc(dONE.h, 16)
                dONE.n += 96
                p4_rd = dONE.n
                qmm, qcp, qrh, qcd, qrs, qga = {}, {}, {}, {}, {}, {}

                def _qrope_fin(fj):
                    fbj = fj % 2
                    _w(t_, sn, qcp[fj])
                    _w(t_, vn, qrs.get(fj - 2))
                    ins2 = None
                    for gg in range(2):
                        ins2 = t_.matmul(rqps[fbj][:, gg * CH:(gg + 1) * CH], pperm_b[:],
                                         qbf[:, fbj, gg * CH:(gg + 1) * CH],
                                         start=True, stop=True)
                    tn.inc(ins2)
                    qrh[fj] = tn.n
                    _w(v_, sn, qcp[fj])
                    _w(v_, gn, qga.get(fj - 2))
                    vn.inc(v_.tensor_mul(qc[:, fbj, :], qps[fbj][:], cs_own[:, 0, :]))
                    qcd[fj] = vn.n
                    _w(v_, tn, qrh[fj])
                    vn.inc(v_.tensor_mul(rs2[:, fbj, :], rqps[fbj][:], cs_own[:, 1, :]))
                    qrs[fj] = vn.n
                    _w(g_, vn, qrs[fj])
                    gn.inc(g_.tensor_add(q_sb[:, fj, :], qc[:, fbj, :], rs2[:, fbj, :]))
                    qga[fj] = gn.n

                for f in range(H):
                    fb = f % 2
                    ws = f % 3
                    _w(y_, tn, qmm.get(f - 3))
                    wqe = dWQs[f % 2]
                    for hp in range(2):
                        y_.dma_start(wq_sb[hp * 64:(hp + 1) * 64, ws, :, :],
                                     wq_b.ap()[f, hp * 64:(hp + 1) * 64, :]
                                     .rearrange("p (dc u) -> p dc u", u=128))\
                            .then_inc(wqe.h, 16)
                    wqe.n += 32
                    _w(t_, wqe, wqe.n)
                    if f == 0:
                        _w(t_, dONE, p4_rd)
                    _w(t_, vn, qcd.get(f - 2))
                    ins = None
                    for dc in range(DC):
                        for gg in range(2):
                            ins = t_.matmul(qps[fb][:, gg * CH:(gg + 1) * CH],
                                            wq_sb[:, ws, dc, :],
                                            xno2[:, dc, gg * CH:(gg + 1) * CH],
                                            start=(dc == 0), stop=(dc == DC - 1))
                    tn.inc(ins)
                    qmm[f] = tn.n
                    _w(s_, tn, qmm[f])
                    _w(s_, tn, qrh.get(f - 2))
                    sn.inc(s_.activation(qbf[:, fb, :], qps[fb][:], AF.Copy))
                    qcp[f] = sn.n
                    if f >= 1:
                        _qrope_fin(f - 1)
                _qrope_fin(H - 1)
                if "q_dbg" in dbg:
                    q_dbg = nc.dram_tensor("q_dbg", [H, 128, NOWN], BF16,
                                           kind="ExternalOutput")
                    _w(y_, gn, gn.n)
                    for f in range(H):
                        y_.dma_start(q_dbg[f, :, :], q_sb[:, f, :]).then_inc(dONE.h, 16)
                        dONE.n += 16
                    y_.wait_ge(dONE.h, dONE.n)
            quiesce()

        if 5 in phases:
            _sto = ExitStack()
            o_sb = _sto.enter_context(nc.sbuf_tensor("o_sb", [128, H, NOWN], BF16,
                                                     side="right"))
            with ExitStack() as _st5:
                def _sb5(name, shape, dtype):
                    return _st5.enter_context(nc.sbuf_tensor(name, shape, dtype))
                k_sb = _sb5("k_sb", [128, 2, NPRE], BF16)
                v_sb = _sb5("v_sb", [128, 2, NT, HD], BF16)
                pt = _sb5("pt", [128, 2, 1024], BF16)
                ptraw = _sb5("ptraw", [128, 2, 1024], BF16)
                rm = _sb5("rm", [128, 8], F32)
                mx_sb = _sb5("mx_sb", [1, 2], F32)
                mbias_all = _sb5("mbias_all", [128, 64], F32)
                rc2 = _sb5("rc2", [1, 2, CH], F32)
                bc_sb = _sb5("bc_sb", [128, 2, CH], F32)
                sst = (_st5.enter_context(nc.psum_tensor("sst0", [128, 1024], F32)),
                       _st5.enter_context(nc.psum_tensor("sst1", [128, 1024], F32)))

                with ExitStack() as _stp:
                    kdiag = _stp.enter_context(
                        nc.sbuf_tensor("kdiag", [128, KV, 1024], BF16))
                    mbps2 = _stp.enter_context(nc.psum_tensor("mbps2", [128, 2], F32))
                    for kv in range(KV):
                        y_.dma_start(kdiag[:, kv, :], k_s[kv, :, 0:1024])\
                            .then_inc(dONE.h, 16)
                        dONE.n += 16
                    kdiag_rd = dONE.n
                    red_done, gp_done, bcp_done, ts_done = {}, {}, {}, {}
                    for hg in range(64):
                        kv, j, g = hg // 8, (hg // 2) % 4, hg % 2
                        h = kv * 4 + j
                        if hg == 0:
                            _w(t_, dONE, kdiag_rd)
                            _w(t_, dPW, perw_done)
                            _w(t_, dP, per_done)
                        _w(t_, vn, red_done.get(hg - 1))
                        sc_u = []
                        for u in range(4):
                            buf, half = sst[u // 2], u % 2
                            tn.inc(t_.matmul(buf[:, half * CH:(half + 1) * CH],
                                             kdiag[:, kv, (g * 4 + u) * 128:
                                                   (g * 4 + u + 1) * 128],
                                             q_sb[:, h, g * CH:(g + 1) * CH],
                                             start=True, stop=True,
                                             skip_group_check=True))
                            sc_u.append(tn.n)
                        o4 = (hg % 2) * 4
                        ins = None
                        for u in range(4):
                            buf, half = sst[u // 2], u % 2
                            _w(v_, tn, sc_u[u])
                            if u == 0:
                                _w(v_, gn, gp_done.get(hg - 2))
                            ins = v_.reduce_max(rm[:, o4 + u:o4 + u + 1],
                                                buf[:, half * CH:(half + 1) * CH],
                                                axis=AX.X)
                        vn.inc(ins)
                        red_done[hg] = vn.n
                        _w(g_, vn, red_done[hg])
                        _w(g_, tn, bcp_done.get(hg - 2))
                        gn.inc(g_.tensor_reduce(mx_sb[0:1, hg % 2:hg % 2 + 1],
                                                rm[:, o4:o4 + 4],
                                                axis=AX.XYZWC, op=ALU.max))
                        gp_done[hg] = gn.n
                        _w(t_, gn, gp_done[hg])
                        _w(t_, vn, ts_done.get(hg - 2))
                        tn.inc(t_.matmul(mbps2[:, hg % 2:hg % 2 + 1], ones1_f[:],
                                         mx_sb[0:1, hg % 2:hg % 2 + 1],
                                         start=True, stop=True,
                                         skip_group_check=True))
                        bcp_done[hg] = tn.n
                        _w(v_, tn, bcp_done[hg])
                        vn.inc(v_.tensor_scalar_mul(mbias_all[:, hg:hg + 1],
                                                    mbps2[:, hg % 2:hg % 2 + 1],
                                                    -ISQ))
                        ts_done[hg] = vn.n
                    _stp.close()

                o_psd = (_st5.enter_context(nc.psum_tensor("o_ps0", [128, CH], F32)),
                         _st5.enter_context(nc.psum_tensor("o_ps1", [128, CH], F32)))
                smbc = (_st5.enter_context(nc.psum_tensor("smbc0", [128, CH], F32)),
                        _st5.enter_context(nc.psum_tensor("smbc1", [128, CH], F32)))
                prep_t, prep_v, prep_g = tn.n, vn.n, gn.n

                engines = [t_, v_, s_, g_, y_]
                idxr = []
                for eng in engines:
                    _w(eng, dP, per_done)
                    _w(eng, dPW, perw_done)
                    _w(eng, tn, prep_t)
                    _w(eng, vn, prep_v)
                    _w(eng, gn, prep_g)
                    idxr.append(eng.value_load(cid_sb[:]))

                sems = [tn, vn, sn, gn, dONE] + dKVs
                snap0 = [s.n for s in sems]
                finals = None
                for c0 in nc.Switch(engines=engines, index=idxr, n=8):
                    for s, n0 in zip(sems, snap0):
                        s.n = n0
                    exp_done, oav_done, sc_done, mask_done = {}, {}, {}, {}
                    kv_tok, av_last = {}, {}
                    oav_stop, norm_done = {}, {}
                    ptraw_prev = {0: None, 1: None}
                    gi = 0
                    hg = 0

                    def _norm_fin(hp):
                        kvp = hp // 8
                        hp_h = kvp * 4 + (hp // 2) % 4
                        hp_g = hp % 2
                        p = hp % 2
                        _w(v_, tn, oav_stop[hp])
                        v_.reciprocal(bc_sb[:, p, :], smbc[p][:])
                        v_.drain()
                        vn.inc(v_.tensor_mul(o_sb[:, hp_h, hp_g * CH:(hp_g + 1) * CH],
                                             o_psd[p][:], bc_sb[:, p, :]))
                        norm_done[hp] = vn.n

                    for kv in range(KV):
                        kb = kv % 2
                        _w(y_, tn, av_last.get(kv - 2))
                        kve = dKVs[kb]
                        y_.dma_start(k_sb[:, kb, :], k_s[kv, :, :]).then_inc(kve.h, 16)
                        y_.dma_start(v_sb[:, kb, :, :],
                                     v_s.ap()[kv, :, :, :].rearrange("kt p d -> p kt d"))\
                            .then_inc(kve.h, 16)
                        kve.n += 32
                        kv_tok[kv] = (kve, kve.n)
                        for j in range(4):
                            h = kv * 4 + j
                            for g in range(2):
                                if g == 0:
                                    kchunks = [0] + [2 + jj for jj in range(c0)]
                                else:
                                    kchunks = [1] + [2 + c0 + jj for jj in range(7 - c0)]
                                halves = [(kcx, hf) for kcx in kchunks for hf in range(2)]
                                nh = len(halves)
                                g0 = gi
                                pend = []
                                for idx, (kcx, hf) in enumerate(halves):
                                    sb2 = gi % 2
                                    if j == 0 and g == 0 and idx == 0:
                                        _wt(t_, kv_tok[kv])
                                    _w(t_, sn, exp_done.get(gi - 2))
                                    ins = None
                                    for u in range(2):
                                        ktile = kcx * 4 + hf * 2 + u
                                        ins = t_.matmul(
                                            sst[sb2][:, u * CH:(u + 1) * CH],
                                            k_sb[:, kb, ktile * 128:(ktile + 1) * 128],
                                            q_sb[:, h, g * CH:(g + 1) * CH],
                                            start=True, stop=True,
                                            skip_group_check=True)
                                    tn.inc(ins)
                                    sc_done[gi] = tn.n
                                    if idx >= 2:
                                        a = pend.pop(0)
                                        _emit_oav(t_, tn, vn, sn, exp_done, mask_done,
                                                  oav_done, onesm_b, v_sb, pt,
                                                  o_psd[hg % 2], smbc[hg % 2], kb,
                                                  a[0], a[1], a[2], a[3], g0, nh,
                                                  norm_done.get(hg - 2))
                                    _w(s_, tn, sc_done[gi])
                                    _w(s_, tn, oav_done.get(gi - 2))
                                    if idx <= 1:
                                        _w(s_, vn, ptraw_prev[idx])
                                        dst = ptraw[:, sb2, :]
                                    else:
                                        dst = pt[:, sb2, :]
                                    sn.inc(s_.activation(dst, sst[sb2][:], AF.Exp,
                                                         bias=mbias_all[:, hg:hg + 1],
                                                         scale=ISQ))
                                    exp_done[gi] = sn.n
                                    if idx <= 1:
                                        _w(v_, sn, exp_done[gi])
                                        vn.inc(v_.tensor_mul(
                                            pt[:, sb2, :], ptraw[:, sb2, :],
                                            maskt_b[:, idx * 1024:(idx + 1) * 1024]))
                                        mask_done[gi] = vn.n
                                        ptraw_prev[idx] = vn.n
                                    pend.append((gi, idx, kcx, hf))
                                    gi += 1
                                while pend:
                                    a = pend.pop(0)
                                    _emit_oav(t_, tn, vn, sn, exp_done, mask_done,
                                              oav_done, onesm_b, v_sb, pt,
                                              o_psd[hg % 2], smbc[hg % 2], kb,
                                              a[0], a[1], a[2], a[3], g0, nh,
                                              norm_done.get(hg - 2))
                                av_last[kv] = tn.n
                                oav_stop[hg] = oav_done[gi - 1]
                                if hg >= 1:
                                    _norm_fin(hg - 1)
                                hg += 1
                    _norm_fin(63)
                    if finals is None:
                        finals = [s.n for s in sems]
                    else:
                        assert finals == [s.n for s in sems], \
                            f"arm {c0} sem counts differ"
                for s, nf in zip(sems, finals):
                    s.n = nf
                if "o_dbg" in dbg:
                    o_dbg2 = nc.dram_tensor("o_dbg", [H, 128, NOWN], BF16,
                                            kind="ExternalOutput")
                    _w(y_, vn, vn.n)
                    for h in range(H):
                        y_.dma_start(o_dbg2[h, :, :], o_sb[:, h, :]).then_inc(dONE.h, 16)
                        dONE.n += 16
                    y_.wait_ge(dONE.h, dONE.n)
            quiesce()
        if q_sb is not None:
            _stq.close()

        if 6 in phases:
            with ExitStack() as _st6:
                def _sb6(name, shape, dtype):
                    return _st6.enter_context(nc.sbuf_tensor(name, shape, dtype))
                wo_sb = _sb6("wo_sb", [128, 2, H, CH], BF16)
                xres = _sb6("xres", [128, 4, CH], F32)
                res_sb = _sb6("res_sb", [128, 4, CH], F32)
                p6 = [_st6.enter_context(nc.psum_tensor(f"p6_{t}", [128, CH], F32))
                      for t in range(8)]
                p6mm = {}
                res_done = {}
                ac_tok = {}
                xr_tok = {}
                for dm in range(8):
                    db = dm % 2
                    _w(y_, tn, p6mm.get((dm - 2, 7)))
                    woe = dWOs[db]
                    y_.dma_start(wo_sb[:, db, :, :],
                                 wo_b.ap()[dm, :, :]
                                 .rearrange("p (hh u) -> p hh u", u=CH))\
                        .then_inc(woe.h, 16)
                    woe.n += 16
                    _w(t_, woe, woe.n)
                    for th in range(2):
                        hx = dm * 2 + th
                        _w(t_, vn, res_done.get((dm - 1, th * 4 + 3)))
                        for hh in range(H):
                            for t4 in range(4):
                                t = th * 4 + t4
                                ins = t_.matmul(p6[t][:],
                                                o_sb[:, hh, t * 128:(t + 1) * 128],
                                                wo_sb[:, db, hh, :],
                                                start=(hh == 0), stop=(hh == H - 1),
                                                skip_group_check=True)
                                if hh == H - 1:
                                    tn.inc(ins)
                                    p6mm[(dm, t)] = tn.n
                        for t4 in range(4):
                            t = th * 4 + t4
                            ti = dm * 8 + t
                            sl = ti % 4
                            prev = (dm, t - 4) if t >= 4 else ((dm - 1, t + 4) if dm else None)
                            _w(y_, vn, res_done.get(prev))
                            xe6 = dXR6[sl]
                            y_.dma_start(xres[:, sl, :],
                                         x_pre[t * 128:(t + 1) * 128,
                                               dm * CH:(dm + 1) * CH]).then_inc(xe6.h, 16)
                            xe6.n += 16
                            xr_tok[(dm, t)] = (xe6, xe6.n)
                            _w(v_, tn, p6mm[(dm, t)])
                            _wt(v_, xr_tok[(dm, t)])
                            _wt(v_, ac_tok.get(prev))
                            vn.inc(v_.scalar_tensor_tensor(res_sb[:, sl, :], p6[t][:],
                                                           1.0, xres[:, sl, :],
                                                           ALU.mult, ALU.add))
                            res_done[(dm, t)] = vn.n
                            _w(g_, vn, res_done[(dm, t)])
                            if dm == 0 and t == 0:
                                for tok_ in p3_out:
                                    _wt(g_, tok_)
                            ace = dACs[sl]
                            g_.dma_start(out_own[t * 128:(t + 1) * 128,
                                                 dm * CH:(dm + 1) * CH],
                                         res_sb[:, sl, :],
                                         accum_op=ALU.add).then_inc(ace.h, 16)
                            ace.n += 16
                            ac_tok[(dm, t)] = (ace, ace.n)
                for ace in dACs:
                    g_.wait_ge(ace.h, ace.n)
        if 5 in phases:
            _sto.close()

    return nc



def _rope_tables():
    THETA, FACTOR, HI_FF, LO_FF, ORIG_MAX = 500000.0, 8.0, 4.0, 1.0, 8192
    inv = 1.0 / THETA ** (np.arange(0, HD, 2, dtype=np.float64) / HD)
    wavelen = 2.0 * np.pi / inv
    low_wl = ORIG_MAX / LO_FF
    high_wl = ORIG_MAX / HI_FF
    smooth = (ORIG_MAX / wavelen - LO_FF) / (HI_FF - LO_FF)
    scaled = np.where(wavelen > low_wl, inv / FACTOR, inv)
    mid = (wavelen <= low_wl) & (wavelen >= high_wl)
    scaled = np.where(mid, (1 - smooth) * inv / FACTOR + smooth * inv, scaled)
    inv_f = scaled.astype(np.float32)
    pos = np.arange(T, dtype=np.float32)
    freqs = pos[:, None] * inv_f[None, :]
    emb = np.concatenate([freqs, freqs], axis=-1)
    return np.cos(emb).astype(np.float32), np.sin(emb).astype(np.float32)


def _host_inputs(L0_attn_x, L0_attn_wq, L0_attn_wk, L0_attn_wv, L0_attn_wo,
                 L0_ffn_gate_w, L0_ffn_gate_s, L0_ffn_up_w, L0_ffn_up_s,
                 L0_ffn_down_w, L0_ffn_down_s):
    x = np.asarray(L0_attn_x, dtype=np.float32)
    cos_t, sin_t = _rope_tables()
    gate_rl_ = np.ascontiguousarray(np.asarray(L0_ffn_gate_w, dtype=np.int8).T)
    up_rl_ = np.ascontiguousarray(np.asarray(L0_ffn_up_w, dtype=np.int8).T)
    down_r8_ = np.ascontiguousarray(
        np.asarray(L0_ffn_down_w, dtype=np.int8).T.reshape(FC, 128, D))
    gs = np.ascontiguousarray(np.asarray(L0_ffn_gate_s, np.float32).reshape(FC, 128).T)
    us = np.ascontiguousarray(np.asarray(L0_ffn_up_s, np.float32).reshape(FC, 128).T)
    dsb_ = np.ascontiguousarray(np.broadcast_to(
        np.asarray(L0_ffn_down_s, np.float32)[None, :], (128, D)))
    mt = np.zeros((128, 2048), dtype=np.float32)
    for j in range(4):
        kt = 128 * j + np.arange(128)[:, None]
        q = np.arange(512)[None, :]
        mt[:, j * 512:(j + 1) * 512] = (kt <= q).astype(np.float32)
    ident = np.eye(128, dtype=np.float32)
    pperm = np.zeros((128, 128), dtype=np.float32)
    for m in range(64):
        pperm[m + 64, m] = -1.0
    for m in range(64, 128):
        pperm[m - 64, m] = 1.0
    shared = dict(wk=np.ascontiguousarray(L0_attn_wk, np.float32),
                  wv=np.ascontiguousarray(L0_attn_wv, np.float32),
                  wq=np.ascontiguousarray(L0_attn_wq, np.float32),
                  wo=np.ascontiguousarray(L0_attn_wo, np.float32),
                  gate_rl=gate_rl_, up_rl=up_rl_, down_r8=down_r8_,
                  gs=gs, us=us, dsb=dsb_, maskt=mt, identf=ident, pperm=pperm,
                  ones1=np.ones((1, 128), dtype=np.float32),
                  onesc=np.ones((128, 1), dtype=np.float32),
                  onesm=np.ones((128, 128), dtype=np.float32))
    in_maps = []
    for c in range(NC_):
        c0, c1 = c, 7 - c
        pos = np.concatenate([
            np.arange(c0 * CH, (c0 + 1) * CH),
            np.arange(c1 * CH, (c1 + 1) * CH),
            np.arange(0, c0 * CH),
            np.arange(0, c1 * CH),
        ])
        ba = np.concatenate([np.zeros(CH, np.int64), np.ones(CH, np.int64),
                             np.zeros(c0 * CH, np.int64), np.ones(c1 * CH, np.int64)])
        m = dict(shared)
        m.update(x_pre=np.ascontiguousarray(x[ba, pos, :]),
                 cosq=np.ascontiguousarray(cos_t[pos].T),
                 sinq=np.ascontiguousarray(sin_t[pos].T),
                 cid=np.array([[c]], dtype=np.int32))
        in_maps.append(m)
    return in_maps


_NC_CACHE = {}


def kernel(**inputs):
    in_maps = _host_inputs(**inputs)
    if "full" not in _NC_CACHE:
        _NC_CACHE["full"] = build()
    res = run_bass_kernel_spmd(_NC_CACHE["full"], in_maps, core_ids=list(range(NC_)))
    out = np.empty((B, T, D), dtype=np.float32)
    for c in range(NC_):
        oo = res.results[c]["out_own"]
        out[0, c * CH:(c + 1) * CH, :] = oo[0:CH]
        out[1, (7 - c) * CH:(8 - c) * CH, :] = oo[CH:NOWN]
    return out


# version 11
# speedup vs baseline: 1.3107x; 1.0553x over previous
import math
from contextlib import ExitStack
import numpy as np
import concourse.bass as bass
import concourse.mybir as mybir
from concourse.bass_utils import run_bass_kernel_spmd

F32 = mybir.dt.float32
BF16 = mybir.dt.bfloat16
I32 = mybir.dt.int32
AF = mybir.ActivationFunctionType
ALU = mybir.AluOpType
AX = mybir.AxisListType

B, T, D, H, KV, HD, DFF = 2, 4096, 4096, 32, 8, 128, 11008
NC_ = 8
CH = 512
NG = 9
NPRE = NG * CH
NT = NG * 4
NOWN = 1024
DC = D // 128
FC = DFF // 128
EPS = 1e-5
ISQ = 1.0 / math.sqrt(HD)
R4096 = 1.0 / 4096.0


class Sem:
    def __init__(self, nc, name):
        self.h = nc.alloc_semaphore(name)
        self.n = 0

    def inc(self, instr, k=1):
        instr.then_inc(self.h, k)
        self.n += k
        return self.n


def _w(eng, sem, val):
    if val is not None and val > 0:
        eng.wait_ge(sem.h, val)


def _wt(eng, token):
    if token is not None:
        sem, v = token
        _w(eng, sem, v)


def _emit_oav(t_, tn, vn, sn, exp_done, mask_done, oav_done, onesm_b, v_sb, pt,
              o_ps, sums_ps, kb, gi, idx, kc, hf, g0, nh, ott_prev):
    _w(t_, sn, exp_done[gi])
    if idx <= 1:
        _w(t_, vn, mask_done[gi])
    if idx == 0:
        _w(t_, vn, ott_prev)
    first = (idx == 0)
    last = (idx == nh - 1)
    ins = None
    for u in range(2):
        ktile = kc * 4 + hf * 2 + u
        st = first and (u == 0)
        sp = last and (u == 1)
        t_.matmul(sums_ps[:], onesm_b[:],
                  pt[:, gi % 2, u * CH:(u + 1) * CH],
                  start=st, stop=sp, skip_group_check=True)
        ins = t_.matmul(o_ps[:], v_sb[:, kb, ktile, :],
                        pt[:, gi % 2, u * CH:(u + 1) * CH],
                        start=st, stop=sp, skip_group_check=True)
    tn.inc(ins)
    oav_done[gi] = tn.n


def build(phases=(1, 2, 3, 4, 5, 6), dbg=()):
    nc = bass.Bass("TRN2", target_bir_lowering=False)
    t_, v_, s_, g_, y_ = nc.tensor, nc.vector, nc.scalar, nc.gpsimd, nc.sync

    def scr(name, shape, dtype):
        k = "ExternalOutput" if name in dbg else "Internal"
        return nc.dram_tensor(name, shape, dtype, kind=k)

    x_pre = nc.dram_tensor("x_pre", [NPRE, D], F32, kind="ExternalInput")
    cosq = nc.dram_tensor("cosq", [HD, NPRE], F32, kind="ExternalInput")
    sinq = nc.dram_tensor("sinq", [HD, NPRE], F32, kind="ExternalInput")
    wk_in = nc.dram_tensor("wk", [D, KV * HD], F32, kind="ExternalInput")
    wv_in = nc.dram_tensor("wv", [D, KV * HD], F32, kind="ExternalInput")
    wq_in = nc.dram_tensor("wq", [D, H * HD], F32, kind="ExternalInput")
    wo_in = nc.dram_tensor("wo", [H * HD, D], F32, kind="ExternalInput")
    gate_rl = nc.dram_tensor("gate_rl", [D, DFF], mybir.dt.int8, kind="ExternalInput")
    up_rl = nc.dram_tensor("up_rl", [D, DFF], mybir.dt.int8, kind="ExternalInput")
    down_r8 = nc.dram_tensor("down_r8", [FC, 128, D], mybir.dt.int8, kind="ExternalInput")
    gs_in = nc.dram_tensor("gs", [128, FC], F32, kind="ExternalInput")
    us_in = nc.dram_tensor("us", [128, FC], F32, kind="ExternalInput")
    dsb_in = nc.dram_tensor("dsb", [128, D], F32, kind="ExternalInput")
    maskt_in = nc.dram_tensor("maskt", [128, 2048], F32, kind="ExternalInput")
    ident_in = nc.dram_tensor("identf", [128, 128], F32, kind="ExternalInput")
    pperm_in = nc.dram_tensor("pperm", [128, 128], F32, kind="ExternalInput")
    ones1_in = nc.dram_tensor("ones1", [1, 128], F32, kind="ExternalInput")
    onesc_in = nc.dram_tensor("onesc", [128, 1], F32, kind="ExternalInput")
    onesm_in = nc.dram_tensor("onesm", [128, 128], F32, kind="ExternalInput")
    cid_in = nc.dram_tensor("cid", [1, 1], I32, kind="ExternalInput")
    out_own = nc.dram_tensor("out_own", [NOWN, D], F32, kind="ExternalOutput")

    xn_s = scr("xn_s", [DC, 128, NPRE], BF16)
    k_s = scr("k_s", [KV, HD, NPRE], BF16)
    v_s = scr("v_s", [KV, NT, 128, HD], BF16)
    h_s = scr("h_s", [FC, 128, NOWN], BF16)
    wq_b = scr("wq_b", [H, 128, DC * 128], BF16)
    wo_b = scr("wo_b", [8, 128, H * CH], BF16)

    tn = Sem(nc, "tn")
    vn = Sem(nc, "vn")
    sn = Sem(nc, "sn")
    gn = Sem(nc, "gn")
    dP = Sem(nc, "dP")
    dPW = Sem(nc, "dPW")
    dWK = Sem(nc, "dWK")
    dWV = Sem(nc, "dWV")
    dXs = [Sem(nc, "dX0"), Sem(nc, "dX1"), Sem(nc, "dX2")]
    dCSs = [Sem(nc, "dCS0"), Sem(nc, "dCS1")]
    dXNs = [Sem(nc, "dXN0"), Sem(nc, "dXN1")]
    dKs = [Sem(nc, "dK0"), Sem(nc, "dK1")]
    dVs = [Sem(nc, "dV0"), Sem(nc, "dV1")]
    dXRs = [Sem(nc, "dXR0"), Sem(nc, "dXR1")]
    dGUs = [Sem(nc, "dGU0"), Sem(nc, "dGU1")]
    dHs = [Sem(nc, "dH0"), Sem(nc, "dH1")]
    dHRs = [Sem(nc, "dHR0"), Sem(nc, "dHR1")]
    dDNs = [Sem(nc, "dDN0"), Sem(nc, "dDN1"),
            Sem(nc, "dDN2"), Sem(nc, "dDN3")]
    dOAs = [Sem(nc, "dOA0"), Sem(nc, "dOA1")]
    dWQs = [Sem(nc, "dWQ0"), Sem(nc, "dWQ1")]
    dONE = Sem(nc, "dONE")
    dKVs = [Sem(nc, "dKV0"), Sem(nc, "dKV1")]
    dWOs = [Sem(nc, "dWO0"), Sem(nc, "dWO1")]
    dXR6 = [Sem(nc, "dXR6a"), Sem(nc, "dXR6b"), Sem(nc, "dXR6c"), Sem(nc, "dXR6d")]
    dACs = [Sem(nc, "dAC0"), Sem(nc, "dAC1"), Sem(nc, "dAC2"), Sem(nc, "dAC3")]
    dQC = [Sem(nc, "dQC0"), Sem(nc, "dQC1")]
    dQS = [Sem(nc, "dQS0"), Sem(nc, "dQS1")]
    dOC = Sem(nc, "dOC")
    dOS = Sem(nc, "dOS")

    all_dma_sems = ([tn, vn, sn, gn, dP, dPW, dWK, dWV, dONE] + dXs + dCSs + dXNs
                    + dKs + dVs + dXRs + dGUs + dHs + dHRs + dDNs + dOAs + dWQs
                    + dKVs + dWOs + dXR6 + dACs + dQC + dQS + [dOC, dOS])

    def quiesce():
        for sem in all_dma_sems:
            if sem.n > 0:
                y_.wait_ge(sem.h, sem.n)
                g_.wait_ge(sem.h, sem.n)
        nc.all_engine_barrier()

    with ExitStack() as _st0:
        def _sb(name, shape, dtype):
            return _st0.enter_context(nc.sbuf_tensor(name, shape, dtype))
        ident_b = _sb("ident_b", [128, 128], BF16)
        ident_f = _sb("ident_f", [128, 128], F32)
        pperm_b = _sb("pperm_b", [128, 128], BF16)
        ones1_f = _sb("ones1_f", [1, 128], F32)
        onesc_b = _sb("onesc_b", [128, 1], BF16)
        onesm_b = _sb("onesm_b", [128, 128], BF16)
        maskt_b = _sb("maskt_b", [128, 2048], BF16)
        gs_sb = _sb("gs_sb", [128, FC], F32)
        us_sb = _sb("us_sb", [128, FC], F32)
        cid_sb = _sb("cid_sb", [1, 1], I32)
        stat = _sb("stat", [128, 24], F32)
        for ap, src in ((ident_f, ident_in), (ones1_f, ones1_in), (gs_sb, gs_in),
                        (us_sb, us_in), (cid_sb, cid_in)):
            dP.n = (y_.dma_start(ap[:], src[:]).then_inc(dP.h, 16), dP.n + 16)[1]
        for ap, src in ((ident_b, ident_in), (pperm_b, pperm_in),
                        (onesc_b, onesc_in), (maskt_b, maskt_in),
                        (onesm_b, onesm_in)):
            dPW.n = (g_.dma_start(ap[:], src[:]).then_inc(dPW.h, 16), dPW.n + 16)[1]
        per_done = dP.n
        perw_done = dPW.n

        if 1 in phases:
            with ExitStack() as _st1:
                def _sb1(name, shape, dtype):
                    return _st1.enter_context(nc.sbuf_tensor(name, shape, dtype))
                def _ps1(name, shape, dtype):
                    return _st1.enter_context(nc.psum_tensor(name, shape, dtype))
                wk_sb = _sb1("wk_sb", [128, DC, KV * HD], BF16)
                x_sb = _sb1("x_sb", [128, 2, D], F32)
                xnb = _sb1("xnb", [128, 2, D], BF16)
                xnf = _sb1("xnf", [128, 2, DC, CH], BF16)
                cos1 = _sb1("cos1", [128, 2, CH], F32)
                sin1 = _sb1("sin1", [128, 2, CH], F32)
                kbf = _sb1("kbf", [128, 2, CH], BF16)
                kc = _sb1("kc", [128, 2, CH], F32)
                rs = _sb1("rs", [128, 2, CH], F32)
                krot = _sb1("krot", [128, 2, CH], BF16)
                rbf = _sb1("rbf", [128, 2, CH], BF16)
                pst0 = _ps1("pst0", [128, 2048], BF16)
                pst1 = _ps1("pst1", [128, 2048], BF16)
                kps = (_ps1("kps0", [128, CH], F32), _ps1("kps1", [128, CH], F32))
                rps = (_ps1("rps0", [128, CH], F32), _ps1("rps1", [128, CH], F32))
                pst = (pst0, pst1)
                wk_r = wk_in.rearrange("(dc p) f -> p dc f", p=128)
                for q in range(4):
                    g_.dma_start(wk_sb[:, q * 8:(q + 1) * 8, :],
                                 wk_r[:, q * 8:(q + 1) * 8, :]).then_inc(dWK.h, 16)
                    dWK.n += 16
                wk_ready = dWK.n

                x_tok = {}
                sq_done = {}
                ln_done = {}
                nmr_prev = {}
                tr_done = {}
                ev_done = {}
                xnst_tok = {}
                cs_tok = {}
                kmm_done, cp_done, rh_done, rcp_done = {}, {}, {}, {}
                kc_done, rsn_done, ga_done = {}, {}, {}
                kst_tok = {}

                def _emit_tr(Tj):
                    if (Tj, 0) in tr_done:
                        return
                    slj = Tj % 2
                    ttj = Tj % 4
                    gj = Tj // 4
                    gj2 = gj % 2
                    _w(t_, sn, ln_done[Tj])
                    if Tj == 0:
                        _w(t_, dPW, perw_done)
                    if ttj == 0:
                        _wt(t_, xnst_tok.get(gj - 2))
                    for half in range(2):
                        _w(t_, vn, ev_done.get((Tj - 1, half)))
                        ins2 = None
                        for j2 in range(16):
                            dc2 = half * 16 + j2
                            ins2 = t_.transpose(pst[half][:, j2 * 128:(j2 + 1) * 128],
                                                xnb[:, slj, dc2 * 128:(dc2 + 1) * 128],
                                                ident_b[:])
                        tn.inc(ins2)
                        tr_done[(Tj, half)] = tn.n
                        _w(v_, tn, tr_done[(Tj, half)])
                        vn.inc(v_.tensor_copy(
                            xnf.ap()[:, gj2, half * 16:(half + 1) * 16,
                                     ttj * 128:(ttj + 1) * 128],
                            pst[half].ap().rearrange("p (dc t) -> p dc t", t=128)))
                        ev_done[(Tj, half)] = vn.n

                def _rope_fin(rj):
                    gj, fj = rj // 8, rj % 8
                    fbj = rj % 2
                    _w(t_, sn, cp_done[rj])
                    _w(t_, sn, rcp_done.get(rj - 2))
                    tn.inc(t_.matmul(rps[fbj][:], pperm_b[:], kbf[:, fbj, :],
                                     start=True, stop=True))
                    rh_done[rj] = tn.n
                    _w(s_, tn, rh_done[rj])
                    _w(s_, gn, ga_done.get(rj - 2))
                    sn.inc(s_.activation(rbf[:, fbj, :], rps[fbj][:], AF.Copy))
                    rcp_done[rj] = sn.n
                    _w(g_, sn, rcp_done[rj])
                    _wt(g_, cs_tok[gj])
                    _w(g_, gn, ga_done.get(rj - 2))
                    g_.tensor_mul(kc[:, fbj, :], kbf[:, fbj, :],
                                  cos1[:, gj % 2, :])
                    g_.tensor_mul(rs[:, fbj, :], rbf[:, fbj, :],
                                  sin1[:, gj % 2, :])
                    g_.drain()
                    _wt(g_, kst_tok.get(rj - 2))
                    gn.inc(g_.tensor_add(krot[:, fbj, :], kc[:, fbj, :],
                                         rs[:, fbj, :]))
                    ga_done[rj] = gn.n
                    g_.drain()
                    ke = dKs[fbj]
                    g_.dma_start(k_s[fj, :, gj * CH:(gj + 1) * CH], krot[:, fbj, :])\
                        .then_inc(ke.h, 16)
                    ke.n += 16
                    kst_tok[rj] = (ke, ke.n)

                def _emit_f(gq, f):
                    ri = gq * 8 + f
                    fb = ri % 2
                    if ri == 0:
                        _w(t_, dWK, wk_ready)
                    if f == 0:
                        _w(t_, vn, ev_done[(gq * 4 + 3, 1)])
                    _w(t_, sn, cp_done.get(ri - 2))
                    ins = None
                    for dc in range(DC):
                        ins = t_.matmul(kps[fb][:],
                                        wk_sb[:, dc, f * 128:(f + 1) * 128],
                                        xnf[:, gq % 2, dc, :],
                                        start=(dc == 0), stop=(dc == DC - 1))
                    tn.inc(ins)
                    kmm_done[ri] = tn.n
                    _w(s_, tn, kmm_done[ri])
                    _w(s_, tn, rh_done.get(ri - 2))
                    _w(s_, gn, ga_done.get(ri - 2))
                    sn.inc(s_.activation(kbf[:, fb, :], kps[fb][:], AF.Copy))
                    cp_done[ri] = sn.n
                    if ri >= 1:
                        _rope_fin(ri - 1)

                for g in range(NG):
                    _w(g_, gn, ga_done.get((g - 2) * 8 + 7))
                    cse = dCSs[g % 2]
                    g_.dma_start(cos1[:, g % 2, :], cosq[:, g * CH:(g + 1) * CH])\
                        .then_inc(cse.h, 16)
                    g_.dma_start(sin1[:, g % 2, :], sinq[:, g * CH:(g + 1) * CH])\
                        .then_inc(cse.h, 16)
                    cse.n += 32
                    cs_tok[g] = (cse, cse.n)

                    for tt in range(4):
                        Ti = g * 4 + tt
                        sl = Ti % 2
                        o8 = sl * 8
                        _w(y_, sn, ln_done.get(Ti - 2))
                        xe = dXs[sl]
                        for hp in range(2):
                            y_.dma_start(x_sb[hp * 64:(hp + 1) * 64, sl, :],
                                         x_pre[Ti * 128 + hp * 64:
                                               Ti * 128 + (hp + 1) * 64, :])\
                                .then_inc(xe.h, 16)
                        xe.n += 32
                        x_tok[Ti] = (xe, xe.n)

                        _wt(s_, x_tok[Ti])
                        _w(s_, vn, nmr_prev.get(Ti - 2))
                        _w(s_, tn, tr_done.get((Ti - 2, 1)))
                        sn.inc(s_.activation(xnb[:, sl, :], x_sb[:, sl, :],
                                             AF.Square,
                                             accum_out=stat[:, o8 + 0:o8 + 1]))
                        ssq_v = sn.n
                        _wt(v_, x_tok[Ti])
                        v_.reduce_sum(stat[:, o8 + 1:o8 + 2], x_sb[:, sl, :], axis=AX.X)
                        v_.drain()
                        v_.tensor_scalar_mul(stat[:, o8 + 2:o8 + 3],
                                             stat[:, o8 + 1:o8 + 2], R4096)
                        v_.drain()
                        v_.tensor_mul(stat[:, o8 + 3:o8 + 4], stat[:, o8 + 2:o8 + 3],
                                      stat[:, o8 + 2:o8 + 3])
                        _w(v_, sn, ssq_v)
                        v_.tensor_scalar(stat[:, o8 + 4:o8 + 5], stat[:, o8 + 0:o8 + 1],
                                         R4096, EPS, ALU.mult, ALU.add)
                        v_.drain()
                        vn.inc(v_.tensor_sub(stat[:, o8 + 5:o8 + 6],
                                             stat[:, o8 + 4:o8 + 5],
                                             stat[:, o8 + 3:o8 + 4]))
                        varf_v = vn.n
                        _w(s_, vn, varf_v)
                        sn.inc(s_.activation(stat[:, o8 + 6:o8 + 7],
                                             stat[:, o8 + 5:o8 + 6], AF.Sqrt))
                        sq_done[Ti] = sn.n
                        _w(v_, sn, sq_done[Ti])
                        v_.reciprocal(stat[:, o8 + 7:o8 + 8], stat[:, o8 + 6:o8 + 7])
                        v_.drain()
                        vn.inc(v_.tensor_scalar(stat[:, o8 + 2:o8 + 3],
                                                stat[:, o8 + 1:o8 + 2],
                                                -R4096, stat[:, o8 + 7:o8 + 8],
                                                ALU.mult, ALU.mult))
                        nmr_done = vn.n
                        nmr_prev[Ti] = vn.n
                        _w(s_, vn, nmr_done)
                        sn.inc(s_.activation(xnb[:, sl, :], x_sb[:, sl, :],
                                             AF.Identity,
                                             bias=stat[:, o8 + 2:o8 + 3],
                                             scale=stat[:, o8 + 7:o8 + 8]))
                        ln_done[Ti] = sn.n

                        if Ti >= 1:
                            _emit_tr(Ti - 1)
                        if g >= 1:
                            _emit_f(g - 1, 2 * tt)
                            _emit_f(g - 1, 2 * tt + 1)
                    _emit_tr(g * 4 + 3)

                    _w(s_, vn, ev_done[(g * 4 + 3, 1)])
                    xne = dXNs[g % 2]
                    for hp in range(2):
                        s_.dma_start(xn_s.ap()[:, hp * 64:(hp + 1) * 64,
                                               g * CH:(g + 1) * CH]
                                     .rearrange("dc p t -> p dc t"),
                                     xnf[hp * 64:(hp + 1) * 64, g % 2, :, :])\
                            .then_inc(xne.h, 16)
                    xne.n += 32
                    xnst_tok[g] = (xne, xne.n)

                for f2 in range(8):
                    _emit_f(NG - 1, f2)
                _rope_fin(NG * 8 - 1)
            quiesce()

        if 2 in phases:
            with ExitStack() as _st2:
                wv_sb = _st2.enter_context(nc.sbuf_tensor("wv_sb", [128, DC, KV * HD], BF16))
                xng = _st2.enter_context(nc.sbuf_tensor("xng", [128, 2, DC, CH], BF16))
                vbf = _st2.enter_context(nc.sbuf_tensor("vbf", [128, 2, KV * HD], BF16))
                vps = (_st2.enter_context(nc.psum_tensor("vps0", [128, 1024], F32)),
                       _st2.enter_context(nc.psum_tensor("vps1", [128, 1024], F32)))
                wv_r = wv_in.rearrange("(dc p) f -> p dc f", p=128)
                for q in range(4):
                    g_.dma_start(wv_sb[:, q * 8:(q + 1) * 8, :],
                                 wv_r[:, q * 8:(q + 1) * 8, :]).then_inc(dWV.h, 16)
                    dWV.n += 16
                wv_ready = dWV.n
                xg_tok = {}
                vmm_done = {}
                vcp_done = {}
                vst_tok = {}
                for g in range(NG):
                    _w(y_, tn, vmm_done.get((g - 2, 3)))
                    xre = dXRs[g % 2]
                    y_.dma_start(xng[:, g % 2, :, :],
                                 xn_s.ap()[:, :, g * CH:(g + 1) * CH]
                                 .rearrange("dc p t -> p dc t")).then_inc(xre.h, 16)
                    xre.n += 16
                    xg_tok[g] = (xre, xre.n)
                    for tt in range(4):
                        Ti = g * 4 + tt
                        sl = Ti % 2
                        if Ti == 0:
                            _w(t_, dWV, wv_ready)
                        _wt(t_, xg_tok[g])
                        _w(t_, sn, vcp_done.get((g, tt - 2),
                                                vcp_done.get((g - 1, tt + 2))))
                        ins = None
                        for dc in range(DC):
                            for half in range(2):
                                ins = t_.matmul(vps[sl][:, half * 512:(half + 1) * 512],
                                                xng[:, g % 2, dc, tt * 128:(tt + 1) * 128],
                                                wv_sb[:, dc, half * 512:(half + 1) * 512],
                                                start=(dc == 0), stop=(dc == DC - 1))
                        tn.inc(ins)
                        vmm_done[(g, tt)] = tn.n
                        _w(s_, tn, vmm_done[(g, tt)])
                        _wt(s_, vst_tok.get((g, tt - 2), vst_tok.get((g - 1, tt + 2))))
                        sn.inc(s_.activation(vbf[:, sl, :], vps[sl][:], AF.Copy))
                        vcp_done[(g, tt)] = sn.n
                        s_.drain()
                        ve = dVs[sl]
                        s_.dma_start(v_s.ap()[:, Ti, :, :].rearrange("kv p d -> p kv d"),
                                     vbf.ap()[:, sl, :].rearrange("p (kv d) -> p kv d", d=HD))\
                            .then_inc(ve.h, 16)
                        ve.n += 16
                        vst_tok[(g, tt)] = (ve, ve.n)
            quiesce()

        if 3 in phases:
            with ExitStack() as _st3:
                def _sb3(name, shape, dtype):
                    return _st3.enter_context(nc.sbuf_tensor(name, shape, dtype))
                xno = _sb3("xno", [128, DC, NOWN], BF16)
                gu_sb = _sb3("gu_sb", [128, 2, 2, DC, 128], BF16)
                silu_sb = _sb3("silu_sb", [128, 2, NOWN], BF16)
                hst = _sb3("hst", [128, 2, NOWN], BF16)
                wqstg = _sb3("wqstg", [128, 2, DC, 128], BF16)
                gps = (_st3.enter_context(nc.psum_tensor("gps0", [128, NOWN], F32)),
                       _st3.enter_context(nc.psum_tensor("gps1", [128, NOWN], F32)))
                ups = (_st3.enter_context(nc.psum_tensor("ups0", [128, NOWN], F32)),
                       _st3.enter_context(nc.psum_tensor("ups1", [128, NOWN], F32)))
                y_.dma_start(xno[:], xn_s.ap()[:, :, 0:NOWN]
                             .rearrange("dc p t -> p dc t")).then_inc(dONE.h, 16)
                dONE.n += 16
                xno_rd = dONE.n
                ga_r = gate_rl.rearrange("(dc p) f -> p dc f", p=128)
                up_r = up_rl.rearrange("(dc p) f -> p dc f", p=128)
                wq_r3 = wq_in.rearrange("(dc p) f -> p dc f", p=128)
                gumm = {}
                sil_done = {}
                hstt = {}
                h_tok = {}
                wqb_ld, wqb_st = {}, {}
                for c in range(FC):
                    cb = c % 2
                    _w(g_, tn, gumm.get(c - 2))
                    gue = dGUs[cb]
                    for gu_i, gu_r in ((0, ga_r), (1, up_r)):
                        for hdc in range(2):
                            g_.dma_start(gu_sb[:, gu_i, cb, hdc * 16:(hdc + 1) * 16, :],
                                         gu_r[:, hdc * 16:(hdc + 1) * 16,
                                              c * 128:(c + 1) * 128]).then_inc(gue.h, 16)
                    gue.n += 64
                    if c >= 2 and c % 2 == 0 and (c - 2) // 2 < H:
                        f = (c - 2) // 2
                        sq = f % 2
                        _wt(g_, wqb_st.get(f - 2))
                        qce = dQC[sq]
                        g_.dma_start(wqstg[:, sq, :, :],
                                     wq_r3[:, :, f * 128:(f + 1) * 128])\
                            .then_inc(qce.h, 16)
                        qce.n += 16
                        wqb_ld[f] = (qce, qce.n)
                        _wt(y_, wqb_ld[f])
                        qse = dQS[sq]
                        y_.dma_start(wq_b.ap()[f, :, :]
                                     .rearrange("p (dc u) -> p dc u", u=128),
                                     wqstg[:, sq, :, :]).then_inc(qse.h, 16)
                        qse.n += 16
                        wqb_st[f] = (qse, qse.n)
                    _w(t_, gue, gue.n)
                    if c == 0:
                        _w(t_, dONE, xno_rd)
                    _w(t_, sn, sil_done.get(c - 2))
                    _w(t_, vn, hstt.get(c - 2))
                    ins = None
                    for dc in range(DC):
                        st0, st1 = (dc == 0), (dc == DC - 1)
                        for gg in range(2):
                            t_.matmul(gps[cb][:, gg * CH:(gg + 1) * CH],
                                      gu_sb[:, 0, cb, dc, :],
                                      xno[:, dc, gg * CH:(gg + 1) * CH],
                                      start=st0, stop=st1)
                        for gg in range(2):
                            ins = t_.matmul(ups[cb][:, gg * CH:(gg + 1) * CH],
                                            gu_sb[:, 1, cb, dc, :],
                                            xno[:, dc, gg * CH:(gg + 1) * CH],
                                            start=st0, stop=st1)
                    tn.inc(ins)
                    gumm[c] = tn.n
                    _w(s_, tn, gumm[c])
                    _w(s_, vn, hstt.get(c - 2))
                    ins = None
                    for gg in range(2):
                        ins = s_.activation(silu_sb[:, cb, gg * CH:(gg + 1) * CH],
                                            gps[cb][:, gg * CH:(gg + 1) * CH],
                                            AF.Silu, scale=gs_sb[:, c:c + 1])
                    sn.inc(ins)
                    sil_done[c] = sn.n
                    _w(v_, sn, sil_done[c])
                    _wt(v_, h_tok.get(c - 2))
                    ins = None
                    for gg in range(2):
                        ins = v_.scalar_tensor_tensor(
                            hst[:, cb, gg * CH:(gg + 1) * CH],
                            ups[cb][:, gg * CH:(gg + 1) * CH],
                            us_sb[:, c:c + 1],
                            silu_sb[:, cb, gg * CH:(gg + 1) * CH],
                            ALU.mult, ALU.mult)
                    vn.inc(ins)
                    hstt[c] = vn.n
                    _w(y_, vn, hstt[c])
                    he = dHs[cb]
                    y_.dma_start(h_s[c, :, :], hst[:, cb, :]).then_inc(he.h, 16)
                    he.n += 16
                    h_tok[c] = (he, he.n)
                h_all = [(dHs[0], dHs[0].n), (dHs[1], dHs[1].n)]
            quiesce()

            with ExitStack() as _st3b:
                def _sb3b(name, shape, dtype):
                    return _st3b.enter_context(nc.sbuf_tensor(name, shape, dtype))
                h_g = _sb3b("h_g", [128, FC, CH], BF16)
                dn2 = _sb3b("dn2", [128, 4, 4, CH], BF16)
                dsb = _sb3b("dsb_t", [128, D], F32)
                out_sb = _sb3b("out_sb", [128, 4, CH], F32)
                wostg = _sb3b("wostg", [128, H, CH], BF16)
                wo_r3 = wo_in.rearrange("(hh p) d -> p hh d", p=128)
                wob_ld = wob_st = None
                dps8 = [_st3b.enter_context(nc.psum_tensor(f"dps{t}", [128, CH], F32))
                        for t in range(8)]
                y_.dma_start(dsb[:], dsb_in[:]).then_inc(dONE.h, 16)
                dONE.n += 16
                dsb_rd = dONE.n
                quads = []
                f0 = 0
                while f0 < FC:
                    quads.append((f0, min(4, FC - f0)))
                    f0 += 4
                NQ = len(quads)
                qmm = {}
                blk_done = {}
                ev_done = {}
                st_tok = {}
                ld_tok = {}
                hg_tok = None
                for g in range(2):
                    _w(y_, tn, blk_done.get((g - 1, 7)))
                    for tok_ in h_all:
                        _wt(y_, tok_)
                    hge = dHRs[g % 2]
                    fcq = [(0, 21), (21, 43), (43, 64), (64, 86)]
                    for a, b in fcq:
                        y_.dma_start(h_g[:, a:b, :],
                                     h_s.ap()[a:b, :, g * CH:(g + 1) * CH]
                                     .rearrange("fc p t -> p fc t")).then_inc(hge.h, 16)
                    hge.n += 64
                    hg_tok = (hge, hge.n)
                    for dblk in range(8):
                        bb = (dblk % 2) * 4
                        bi2 = g * 8 + dblk
                        dm2 = bi2 // 2
                        for qi, (f0, qn) in enumerate(quads):
                            if qi in (8, 15) and bi2 % 2 == 0:
                                _wt(g_, wob_st)
                                h16 = 0 if qi == 8 else 1
                                g_.dma_start(wostg[:, h16 * 16:(h16 + 1) * 16, :],
                                             wo_r3[:, h16 * 16:(h16 + 1) * 16,
                                                   dm2 * CH:(dm2 + 1) * CH])\
                                    .then_inc(dOC.h, 16)
                                dOC.n += 16
                                wob_ld = (dOC, dOC.n)
                            if qi == 8 and bi2 % 2 == 1:
                                _wt(y_, wob_ld)
                                y_.dma_start(wo_b.ap()[dm2, :, :]
                                             .rearrange("p (hh u) -> p hh u", u=CH),
                                             wostg[:]).then_inc(dOS.h, 16)
                                dOS.n += 16
                                wob_st = (dOS, dOS.n)
                            qq = (g * 8 + dblk) * NQ + qi
                            slot = qq % 4
                            _w(g_, tn, qmm.get(qq - 4))
                            dne = dDNs[slot]
                            g_.dma_start(dn2[:, slot, 0:qn, :],
                                         down_r8[f0:f0 + qn, :,
                                                 dblk * CH:(dblk + 1) * CH]
                                         .rearrange("f p d -> p f d")).then_inc(dne.h, 16)
                            dne.n += 16
                            ld_tok[qq] = (dne, dne.n)
                            _wt(t_, ld_tok[qq])
                            if qi == 0:
                                if dblk == 0:
                                    _wt(t_, hg_tok)
                                bi = g * 8 + dblk - 2
                                if bi >= 0:
                                    _w(t_, vn, ev_done.get((bi // 8, bi % 8, 3)))
                            ins = None
                            for q in range(qn):
                                fc = f0 + q
                                for tt in range(4):
                                    ins = t_.matmul(dps8[bb + tt][:],
                                                    h_g[:, fc, tt * 128:(tt + 1) * 128],
                                                    dn2[:, slot, q, :],
                                                    start=(fc == 0), stop=(fc == FC - 1))
                            tn.inc(ins)
                            qmm[qq] = tn.n
                        blk_done[(g, dblk)] = tn.n
                        for tt in range(4):
                            _w(v_, tn, blk_done[(g, dblk)])
                            if dblk == 0 and g == 0 and tt == 0:
                                _w(v_, dONE, dsb_rd)
                            _wt(v_, st_tok.get((g, dblk - 1, tt),
                                               st_tok.get((g - 1, 7, tt))))
                            vn.inc(v_.tensor_mul(out_sb[:, tt, :], dps8[bb + tt][:],
                                                 dsb[:, dblk * CH:(dblk + 1) * CH]))
                            ev_done[(g, dblk, tt)] = vn.n
                            _w(y_, vn, ev_done[(g, dblk, tt)])
                            oae = dOAs[(dblk * 4 + tt) % 2]
                            y_.dma_start(out_own[g * CH + tt * 128:
                                                 g * CH + (tt + 1) * 128,
                                                 dblk * CH:(dblk + 1) * CH],
                                         out_sb[:, tt, :]).then_inc(oae.h, 16)
                            oae.n += 16
                            st_tok[(g, dblk, tt)] = (oae, oae.n)
                p3_out = [(dOAs[0], dOAs[0].n), (dOAs[1], dOAs[1].n)]
            quiesce()
        else:
            p3_out = []

        q_sb = None
        if 4 in phases:
            _stq = ExitStack()
            q_sb = _stq.enter_context(nc.sbuf_tensor("q_sb", [128, H, NOWN], BF16))
            with ExitStack() as _st4:
                def _sb4(name, shape, dtype):
                    return _st4.enter_context(nc.sbuf_tensor(name, shape, dtype))
                xno2 = _sb4("xno2", [128, DC, NOWN], BF16)
                wq_sb = _sb4("wq_sb", [128, 3, DC, 128], BF16)
                qbf = _sb4("qbf", [128, 2, NOWN], BF16)
                qc = _sb4("qc", [128, 2, NOWN], F32)
                rs2 = _sb4("rs2", [128, 2, NOWN], F32)
                cs_own = _sb4("cs_own", [128, 2, NOWN], F32)
                qps = (_st4.enter_context(nc.psum_tensor("qps0", [128, NOWN], F32)),
                       _st4.enter_context(nc.psum_tensor("qps1", [128, NOWN], F32)))
                rqps = (_st4.enter_context(nc.psum_tensor("rqps0", [128, NOWN], F32)),
                        _st4.enter_context(nc.psum_tensor("rqps1", [128, NOWN], F32)))
                for hp in range(4):
                    y_.dma_start(xno2[hp * 32:(hp + 1) * 32, :, :],
                                 xn_s.ap()[:, hp * 32:(hp + 1) * 32, 0:NOWN]
                                 .rearrange("dc p t -> p dc t")).then_inc(dONE.h, 16)
                g_.dma_start(cs_own[:, 0, :], cosq[:, 0:NOWN]).then_inc(dONE.h, 16)
                g_.dma_start(cs_own[:, 1, :], sinq[:, 0:NOWN]).then_inc(dONE.h, 16)
                dONE.n += 96
                p4_rd = dONE.n
                qmm, qcp, qrh, qcd, qrs, qga = {}, {}, {}, {}, {}, {}

                def _qrope_fin(fj):
                    fbj = fj % 2
                    _w(t_, sn, qcp[fj])
                    _w(t_, vn, qrs.get(fj - 2))
                    ins2 = None
                    for gg in range(2):
                        ins2 = t_.matmul(rqps[fbj][:, gg * CH:(gg + 1) * CH], pperm_b[:],
                                         qbf[:, fbj, gg * CH:(gg + 1) * CH],
                                         start=True, stop=True)
                    tn.inc(ins2)
                    qrh[fj] = tn.n
                    _w(v_, sn, qcp[fj])
                    _w(v_, gn, qga.get(fj - 2))
                    vn.inc(v_.tensor_mul(qc[:, fbj, :], qps[fbj][:], cs_own[:, 0, :]))
                    qcd[fj] = vn.n
                    _w(v_, tn, qrh[fj])
                    vn.inc(v_.tensor_mul(rs2[:, fbj, :], rqps[fbj][:], cs_own[:, 1, :]))
                    qrs[fj] = vn.n
                    _w(g_, vn, qrs[fj])
                    gn.inc(g_.tensor_add(q_sb[:, fj, :], qc[:, fbj, :], rs2[:, fbj, :]))
                    qga[fj] = gn.n

                for f in range(H):
                    fb = f % 2
                    ws = f % 3
                    _w(y_, tn, qmm.get(f - 3))
                    wqe = dWQs[f % 2]
                    for hp in range(2):
                        y_.dma_start(wq_sb[hp * 64:(hp + 1) * 64, ws, :, :],
                                     wq_b.ap()[f, hp * 64:(hp + 1) * 64, :]
                                     .rearrange("p (dc u) -> p dc u", u=128))\
                            .then_inc(wqe.h, 16)
                    wqe.n += 32
                    _w(t_, wqe, wqe.n)
                    if f == 0:
                        _w(t_, dONE, p4_rd)
                    _w(t_, vn, qcd.get(f - 2))
                    ins = None
                    for dc in range(DC):
                        for gg in range(2):
                            ins = t_.matmul(qps[fb][:, gg * CH:(gg + 1) * CH],
                                            wq_sb[:, ws, dc, :],
                                            xno2[:, dc, gg * CH:(gg + 1) * CH],
                                            start=(dc == 0), stop=(dc == DC - 1))
                    tn.inc(ins)
                    qmm[f] = tn.n
                    _w(s_, tn, qmm[f])
                    _w(s_, tn, qrh.get(f - 2))
                    sn.inc(s_.activation(qbf[:, fb, :], qps[fb][:], AF.Copy))
                    qcp[f] = sn.n
                    if f >= 1:
                        _qrope_fin(f - 1)
                _qrope_fin(H - 1)
                if "q_dbg" in dbg:
                    q_dbg = nc.dram_tensor("q_dbg", [H, 128, NOWN], BF16,
                                           kind="ExternalOutput")
                    _w(y_, gn, gn.n)
                    for f in range(H):
                        y_.dma_start(q_dbg[f, :, :], q_sb[:, f, :]).then_inc(dONE.h, 16)
                        dONE.n += 16
                    y_.wait_ge(dONE.h, dONE.n)
            quiesce()

        if 5 in phases:
            _sto = ExitStack()
            o_sb = _sto.enter_context(nc.sbuf_tensor("o_sb", [128, H, NOWN], BF16,
                                                     side="right"))
            with ExitStack() as _st5:
                def _sb5(name, shape, dtype):
                    return _st5.enter_context(nc.sbuf_tensor(name, shape, dtype))
                k_sb = _sb5("k_sb", [128, 2, NPRE], BF16)
                v_sb = _sb5("v_sb", [128, 2, NT, HD], BF16)
                pt = _sb5("pt", [128, 2, 1024], BF16)
                ptraw = _sb5("ptraw", [128, 2, 1024], BF16)
                rm = _sb5("rm", [128, 8], F32)
                mx_sb = _sb5("mx_sb", [1, 2], F32)
                mbias_all = _sb5("mbias_all", [128, 64], F32)
                rc2 = _sb5("rc2", [1, 2, CH], F32)
                bc_sb = _sb5("bc_sb", [128, 2, CH], F32)
                sst = (_st5.enter_context(nc.psum_tensor("sst0", [128, 1024], F32)),
                       _st5.enter_context(nc.psum_tensor("sst1", [128, 1024], F32)))

                with ExitStack() as _stp:
                    kdiag = _stp.enter_context(
                        nc.sbuf_tensor("kdiag", [128, KV, 1024], BF16))
                    mbps2 = _stp.enter_context(nc.psum_tensor("mbps2", [128, 2], F32))
                    for kv in range(KV):
                        y_.dma_start(kdiag[:, kv, :], k_s[kv, :, 0:1024])\
                            .then_inc(dONE.h, 16)
                        dONE.n += 16
                    kdiag_rd = dONE.n
                    red_done, gp_done, bcp_done, ts_done = {}, {}, {}, {}
                    for hg in range(64):
                        kv, j, g = hg // 8, (hg // 2) % 4, hg % 2
                        h = kv * 4 + j
                        if hg == 0:
                            _w(t_, dONE, kdiag_rd)
                            _w(t_, dPW, perw_done)
                            _w(t_, dP, per_done)
                        _w(t_, vn, red_done.get(hg - 1))
                        sc_u = []
                        for u in range(4):
                            buf, half = sst[u // 2], u % 2
                            tn.inc(t_.matmul(buf[:, half * CH:(half + 1) * CH],
                                             kdiag[:, kv, (g * 4 + u) * 128:
                                                   (g * 4 + u + 1) * 128],
                                             q_sb[:, h, g * CH:(g + 1) * CH],
                                             start=True, stop=True,
                                             skip_group_check=True))
                            sc_u.append(tn.n)
                        o4 = (hg % 2) * 4
                        ins = None
                        for u in range(4):
                            buf, half = sst[u // 2], u % 2
                            _w(v_, tn, sc_u[u])
                            if u == 0:
                                _w(v_, gn, gp_done.get(hg - 2))
                            ins = v_.reduce_max(rm[:, o4 + u:o4 + u + 1],
                                                buf[:, half * CH:(half + 1) * CH],
                                                axis=AX.X)
                        vn.inc(ins)
                        red_done[hg] = vn.n
                        _w(g_, vn, red_done[hg])
                        _w(g_, tn, bcp_done.get(hg - 2))
                        gn.inc(g_.tensor_reduce(mx_sb[0:1, hg % 2:hg % 2 + 1],
                                                rm[:, o4:o4 + 4],
                                                axis=AX.XYZWC, op=ALU.max))
                        gp_done[hg] = gn.n
                        _w(t_, gn, gp_done[hg])
                        _w(t_, vn, ts_done.get(hg - 2))
                        tn.inc(t_.matmul(mbps2[:, hg % 2:hg % 2 + 1], ones1_f[:],
                                         mx_sb[0:1, hg % 2:hg % 2 + 1],
                                         start=True, stop=True,
                                         skip_group_check=True))
                        bcp_done[hg] = tn.n
                        _w(v_, tn, bcp_done[hg])
                        vn.inc(v_.tensor_scalar_mul(mbias_all[:, hg:hg + 1],
                                                    mbps2[:, hg % 2:hg % 2 + 1],
                                                    -ISQ))
                        ts_done[hg] = vn.n
                    _stp.close()

                o_psd = (_st5.enter_context(nc.psum_tensor("o_ps0", [128, CH], F32)),
                         _st5.enter_context(nc.psum_tensor("o_ps1", [128, CH], F32)))
                smbc = (_st5.enter_context(nc.psum_tensor("smbc0", [128, CH], F32)),
                        _st5.enter_context(nc.psum_tensor("smbc1", [128, CH], F32)))
                prep_t, prep_v, prep_g = tn.n, vn.n, gn.n

                engines = [t_, v_, s_, g_, y_]
                idxr = []
                for eng in engines:
                    _w(eng, dP, per_done)
                    _w(eng, dPW, perw_done)
                    _w(eng, tn, prep_t)
                    _w(eng, vn, prep_v)
                    _w(eng, gn, prep_g)
                    idxr.append(eng.value_load(cid_sb[:]))

                sems = [tn, vn, sn, gn, dONE] + dKVs
                snap0 = [s.n for s in sems]
                finals = None
                for c0 in nc.Switch(engines=engines, index=idxr, n=8):
                    for s, n0 in zip(sems, snap0):
                        s.n = n0
                    exp_done, oav_done, sc_done, mask_done = {}, {}, {}, {}
                    kv_tok, av_last = {}, {}
                    oav_stop, norm_done = {}, {}
                    ptraw_prev = {0: None, 1: None}
                    gi = 0
                    hg = 0

                    def _norm_fin(hp):
                        kvp = hp // 8
                        hp_h = kvp * 4 + (hp // 2) % 4
                        hp_g = hp % 2
                        p = hp % 2
                        _w(v_, tn, oav_stop[hp])
                        v_.reciprocal(bc_sb[:, p, :], smbc[p][:])
                        v_.drain()
                        vn.inc(v_.tensor_mul(o_sb[:, hp_h, hp_g * CH:(hp_g + 1) * CH],
                                             o_psd[p][:], bc_sb[:, p, :]))
                        norm_done[hp] = vn.n

                    for kv in range(KV):
                        kb = kv % 2
                        _w(y_, tn, av_last.get(kv - 2))
                        kve = dKVs[kb]
                        y_.dma_start(k_sb[:, kb, :], k_s[kv, :, :]).then_inc(kve.h, 16)
                        y_.dma_start(v_sb[:, kb, :, :],
                                     v_s.ap()[kv, :, :, :].rearrange("kt p d -> p kt d"))\
                            .then_inc(kve.h, 16)
                        kve.n += 32
                        kv_tok[kv] = (kve, kve.n)
                        for j in range(4):
                            h = kv * 4 + j
                            for g in range(2):
                                if g == 0:
                                    kchunks = [0] + [2 + jj for jj in range(c0)]
                                else:
                                    kchunks = [1] + [2 + c0 + jj for jj in range(7 - c0)]
                                halves = [(kcx, hf) for kcx in kchunks for hf in range(2)]
                                nh = len(halves)
                                g0 = gi
                                pend = []
                                for idx, (kcx, hf) in enumerate(halves):
                                    sb2 = gi % 2
                                    if j == 0 and g == 0 and idx == 0:
                                        _wt(t_, kv_tok[kv])
                                    _w(t_, sn, exp_done.get(gi - 2))
                                    ins = None
                                    for u in range(2):
                                        ktile = kcx * 4 + hf * 2 + u
                                        ins = t_.matmul(
                                            sst[sb2][:, u * CH:(u + 1) * CH],
                                            k_sb[:, kb, ktile * 128:(ktile + 1) * 128],
                                            q_sb[:, h, g * CH:(g + 1) * CH],
                                            start=True, stop=True,
                                            skip_group_check=True)
                                    tn.inc(ins)
                                    sc_done[gi] = tn.n
                                    if idx >= 2:
                                        a = pend.pop(0)
                                        _emit_oav(t_, tn, vn, sn, exp_done, mask_done,
                                                  oav_done, onesm_b, v_sb, pt,
                                                  o_psd[hg % 2], smbc[hg % 2], kb,
                                                  a[0], a[1], a[2], a[3], g0, nh,
                                                  norm_done.get(hg - 2))
                                    _w(s_, tn, sc_done[gi])
                                    _w(s_, tn, oav_done.get(gi - 2))
                                    if idx <= 1:
                                        _w(s_, vn, ptraw_prev[idx])
                                        dst = ptraw[:, sb2, :]
                                    else:
                                        dst = pt[:, sb2, :]
                                    sn.inc(s_.activation(dst, sst[sb2][:], AF.Exp,
                                                         bias=mbias_all[:, hg:hg + 1],
                                                         scale=ISQ))
                                    exp_done[gi] = sn.n
                                    if idx <= 1:
                                        _w(v_, sn, exp_done[gi])
                                        vn.inc(v_.tensor_mul(
                                            pt[:, sb2, :], ptraw[:, sb2, :],
                                            maskt_b[:, idx * 1024:(idx + 1) * 1024]))
                                        mask_done[gi] = vn.n
                                        ptraw_prev[idx] = vn.n
                                    pend.append((gi, idx, kcx, hf))
                                    gi += 1
                                while pend:
                                    a = pend.pop(0)
                                    _emit_oav(t_, tn, vn, sn, exp_done, mask_done,
                                              oav_done, onesm_b, v_sb, pt,
                                              o_psd[hg % 2], smbc[hg % 2], kb,
                                              a[0], a[1], a[2], a[3], g0, nh,
                                              norm_done.get(hg - 2))
                                av_last[kv] = tn.n
                                oav_stop[hg] = oav_done[gi - 1]
                                if hg >= 1:
                                    _norm_fin(hg - 1)
                                hg += 1
                    _norm_fin(63)
                    if finals is None:
                        finals = [s.n for s in sems]
                    else:
                        assert finals == [s.n for s in sems], \
                            f"arm {c0} sem counts differ"
                for s, nf in zip(sems, finals):
                    s.n = nf
                if "o_dbg" in dbg:
                    o_dbg2 = nc.dram_tensor("o_dbg", [H, 128, NOWN], BF16,
                                            kind="ExternalOutput")
                    _w(y_, vn, vn.n)
                    for h in range(H):
                        y_.dma_start(o_dbg2[h, :, :], o_sb[:, h, :]).then_inc(dONE.h, 16)
                        dONE.n += 16
                    y_.wait_ge(dONE.h, dONE.n)
            quiesce()
        if q_sb is not None:
            _stq.close()

        if 6 in phases:
            with ExitStack() as _st6:
                def _sb6(name, shape, dtype):
                    return _st6.enter_context(nc.sbuf_tensor(name, shape, dtype))
                wo_sb = _sb6("wo_sb", [128, 2, H, CH], BF16)
                xres = _sb6("xres", [128, 4, CH], F32)
                res_sb = _sb6("res_sb", [128, 4, CH], F32)
                p6 = [_st6.enter_context(nc.psum_tensor(f"p6_{t}", [128, CH], F32))
                      for t in range(8)]
                p6mm = {}
                res_done = {}
                ac_tok = {}
                xr_tok = {}
                for dm in range(8):
                    db = dm % 2
                    _w(y_, tn, p6mm.get((dm - 2, 7)))
                    woe = dWOs[db]
                    y_.dma_start(wo_sb[:, db, :, :],
                                 wo_b.ap()[dm, :, :]
                                 .rearrange("p (hh u) -> p hh u", u=CH))\
                        .then_inc(woe.h, 16)
                    woe.n += 16
                    _w(t_, woe, woe.n)
                    for th in range(2):
                        hx = dm * 2 + th
                        _w(t_, vn, res_done.get((dm - 1, th * 4 + 3)))
                        for hh in range(H):
                            for t4 in range(4):
                                t = th * 4 + t4
                                ins = t_.matmul(p6[t][:],
                                                o_sb[:, hh, t * 128:(t + 1) * 128],
                                                wo_sb[:, db, hh, :],
                                                start=(hh == 0), stop=(hh == H - 1),
                                                skip_group_check=True)
                                if hh == H - 1:
                                    tn.inc(ins)
                                    p6mm[(dm, t)] = tn.n
                        for t4 in range(4):
                            t = th * 4 + t4
                            ti = dm * 8 + t
                            sl = ti % 4
                            prev = (dm, t - 4) if t >= 4 else ((dm - 1, t + 4) if dm else None)
                            _w(y_, vn, res_done.get(prev))
                            xe6 = dXR6[sl]
                            y_.dma_start(xres[:, sl, :],
                                         x_pre[t * 128:(t + 1) * 128,
                                               dm * CH:(dm + 1) * CH]).then_inc(xe6.h, 16)
                            xe6.n += 16
                            xr_tok[(dm, t)] = (xe6, xe6.n)
                            _w(v_, tn, p6mm[(dm, t)])
                            _wt(v_, xr_tok[(dm, t)])
                            _wt(v_, ac_tok.get(prev))
                            vn.inc(v_.scalar_tensor_tensor(res_sb[:, sl, :], p6[t][:],
                                                           1.0, xres[:, sl, :],
                                                           ALU.mult, ALU.add))
                            res_done[(dm, t)] = vn.n
                            _w(g_, vn, res_done[(dm, t)])
                            if dm == 0 and t == 0:
                                for tok_ in p3_out:
                                    _wt(g_, tok_)
                            ace = dACs[sl]
                            g_.dma_start(out_own[t * 128:(t + 1) * 128,
                                                 dm * CH:(dm + 1) * CH],
                                         res_sb[:, sl, :],
                                         accum_op=ALU.add).then_inc(ace.h, 16)
                            ace.n += 16
                            ac_tok[(dm, t)] = (ace, ace.n)
                for ace in dACs:
                    g_.wait_ge(ace.h, ace.n)
        if 5 in phases:
            _sto.close()

    return nc



def _rope_tables():
    THETA, FACTOR, HI_FF, LO_FF, ORIG_MAX = 500000.0, 8.0, 4.0, 1.0, 8192
    inv = 1.0 / THETA ** (np.arange(0, HD, 2, dtype=np.float64) / HD)
    wavelen = 2.0 * np.pi / inv
    low_wl = ORIG_MAX / LO_FF
    high_wl = ORIG_MAX / HI_FF
    smooth = (ORIG_MAX / wavelen - LO_FF) / (HI_FF - LO_FF)
    scaled = np.where(wavelen > low_wl, inv / FACTOR, inv)
    mid = (wavelen <= low_wl) & (wavelen >= high_wl)
    scaled = np.where(mid, (1 - smooth) * inv / FACTOR + smooth * inv, scaled)
    inv_f = scaled.astype(np.float32)
    pos = np.arange(T, dtype=np.float32)
    freqs = pos[:, None] * inv_f[None, :]
    emb = np.concatenate([freqs, freqs], axis=-1)
    return np.cos(emb).astype(np.float32), np.sin(emb).astype(np.float32)


def _host_inputs(L0_attn_x, L0_attn_wq, L0_attn_wk, L0_attn_wv, L0_attn_wo,
                 L0_ffn_gate_w, L0_ffn_gate_s, L0_ffn_up_w, L0_ffn_up_s,
                 L0_ffn_down_w, L0_ffn_down_s):
    x = np.asarray(L0_attn_x, dtype=np.float32)
    cos_t, sin_t = _rope_tables()
    gate_rl_ = np.ascontiguousarray(np.asarray(L0_ffn_gate_w, dtype=np.int8).T)
    up_rl_ = np.ascontiguousarray(np.asarray(L0_ffn_up_w, dtype=np.int8).T)
    down_r8_ = np.ascontiguousarray(
        np.asarray(L0_ffn_down_w, dtype=np.int8).T.reshape(FC, 128, D))
    gs = np.ascontiguousarray(np.asarray(L0_ffn_gate_s, np.float32).reshape(FC, 128).T)
    us = np.ascontiguousarray(np.asarray(L0_ffn_up_s, np.float32).reshape(FC, 128).T)
    dsb_ = np.ascontiguousarray(np.broadcast_to(
        np.asarray(L0_ffn_down_s, np.float32)[None, :], (128, D)))
    mt = np.zeros((128, 2048), dtype=np.float32)
    for j in range(4):
        kt = 128 * j + np.arange(128)[:, None]
        q = np.arange(512)[None, :]
        mt[:, j * 512:(j + 1) * 512] = (kt <= q).astype(np.float32)
    ident = np.eye(128, dtype=np.float32)
    pperm = np.zeros((128, 128), dtype=np.float32)
    for m in range(64):
        pperm[m + 64, m] = -1.0
    for m in range(64, 128):
        pperm[m - 64, m] = 1.0
    shared = dict(wk=np.ascontiguousarray(L0_attn_wk, np.float32),
                  wv=np.ascontiguousarray(L0_attn_wv, np.float32),
                  wq=np.ascontiguousarray(L0_attn_wq, np.float32),
                  wo=np.ascontiguousarray(L0_attn_wo, np.float32),
                  gate_rl=gate_rl_, up_rl=up_rl_, down_r8=down_r8_,
                  gs=gs, us=us, dsb=dsb_, maskt=mt, identf=ident, pperm=pperm,
                  ones1=np.ones((1, 128), dtype=np.float32),
                  onesc=np.ones((128, 1), dtype=np.float32),
                  onesm=np.ones((128, 128), dtype=np.float32))
    in_maps = []
    for c in range(NC_):
        c0, c1 = c, 7 - c
        pos = np.concatenate([
            np.arange(c0 * CH, (c0 + 1) * CH),
            np.arange(c1 * CH, (c1 + 1) * CH),
            np.arange(0, c0 * CH),
            np.arange(0, c1 * CH),
        ])
        ba = np.concatenate([np.zeros(CH, np.int64), np.ones(CH, np.int64),
                             np.zeros(c0 * CH, np.int64), np.ones(c1 * CH, np.int64)])
        m = dict(shared)
        m.update(x_pre=np.ascontiguousarray(x[ba, pos, :]),
                 cosq=np.ascontiguousarray(cos_t[pos].T),
                 sinq=np.ascontiguousarray(sin_t[pos].T),
                 cid=np.array([[c]], dtype=np.int32))
        in_maps.append(m)
    return in_maps


_NC_CACHE = {}


def kernel(**inputs):
    in_maps = _host_inputs(**inputs)
    if "full" not in _NC_CACHE:
        _NC_CACHE["full"] = build()
    res = run_bass_kernel_spmd(_NC_CACHE["full"], in_maps, core_ids=list(range(NC_)))
    out = np.empty((B, T, D), dtype=np.float32)
    for c in range(NC_):
        oo = res.results[c]["out_own"]
        out[0, c * CH:(c + 1) * CH, :] = oo[0:CH]
        out[1, (7 - c) * CH:(8 - c) * CH, :] = oo[CH:NOWN]
    return out
